# version 04
# baseline (speedup 1.0000x reference)
import math
from contextlib import ExitStack

import numpy as np
import ml_dtypes

import concourse.bass as bass
import concourse.mybir as mybir
from concourse.bass_utils import run_bass_kernel_spmd

F32 = mybir.dt.float32
BF16 = mybir.dt.bfloat16
I32 = mybir.dt.int32
AF = mybir.ActivationFunctionType
ALU = mybir.AluOpType
AX = mybir.AxisListType.X

B, T, D = 4, 8192, 1024
NH, HDQ, HDV = 8, 128, 256
TOW = T // 2
C = 256
NCH = TOW // C
SCALE = 1.0 / math.sqrt(128.0)
EPS = 1e-5
TWO_PI = 2.0 * math.pi
NCORE = 8

DBG_MAX_PHASE = 99
DBG_SCROUT = False
DBG_SERIAL = 0

_HOST_CACHE = {}


def _host_consts():
    if _HOST_CACHE:
        return _HOST_CACHE
    p = np.arange(128)
    i_loc = 16 * (p // 32) + (p % 16)
    is_e = (p % 32) < 16
    perm = np.zeros(1024, dtype=np.int64)
    for h in range(NH):
        perm[128 * h: 128 * h + 128] = 2 * (64 * h + i_loc) + (~is_e).astype(np.int64)
    _HOST_CACHE["perm_qk"] = perm
    sgn = np.where(is_e, -1.0, 1.0)

    def tables(t0, n_tok, kind):
        t = np.arange(t0, t0 + n_tok, dtype=np.float64)
        nch = n_tok // C
        cs = np.empty((128, nch, NH, C), dtype=np.float64)
        sn = np.empty((128, nch, NH, C), dtype=np.float64)
        for h in range(NH):
            i = (64 * h + i_loc).astype(np.float64)
            sv = (2.0 * i + 0.4 * D) / (1.4 * D)
            inv = 10000.0 ** (-i / 512.0)
            a = t[None, :] * inv[:, None]
            s = sv[:, None] ** (t[None, :] / 512.0)
            s = s * SCALE if kind == "q" else 1.0 / s
            cs[:, :, h, :] = (np.cos(a) * s).reshape(128, nch, C)
            sn[:, :, h, :] = (np.sin(a) * s * sgn[:, None]).reshape(128, nch, C)
        return (np.ascontiguousarray(cs.reshape(128, -1)).astype(ml_dtypes.bfloat16),
                np.ascontiguousarray(sn.reshape(128, -1)).astype(ml_dtypes.bfloat16))

    _HOST_CACHE["csq0"], _HOST_CACHE["snq0"] = tables(0, TOW, "q")
    _HOST_CACHE["csq1"], _HOST_CACHE["snq1"] = tables(TOW, TOW, "q")
    _HOST_CACHE["csk0"], _HOST_CACHE["snk0"] = tables(0, TOW, "k")
    _HOST_CACHE["csk1"], _HOST_CACHE["snk1"] = tables(TOW, TOW, "k")

    j = np.arange(C)
    for tile in range(2):
        dlt = j[None, :] - (p[:, None] + 128 * tile)
        _HOST_CACHE[f"delta{tile}"] = np.where(dlt >= 0, dlt, 1.0e9).astype(np.float32)
    _HOST_CACHE["qexp"] = np.ascontiguousarray(
        np.broadcast_to(j[None, :].astype(np.float32), (128, C)))
    _HOST_CACHE["kexp"] = np.stack([256.0 - p, 128.0 - p], axis=1).astype(np.float32)
    _HOST_CACHE["c256"] = np.full((128, 1), 256.0, np.float32)
    _HOST_CACHE["ones_c"] = np.ones((128, 1), np.float32)
    _HOST_CACHE["ones_r"] = np.ones((1, 128), np.float32)
    _HOST_CACHE["ident"] = np.eye(128).astype(ml_dtypes.bfloat16)
    _HOST_CACHE["eps1"] = np.full((1, 1), EPS, np.float32)
    _HOST_CACHE["epsc"] = np.full((128, 1), EPS, np.float32)
    return _HOST_CACHE


ENGS = ("t", "v", "sc", "sy", "g")


class Rec:
    def __init__(self, counts):
        self.streams = {e: [] for e in ENGS}
        self.counts = counts

    def op(self, eng, fn, waits=(), inc=None, amt=None):
        w = [(s, v) for (s, v) in waits if v and v > 0]
        ser = getattr(self, "serial", False)
        if ser is True or (isinstance(ser, (set, frozenset)) and eng in ser):
            w = [(sname, val) for sname, val in self.counts.items() if val > 0]
        if amt is None:
            amt = 16 if eng in ("sy", "g") else 1
        self.streams[eng].append([fn, w, inc, amt])
        if inc is not None:
            self.counts[inc] = self.counts.get(inc, 0) + amt
            return self.counts[inc]
        return None

    def val(self, sem):
        return self.counts.get(sem, 0)

    def finish(self):
        bar = []
        for e in ENGS:
            st = self.streams[e]
            if not st:
                continue
            fn, w, inc, amt = st[-1]
            if inc is None:
                st[-1] = [fn, w, "bar", amt]
                self.counts["bar"] = self.counts.get("bar", 0) + amt
                bar.append(("bar", self.counts["bar"]))
            else:
                bar.append((inc, self.counts[inc]))
        return bar


def _replay(rec, block, sems):
    def run(stream):
        def body(eng):
            for fn, waits, inc, amt in stream:
                for s, v in waits:
                    eng.wait_ge(sems[s], v)
                inst = fn(eng)
                if inc is not None:
                    inst.then_inc(sems[inc], amt)
        return body
    if rec.streams["t"]:
        block.tensor(run(rec.streams["t"]))
    if rec.streams["v"]:
        block.vector(run(rec.streams["v"]))
    if rec.streams["sc"]:
        block.scalar(run(rec.streams["sc"]))
    if rec.streams["sy"]:
        block.sync(run(rec.streams["sy"]))
    if rec.streams["g"]:
        block.gpsimd(run(rec.streams["g"]))


def _assemble(nc, recs, sems):
    for i, rec in enumerate(recs):
        with nc.Block(f"ph{i}") as blk:
            _replay(rec, blk, sems)
    return nc


def build_program():
    nc = bass.Bass("TRN2", target_bir_lowering=False)
    scr_kind = "ExternalOutput" if DBG_SCROUT else "Internal"
    dI = lambda n, sh, dt: nc.dram_tensor(n, sh, dt, kind="ExternalInput").ap()
    dS = lambda n, sh, dt: nc.dram_tensor(n, sh, dt, kind=scr_kind).ap()

    xT_own = dI("xT_own", [D, TOW], F32)
    xT_prev = dI("xT_prev", [D, TOW], F32)
    wqkT = dI("wqkT", [D, 2048], F32)
    wvgT = dI("wvgT", [D, 4096], F32)
    wpT = dI("wpT", [2048, 1024], F32)
    W1 = dI("W1", [1024, 4096], F32)
    W2 = dI("W2", [4096, 1024], F32)
    csq = dI("csq", [128, NCH * NH * C], BF16)
    snq = dI("snq", [128, NCH * NH * C], BF16)
    csk_o = dI("csk_o", [128, NCH * NH * C], BF16)
    snk_o = dI("snk_o", [128, NCH * NH * C], BF16)
    csk_p = dI("csk_p", [128, NCH * NH * C], BF16)
    snk_p = dI("snk_p", [128, NCH * NH * C], BF16)
    delta0 = dI("delta0", [128, C], F32)
    delta1 = dI("delta1", [128, C], F32)
    qexp = dI("qexp", [128, C], F32)
    kexp = dI("kexp", [128, 2], F32)
    c256 = dI("c256", [128, 1], F32)
    decay_c = dI("decay_c", [128, NH], F32)
    b1c = dI("b1c", [128, 32], F32)
    b2c = dI("b2c", [128, 8], F32)
    qpc = dI("qpc", [128, 32], F32)
    qrc = dI("qrc", [128, 32], F32)
    qic = dI("qic", [128, 32], F32)
    ones_c_i = dI("ones_c", [128, 1], F32)
    ones_r_i = dI("ones_r", [1, 128], F32)
    ident_i = dI("ident", [128, 128], BF16)
    eps1_i = dI("eps1", [1, 1], F32)
    epsc_i = dI("epsc", [128, 1], F32)
    carry_i = dI("carry", [128, 1], F32)
    outT = nc.dram_tensor("outT", [D, TOW], F32, kind="ExternalOutput").ap()

    ab_scr = dS("ab_scr", [2 * NCH, 1, 512], F32)
    h_scr = dS("h_scr", [2 * NCH, 128, 2048], BF16)
    q_scr = dS("q_scr", [NCH, 128, 2048], BF16)
    kf_scr = dS("kf_scr", [NCH, 128, 2048], BF16)
    kw_scr = dS("kw_scr", [2 * NCH, 2, 128, 1024], BF16)
    v_scr = dS("v_scr", [2 * NCH, 2, 128, 2048], BF16)
    g_scr = dS("g_scr", [NCH, 2, 128, 2048], BF16)
    x1_scr = dS("x1_scr", [NCH, 128, 2048], F32)
    dbg_st = dS("dbg_st", [NCH, 128, 64], F32)
    dbg_osq = dS("dbg_osq", [NCH, 2, 128, 2048], F32)
    dbg_a = dS("dbg_a", [NCH, 2, 128, 64], F32)
    dbg_act = dS("dbg_act", [NCH, 128, 8192], BF16)
    dbg_qc = dS("dbg_qc", [128, 64], F32)
    dbg_b = dS("dbg_b", [NCH, 2, 128, 64], F32)
    h2_scr = dS("h2_scr", [NCH, 128, 2048], BF16)

    xo3 = xT_own.rearrange("(c p) n -> p c n", p=128)
    xp3 = xT_prev.rearrange("(c p) n -> p c n", p=128)
    out3 = outT.rearrange("(c p) n -> p c n", p=128)

    counts = {}
    recs = []
    with ExitStack() as stack:
        e = stack.enter_context
        sems = {n: e(nc.semaphore(n)) for n in
                ["ld", "gw", "tm", "vm", "sm", "st", "bar", "pm"]}

        arena = e(nc.sbuf_tensor("arena", [128, 32768], BF16))
        px = [e(nc.sbuf_tensor(f"px{i}", [128, 2048], F32)) for i in range(2)]
        ones_c = e(nc.sbuf_tensor("ones_c_s", [128, 1], F32))
        ones_r = e(nc.sbuf_tensor("ones_r_s", [1, 128], F32))
        ident = e(nc.sbuf_tensor("ident_s", [128, 128], BF16))
        eps1 = e(nc.sbuf_tensor("eps1_s", [1, 1], F32))
        epsc = e(nc.sbuf_tensor("epsc_s", [128, 1], F32))
        carry = e(nc.sbuf_tensor("carry_s", [128, 1], F32))
        dcol = e(nc.sbuf_tensor("dcol", [128, NH], F32))
        lnd = e(nc.sbuf_tensor("lnd", [128, NH], F32))
        kfac = e(nc.sbuf_tensor("kfac", [128, 16], F32))
        kx_s = e(nc.sbuf_tensor("kx_s", [128, 2], F32))
        c256_s = e(nc.sbuf_tensor("c256_s", [128, 1], F32))

        rec = Rec(counts)
        for sb, di in [(ones_c, ones_c_i), (ones_r, ones_r_i), (ident, ident_i),
                       (eps1, eps1_i), (epsc, epsc_i), (carry, carry_i),
                       (dcol, decay_c), (kx_s, kexp), (c256_s, c256)]:
            rec.op("sy", (lambda sb=sb, di=di: lambda s: s.dma_start(out=sb[:], in_=di))(),
                   inc="ld")
        ldv = rec.val("ld")
        smln = rec.op("sc", lambda sc: sc.activation(out=lnd[:], in_=dcol[:],
                                                     func=AF.Ln),
                      waits=[("ld", ldv)], inc="sm")
        for h in range(NH):
            for t in range(2):
                rec.op("sc", (lambda h=h, t=t: lambda sc: sc.activation(
                    out=kfac[:, 2 * h + t: 2 * h + t + 1], in_=kx_s[:, t:t + 1],
                    func=AF.Exp, scale=lnd[:, h:h + 1]))(),
                    waits=[("sm", smln)], inc="sm")
        rec.op("g", lambda g: g.dma_start(
            out=arena[:, 0:16384].rearrange("p (c n) -> p c n", c=8),
            in_=wqkT.rearrange("(c p) n -> p c n", p=128)), inc="gw")
        gw_wqk = rec.val("gw")
        barrier = rec.finish()
        recs.append(rec)

        ab_gw = [0] * (2 * NCH)
        with ExitStack() as s01:
            pe = s01.enter_context
            ps_row = [pe(nc.psum_tensor(f"ps_row{i}", [1, 512], F32)) for i in range(2)]
            rows05 = pe(nc.sbuf_tensor("rows05", [1, 2560], F32))
            xsq2 = [pe(nc.sbuf_tensor(f"xsq{i}", [128, 2048], F32)) for i in range(2)]

            rec = Rec(counts)
            ch_done = [0] * (2 * NCH)
            row_mm = [0] * (2 * NCH)
            row_ev = [0] * (2 * NCH)
            for m in range(2 * NCH):
                xb = px[m % 2]
                xq = xsq2[m % 2]
                pr = ps_row[m % 2]
                r0 = 1280 * (m % 2)
                src = xp3 if m < NCH else xo3
                off = (m % NCH) * C
                w = ([("vm", ch_done[m - 2])] if m >= 2 else list(barrier))
                ldx = rec.op("sy", (lambda xb=xb, src=src, off=off: lambda s: s.dma_start(
                    out=xb[:].rearrange("p (c n) -> p c n", c=8),
                    in_=src[:, :, off:off + C]))(), waits=w, inc="ld")
                smv = rec.op("sc", (lambda xb=xb, xq=xq: lambda sc: sc.activation(
                    out=xq[:], in_=xb[:], func=AF.Square))(),
                    waits=[("ld", ldx), ("tm", row_mm[m - 2] if m >= 2 else 0)],
                    inc="sm")

                def mk_rows(xb=xb, xq=xq, pr=pr):
                    def f(t):
                        r = None
                        for cc in range(8):
                            r = t.matmul(pr[0:1, 0:256], ones_c[:],
                                         xb[:, 256 * cc:256 * cc + 256],
                                         start=(cc == 0), stop=(cc == 7))
                        for cc in range(8):
                            r = t.matmul(pr[0:1, 256:512], ones_c[:],
                                         xq[:, 256 * cc:256 * cc + 256],
                                         start=(cc == 0), stop=(cc == 7))
                        return r
                    return f
                row_mm[m] = tmv = rec.op("t", mk_rows(),
                                         waits=[("sm", smv),
                                                ("vm", row_ev[m - 2] if m >= 2 else 0)],
                                         inc="tm")
                ee = rec.op("v", (lambda pr=pr, r0=r0: lambda v: v.tensor_copy(
                    rows05[0:1, r0:r0 + 512], pr[0:1, 0:512]))(),
                    waits=[("tm", tmv), ("gw", ab_gw[m - 2] if m >= 2 else 0)], inc="vm")
                row_ev[m] = ee
                ee = rec.op("v", (lambda r0=r0: lambda v: v.tensor_scalar(
                    out=rows05[0:1, r0 + 512:r0 + 768], in0=rows05[0:1, r0:r0 + 256],
                    scalar1=1.0 / D, scalar2=None, op0=ALU.mult))(),
                    waits=[("vm", ee)], inc="vm")
                ee = rec.op("v", (lambda r0=r0: lambda v: v.tensor_tensor(
                    out=rows05[0:1, r0 + 768:r0 + 1024], in0=rows05[0:1, r0 + 512:r0 + 768],
                    in1=rows05[0:1, r0 + 512:r0 + 768], op=ALU.mult))(),
                    waits=[("vm", ee)], inc="vm")
                vmv = rec.op("v", (lambda r0=r0: lambda v: v.scalar_tensor_tensor(
                    out=rows05[0:1, r0 + 1024:r0 + 1280], in0=rows05[0:1, r0 + 256:r0 + 512],
                    scalar=1.0 / D, in1=rows05[0:1, r0 + 768:r0 + 1024],
                    op0=ALU.mult, op1=ALU.subtract))(), waits=[("vm", ee)], inc="vm")
                smln = rec.op("sc", (lambda r0=r0: lambda sc: sc.activation(
                    out=rows05[0:1, r0 + 1024:r0 + 1280], in_=rows05[0:1, r0 + 1024:r0 + 1280],
                    func=AF.Ln, bias=eps1[0:1, 0:1]))(), waits=[("vm", vmv)], inc="sm")
                smln = rec.op("sc", (lambda r0=r0: lambda sc: sc.activation(
                    out=rows05[0:1, r0:r0 + 256], in_=rows05[0:1, r0 + 1024:r0 + 1280],
                    func=AF.Exp, scale=-0.5))(), waits=[("sm", smln)], inc="sm")
                vmv = rec.op("v", (lambda r0=r0: lambda v: v.scalar_tensor_tensor(
                    out=rows05[0:1, r0 + 256:r0 + 512], in0=rows05[0:1, r0 + 512:r0 + 768],
                    scalar=-1.0, in1=rows05[0:1, r0:r0 + 256],
                    op0=ALU.mult, op1=ALU.mult))(), waits=[("sm", smln)], inc="vm")
                ch_done[m] = vmv
                ab_gw[m] = rec.op("g", (lambda m=m, r0=r0: lambda g: g.dma_start(
                    out=ab_scr[m], in_=rows05[0:1, r0:r0 + 512]))(),
                    waits=[("vm", vmv)], inc="gw")
            barrier = rec.finish()
            recs.append(rec)
            if DBG_MAX_PHASE < 2:
                return _assemble(nc, recs, sems)

            rec = Rec(counts)
            with ExitStack() as p1s:
                pe1 = p1s.enter_context
                h_t = [pe1(nc.sbuf_tensor(f"p1a_h{i}", [128, 2048], BF16)) for i in range(2)]
                cs_t = [pe1(nc.sbuf_tensor(f"p1a_cs{i}", [128, 2048], BF16)) for i in range(2)]
                sn_t = [pe1(nc.sbuf_tensor(f"p1a_sn{i}", [128, 2048], BF16)) for i in range(2)]
                csk_t = [pe1(nc.sbuf_tensor(f"p1a_ck{i}", [128, 2048], BF16)) for i in range(2)]
                snk_t = [pe1(nc.sbuf_tensor(f"p1a_sk{i}", [128, 2048], BF16)) for i in range(2)]
                ab_t = [pe1(nc.sbuf_tensor(f"p1a_ab{i}", [1, 512], F32)) for i in range(2)]
                t1f = pe1(nc.sbuf_tensor("p1a_t1", [128, 1024], F32))
                t2f = pe1(nc.sbuf_tensor("p1a_t2", [128, 1024], F32))
                q_sb = pe1(nc.sbuf_tensor("p1a_q", [128, 2048], BF16))
                k_sb = pe1(nc.sbuf_tensor("p1a_k", [128, 2048], BF16))
                kw_t = pe1(nc.sbuf_tensor("p1a_kw", [128, 2048], BF16))
                ps_ab = pe1(nc.psum_tensor("p1a_psab", [128, 512], F32))
                ps_mm = [pe1(nc.psum_tensor(f"p1a_mm{i}", [128, 1024], F32))
                         for i in range(2)]
                ps_tr = pe1(nc.psum_tensor("p1a_tr", [128, 1024], BF16))

                SWAP = list(range(16, 32)) + list(range(16))
                ch1 = [0] * (2 * NCH)
                pc_evac = {}
                trw = {}
                hw_gw = [0] * (2 * NCH)
                qw_gw = [0] * (2 * NCH)
                kfw_gw = [0] * (2 * NCH)
                kww_gw = [0] * (2 * NCH)
                for m in range(2 * NCH):
                    own = m >= NCH
                    mm = m % NCH
                    xb, hb = px[m % 2], h_t[m % 2]
                    csb, snb = cs_t[m % 2], sn_t[m % 2]
                    cskb, snkb = csk_t[m % 2], snk_t[m % 2]
                    abb = ab_t[m % 2]
                    src = xo3 if own else xp3
                    off = mm * C
                    wfree = ([("vm", ch1[m - 2])] if m >= 2 else list(barrier))
                    rec.op("sy", (lambda xb=xb, src=src, off=off: lambda s: s.dma_start(
                        out=xb[:].rearrange("p (c n) -> p c n", c=8),
                        in_=src[:, :, off:off + C]))(), waits=wfree, inc="ld")
                    if own:
                        rec.op("sy", (lambda csb=csb, mm=mm: lambda s: s.dma_start(
                            out=csb[:], in_=csq[:, 2048 * mm:2048 * (mm + 1)]))(),
                            inc="ld")
                        rec.op("sy", (lambda snb=snb, mm=mm: lambda s: s.dma_start(
                            out=snb[:], in_=snq[:, 2048 * mm:2048 * (mm + 1)]))(),
                            inc="ld")
                    ck = csk_o if own else csk_p
                    sk = snk_o if own else snk_p
                    rec.op("sy", (lambda cskb=cskb, ck=ck, mm=mm: lambda s: s.dma_start(
                        out=cskb[:], in_=ck[:, 2048 * mm:2048 * (mm + 1)]))(), inc="ld")
                    rec.op("sy", (lambda snkb=snkb, sk=sk, mm=mm: lambda s: s.dma_start(
                        out=snkb[:], in_=sk[:, 2048 * mm:2048 * (mm + 1)]))(), inc="ld")
                    ldt = rec.op("sy", (lambda abb=abb, m=m: lambda s: s.dma_start(
                        out=abb[:], in_=ab_scr[m]))(), waits=[("gw", ab_gw[m])], inc="ld")

                    tmv = rec.op("t", (lambda abb=abb: lambda t: (
                        t.matmul(ps_ab[:, 0:256], ones_r[:], abb[0:1, 0:256],
                                 start=True, stop=True),
                        t.matmul(ps_ab[:, 256:512], ones_r[:], abb[0:1, 256:512],
                                 start=True, stop=True))[-1])(),
                        waits=[("ld", ldt), ("vm", ch1[m - 1] if m >= 1 else 0)],
                        inc="tm")

                    def mk_applyA(xb=xb, hb=hb):
                        def f(v):
                            r = None
                            for cc in range(8):
                                sl = slice(256 * cc, 256 * cc + 256)
                                dst = (t1f if cc < 4 else t2f)[:, 256 * (cc % 4):
                                                               256 * (cc % 4) + 256]
                                r = v.tensor_tensor(out=dst, in0=xb[:, sl],
                                                    in1=ps_ab[:, 0:256], op=ALU.mult)
                            return r
                        return f
                    def mk_applyB(hb=hb):
                        def f(v):
                            r = None
                            for cc in range(8):
                                sl = slice(256 * cc, 256 * cc + 256)
                                src = (t1f if cc < 4 else t2f)[:, 256 * (cc % 4):
                                                               256 * (cc % 4) + 256]
                                r = v.tensor_tensor(out=hb[:, sl], in0=src,
                                                    in1=ps_ab[:, 256:512], op=ALU.add)
                            return r
                        return f
                    va_ = rec.op("v", mk_applyA(),
                                 waits=[("tm", tmv),
                                        ("gw", hw_gw[m - 2] if m >= 2 else 0)],
                                 inc="vm")
                    vh = rec.op("v", mk_applyB(), waits=[("vm", va_)], inc="vm")
                    hw_gw[m] = rec.op("g", (lambda hb=hb, m=m: lambda g: g.dma_start(
                        out=h_scr[m], in_=hb[:]))(), waits=[("vm", vh)], inc="gw")

                    pieces = ([(0, "q"), (1, "q"), (2, "k"), (3, "k")] if own
                              else [(2, "k"), (3, "k")])
                    npieces = len(pieces)
                    for pi, (pc, kind) in enumerate(pieces):
                        dst = ps_mm[pi % 2]
                        wts = [("vm", vh), ("gw", gw_wqk)]
                        if pi >= 2:
                            wts.append(("vm", pc_evac[(m, pi - 2)]))
                        elif m >= 1:
                            prev_np = 4 if (m - 1) >= NCH else 2
                            cand = [q for q in range(prev_np) if q % 2 == pi % 2]
                            wts.append(("vm", pc_evac[(m - 1, max(cand))]))
                        def mk_piece(dst=dst, pc=pc, hb=hb):
                            def f(t):
                                r = None
                                for oi in range(4):
                                    oc = 4 * pc + oi
                                    for cc in range(8):
                                        r = t.matmul(
                                            dst[:, 256 * oi:256 * oi + 256],
                                            arena[:, 2048 * cc + 128 * oc:
                                                  2048 * cc + 128 * oc + 128],
                                            hb[:, 256 * cc:256 * cc + 256],
                                            start=(cc == 0), stop=(cc == 7))
                                return r
                            return f
                        tmp = rec.op("t", mk_piece(), waits=wts, inc="tm")
                        half = pc % 2
                        if kind == "q":
                            csx, snx, dstq = csb, snb, q_sb
                            wgw = qw_gw[m - 1] if m >= 1 else 0
                        else:
                            csx, snx, dstq = cskb, snkb, k_sb
                            wgw = max(kfw_gw[m - 1] if m >= 1 else 0,
                                      kww_gw[m - 1] if m >= 1 else 0)
                        sl = slice(1024 * half, 1024 * half + 1024)
                        def mk_xpos(dst=dst, csx=csx, snx=snx, dstq=dstq, sl=sl):
                            def f(v):
                                v.tensor_tensor(out=t1f[:], in0=dst[:], in1=csx[:, sl],
                                                op=ALU.mult)
                                v.stream_shuffle(out=t2f[:], in_=dst[:], mask=SWAP)
                                v.tensor_tensor(out=t2f[:], in0=t2f[:], in1=snx[:, sl],
                                                op=ALU.mult)
                                return v.tensor_tensor(out=dstq[:, sl], in0=t1f[:],
                                                       in1=t2f[:], op=ALU.add)
                            return f
                        pc_evac[(m, pi)] = rec.op("v", mk_xpos(),
                                                  waits=[("tm", tmp), ("ld", ldt),
                                                         ("gw", wgw)], inc="vm")
                    kdone = pc_evac[(m, npieces - 1)]

                    for ts in range(2):
                        prev = trw.get((m, ts - 1), trw.get((m - 1, 1), 0))
                        def mk_tr(ts=ts):
                            def f(t):
                                r = None
                                for h in range(NH):
                                    r = t.transpose(
                                        ps_tr[:, 128 * h:128 * h + 128],
                                        k_sb[:, 256 * h + 128 * ts:
                                             256 * h + 128 * ts + 128], ident[:])
                                return r
                            return f
                        tmv2 = rec.op("t", mk_tr(), waits=[("vm", kdone), ("vm", prev)],
                                      inc="tm")
                        def mk_trev(ts=ts):
                            def f(v):
                                r = None
                                for h in range(NH):
                                    r = v.tensor_scalar(
                                        out=kw_t[:, 1024 * ts + 128 * h:
                                                 1024 * ts + 128 * h + 128],
                                        in0=ps_tr[:, 128 * h:128 * h + 128],
                                        scalar1=kfac[:, 2 * h + ts:2 * h + ts + 1],
                                        scalar2=None, op0=ALU.mult)
                                return r
                            return f
                        w2_ = [("tm", tmv2)]
                        if ts == 0:
                            w2_.append(("gw", kww_gw[m - 1] if m >= 1 else 0))
                        trw[(m, ts)] = rec.op("v", mk_trev(), waits=w2_, inc="vm")

                    if own:
                        qw_gw[m] = rec.op("g", (lambda mm=mm: lambda g: g.dma_start(
                            out=q_scr[mm], in_=q_sb[:]))(),
                            waits=[("vm", pc_evac[(m, 1)])], inc="gw")
                        kfw_gw[m] = rec.op("g", (lambda mm=mm: lambda g: g.dma_start(
                            out=kf_scr[mm], in_=k_sb[:]))(), waits=[("vm", kdone)],
                            inc="gw")
                    kww_gw[m] = rec.op("g", (lambda m=m: lambda g: g.dma_start(
                        out=kw_scr[m].rearrange("a p n -> p a n"),
                        in_=kw_t[:].rearrange("p (a n) -> p a n", a=2)))(),
                        waits=[("vm", trw[(m, 1)])], inc="gw")
                    ch1[m] = trw[(m, 1)]
                barrier = rec.finish()
            recs.append(rec)
            if DBG_MAX_PHASE < 3:
                return _assemble(nc, recs, sems)

            rec = Rec(counts)
            with ExitStack() as p1bs:
                pe2 = p1bs.enter_context
                h_t2 = [pe2(nc.sbuf_tensor(f"p1b_h{i}", [128, 2048], BF16))
                        for i in range(2)]
                v_t2 = pe2(nc.sbuf_tensor("p1b_v", [128, 4096], BF16))
                g_t2 = pe2(nc.sbuf_tensor("p1b_g", [128, 4096], BF16))
                ps_mm2 = [pe2(nc.psum_tensor(f"p1b_mm{i}", [128, 1024], F32))
                          for i in range(3)]

                rec.op("g", lambda g: g.dma_start(
                    out=arena[:].rearrange("p (c n) -> p c n", c=8),
                    in_=wvgT.rearrange("(c p) n -> p c n", p=128)),
                    waits=barrier, inc="gw")
                gw_vg = rec.val("gw")

                ch2 = [0] * (2 * NCH)
                vw_gw = [0] * (2 * NCH)
                gw_gw = [0] * (2 * NCH)
                piece_ev = {}
                pln = {}
                for m in range(2 * NCH):
                    own = m >= NCH
                    mm = m % NCH
                    hb = h_t2[m % 2]
                    wfree = ([("vm", ch2[m - 2])] if m >= 2 else list(barrier))
                    ldh = rec.op("sy", (lambda hb=hb, m=m: lambda s: s.dma_start(
                        out=hb[:], in_=h_scr[m]))(),
                        waits=wfree + [("gw", gw_vg), ("gw", hw_gw[m])], inc="ld")
                    pl = [(ts, hf, "v") for ts in range(2) for hf in range(2)]
                    if own:
                        pl += [(ts, hf, "g") for ts in range(2) for hf in range(2)]
                    pln[m] = len(pl)
                    for pi, (ts, hf, kind) in enumerate(pl):
                        dst = ps_mm2[pi % 3]
                        off = (0 if kind == "v" else 2048) + 1024 * hf
                        wts = [("ld", ldh)]
                        if pi >= 3:
                            wts.append(piece_ev[(m, pi - 3)])
                        elif m >= 1:
                            cand = [q for q in range(pln[m - 1]) if q % 3 == pi % 3]
                            wts.append(piece_ev[(m - 1, max(cand))])
                        def mk_vg(dst=dst, off=off, hb=hb, ts=ts):
                            def f(t):
                                r = None
                                for cc in range(8):
                                    for qn in range(2):
                                        r = t.matmul(
                                            dst[:, 512 * qn:512 * qn + 512],
                                            hb[:, 256 * cc + 128 * ts:
                                               256 * cc + 128 * ts + 128],
                                            arena[:, 4096 * cc + off + 512 * qn:
                                                  4096 * cc + off + 512 * qn + 512],
                                            start=(cc == 0), stop=(cc == 7))
                                return r
                            return f
                        tmv = rec.op("t", mk_vg(), waits=wts, inc="tm")
                        dsl = slice(2048 * ts + 1024 * hf, 2048 * ts + 1024 * hf + 1024)
                        if kind == "v":
                            ev = rec.op("v", (lambda dst=dst, dsl=dsl: lambda v:
                                              v.tensor_copy(v_t2[:, dsl], dst[:]))(),
                                        waits=[("tm", tmv),
                                               ("gw", vw_gw[m - 1] if m >= 1 else 0)],
                                        inc="vm")
                            piece_ev[(m, pi)] = ("vm", ev)
                        else:
                            ev = rec.op("sc", (lambda dst=dst, dsl=dsl: lambda sc:
                                               sc.activation(out=g_t2[:, dsl],
                                                             in_=dst[:], func=AF.Silu))(),
                                        waits=[("tm", tmv),
                                               ("gw", gw_gw[m - 1] if m >= 1 else 0)],
                                        inc="sm")
                            piece_ev[(m, pi)] = ("sm", ev)
                    vw_gw[m] = rec.op("g", (lambda m=m: lambda g: g.dma_start(
                        out=v_scr[m].rearrange("a p n -> p a n"),
                        in_=v_t2[:].rearrange("p (a n) -> p a n", a=2)))(),
                        waits=[piece_ev[(m, 3)]], inc="gw")
                    if own:
                        gw_gw[m] = rec.op("g", (lambda mm=mm: lambda g: g.dma_start(
                            out=g_scr[mm].rearrange("a p n -> p a n"),
                            in_=g_t2[:].rearrange("p (a n) -> p a n", a=2)))(),
                            waits=[piece_ev[(m, 7)]], inc="gw")
                    lsem, lval = piece_ev[(m, len(pl) - 1)]
                    if lsem == "vm":
                        ch2[m] = lval
                    else:
                        ch2[m] = rec.op("v", lambda v: v.tensor_copy(
                            v_t2[:, 0:2], v_t2[:, 0:2]), waits=[(lsem, lval)], inc="vm")
                barrier = rec.finish()
            recs.append(rec)
        if DBG_MAX_PHASE < 4:
            return _assemble(nc, recs, sems)

        rec = Rec(counts)
        rec.serial = bool(DBG_SERIAL & 1)
        x1w_gw = [0] * NCH
        h2w_gw = [0] * NCH
        with ExitStack() as p2s:
            pe3 = p2s.enter_context
            dmask = [pe3(nc.sbuf_tensor(f"p2_dm{i}", [128, 2048], BF16)) for i in range(2)]
            qfac = pe3(nc.sbuf_tensor("p2_qfac", [128, 2048], BF16))
            alph = pe3(nc.sbuf_tensor("p2_alpha", [128, NH], F32))
            S = pe3(nc.sbuf_tensor("p2_S", [128, 2048], F32))
            S_bf = pe3(nc.sbuf_tensor("p2_Sbf", [128, 2048], BF16))
            dl_t = [pe3(nc.sbuf_tensor(f"p2_dl{i}", [128, 256], F32)) for i in range(2)]
            qx_t = pe3(nc.sbuf_tensor("p2_qx", [128, 256], F32))
            q_t = pe3(nc.sbuf_tensor("p2_q", [128, 2048], BF16))
            kf_t = pe3(nc.sbuf_tensor("p2_kf", [128, 2048], BF16))
            kw_t2 = [pe3(nc.sbuf_tensor(f"p2_kw{i}", [128, 2048], BF16)) for i in range(2)]
            v_t3 = [pe3(nc.sbuf_tensor(f"p2_v{i}", [128, 4096], BF16)) for i in range(2)]
            g_t3 = [pe3(nc.sbuf_tensor(f"p2_g{i}", [128, 4096], BF16)) for i in range(2)]
            qf_t = pe3(nc.sbuf_tensor("p2_qf", [128, 2048], BF16))
            sc_bf = pe3(nc.sbuf_tensor("p2_scbf", [128, 4096], BF16))
            stats = pe3(nc.sbuf_tensor("p2_stats", [128, 64], F32))
            nrm_t = pe3(nc.sbuf_tensor("p2_nrm", [128, 2048], BF16))
            z_t = pe3(nc.sbuf_tensor("p2_z", [128, 2048], BF16))
            zf_t = pe3(nc.sbuf_tensor("p2_zf", [128, 4096], BF16))
            h2_t = pe3(nc.sbuf_tensor("p2_h2", [128, 2048], BF16))
            ps_sc = [pe3(nc.psum_tensor(f"p2_pssc{i}", [128, 512], F32)) for i in range(2)]
            ps_o = pe3(nc.psum_tensor("p2_pso", [128, 2048], F32))
            ps_tr2 = pe3(nc.psum_tensor("p2_pstr", [128, 1024], BF16))
            ps_ab2 = pe3(nc.psum_tensor("p2_psab", [128, 512], F32))
            x1_t = px[0]
            scrA = pe3(nc.sbuf_tensor("scrA", [128, 2048], F32))
            rows = pe3(nc.sbuf_tensor("rows_p", [1, 1280], F32))
            o_sq = scrA

            rec.op("g", lambda g: g.dma_start(
                out=arena[:, 0:16384].rearrange("p (j n) -> p j n", j=16),
                in_=wpT.rearrange("(j p) n -> p j n", p=128)),
                waits=barrier, inc="gw")
            gw_wp = rec.val("gw")

            rec.op("sy", lambda s: s.dma_start(out=dl_t[0][:], in_=delta0),
                   waits=barrier, inc="ld")
            rec.op("sy", lambda s: s.dma_start(out=dl_t[1][:], in_=delta1), inc="ld")
            ldq = rec.op("sy", lambda s: s.dma_start(out=qx_t[:], in_=qexp), inc="ld")
            first = True
            for tile in range(2):
                for h in range(NH):
                    rec.op("sc", (lambda tile=tile, h=h: lambda sc: sc.activation(
                        out=dmask[tile][:, 256 * h:256 * h + 256], in_=dl_t[tile][:],
                        func=AF.Exp, scale=lnd[:, h:h + 1]))(),
                        waits=[("ld", ldq)] + (list(barrier) if first else []),
                        inc="sm")
                    first = False
            for h in range(NH):
                rec.op("sc", (lambda h=h: lambda sc: sc.activation(
                    out=qfac[:, 256 * h:256 * h + 256], in_=qx_t[:],
                    func=AF.Exp, scale=lnd[:, h:h + 1]))(), inc="sm")
            for h in range(NH):
                rec.op("sc", (lambda h=h: lambda sc: sc.activation(
                    out=alph[:, h:h + 1], in_=c256_s[:], func=AF.Exp,
                    scale=lnd[:, h:h + 1]))(), inc="sm")
            consts_sm = rec.val("sm")
            su_done = rec.op("v", lambda v: v.memset(S[:], 0.0), waits=barrier, inc="vm")

            ch3 = [0] * (2 * NCH)
            prev_mask = (0, 0)
            prev_av = 0
            prev_scq = 0
            for m in range(2 * NCH):
                if DBG_SERIAL & 2:
                    rec.serial = m < 18
                if DBG_SERIAL & 4:
                    rec.serial = m < 16
                if DBG_SERIAL & 8:
                    rec.serial = m in (16, 17)
                if DBG_SERIAL & 16:
                    rec.serial = {"v"} if m < 18 else False
                if DBG_SERIAL & 32:
                    rec.serial = {"t"} if m < 18 else False
                if DBG_SERIAL & 64:
                    rec.serial = {"sc"} if m < 18 else False
                if DBG_SERIAL & 128:
                    rec.serial = {"sy", "g"} if m < 18 else False
                own = m >= NCH
                mm = m % NCH
                bsel = m % 2
                kwb, vb = kw_t2[bsel], v_t3[bsel]
                wfree = ([("vm", ch3[m - 2])] if m >= 2 else [("sm", consts_sm)])
                rec.op("sy", (lambda kwb=kwb, m=m: lambda s: s.dma_start(
                    out=kwb[:].rearrange("p (a n) -> p a n", a=2),
                    in_=kw_scr[m].rearrange("a p n -> p a n")))(),
                    waits=wfree + [("gw", kww_gw[m])], inc="ld")
                ldv_ = rec.op("sy", (lambda vb=vb, m=m: lambda s: s.dma_start(
                    out=vb[:].rearrange("p (a n) -> p a n", a=2),
                    in_=v_scr[m].rearrange("a p n -> p a n")))(),
                    waits=[("gw", vw_gw[m])], inc="ld")
                if not own:
                    def mk_U(kwb=kwb, vb=vb):
                        def f(t):
                            r = None
                            for h in range(NH):
                                for tt in range(2):
                                    r = t.matmul(ps_o[:, 256 * h:256 * h + 256],
                                                 kwb[:, 1024 * tt + 128 * h:
                                                    1024 * tt + 128 * h + 128],
                                                 vb[:, 2048 * tt + 256 * h:
                                                    2048 * tt + 256 * h + 256],
                                                 start=(tt == 0), stop=(tt == 1))
                            return r
                        return f
                    tmu = rec.op("t", mk_U(), waits=[("ld", ldv_), ("vm", su_done)],
                                 inc="tm")
                    def mk_sua():
                        def f(v):
                            r = None
                            for h in range(NH):
                                r = v.tensor_scalar(out=S[:, 256 * h:256 * h + 256],
                                                    in0=S[:, 256 * h:256 * h + 256],
                                                    scalar1=alph[:, h:h + 1],
                                                    scalar2=None, op0=ALU.mult)
                            return r
                        return f
                    ea = rec.op("v", mk_sua(), waits=[("tm", tmu)], inc="vm")
                    su_done = rec.op("v", lambda v: v.tensor_tensor(
                        out=S[:], in0=S[:], in1=ps_o[:], op=ALU.add),
                        waits=[("vm", ea)], inc="vm")
                    ch3[m] = su_done
                    if m == NCH - 1:
                        ea = rec.op("v", lambda v: v.tensor_scalar(
                            out=S[:], in0=S[:], scalar1=carry[:, 0:1], scalar2=None,
                            op0=ALU.mult), waits=[("vm", su_done)], inc="vm")
                        su_done = rec.op("v", lambda v: v.tensor_copy(S_bf[:], S[:]),
                                         waits=[("vm", ea)], inc="vm")
                    continue

                gb = g_t3[bsel]
                ldq_ = rec.op("sy", (lambda mm=mm: lambda s: s.dma_start(
                    out=q_t[:], in_=q_scr[mm]))(),
                    waits=[("tm", prev_scq), ("vm", rec.val("vm") if m > NCH else 0),
                           ("gw", qw_gw[NCH + mm])], inc="ld")
                ldkf = rec.op("sy", (lambda mm=mm: lambda s: s.dma_start(
                    out=kf_t[:], in_=kf_scr[mm]))(),
                    waits=[("gw", kfw_gw[NCH + mm])], inc="ld")
                ldg = rec.op("sy", (lambda gb=gb, mm=mm: lambda s: s.dma_start(
                    out=gb[:].rearrange("p (a n) -> p a n", a=2),
                    in_=g_scr[mm].rearrange("a p n -> p a n")))(),
                    waits=[("gw", gw_gw[NCH + mm])], inc="ld")
                ldx_ = rec.op("sy", (lambda mm=mm: lambda s: s.dma_start(
                    out=x1_t[:].rearrange("p (c n) -> p c n", c=8),
                    in_=xo3[:, :, C * mm:C * mm + C]))(),
                    waits=[("vm", ch3[m - 1]),
                           ("gw", x1w_gw[mm - 1] if mm >= 1 else 0)], inc="ld")
                vqf = rec.op("v", lambda v: v.tensor_tensor(
                    out=qf_t[:], in0=q_t[:], in1=qfac[:], op=ALU.mult),
                    waits=[("ld", ldq_), ("tm", rec.val("tm")), ("sm", consts_sm)],
                    inc="vm")
                mask_done = {}
                tms_last = 0
                for h in range(NH):
                    wts = [("ld", ldkf), ("sm", consts_sm)]
                    if h >= 2:
                        wts.append(("vm", mask_done[h - 2]))
                    else:
                        wts.append(("vm", prev_mask[h]))
                        wts.append(("tm", prev_av))
                    def mk_sc(h=h):
                        def f(t):
                            r = None
                            for tt in range(2):
                                r = t.matmul(ps_sc[h % 2][:, 256 * tt:256 * tt + 256],
                                             kf_t[:, 256 * h + 128 * tt:
                                                  256 * h + 128 * tt + 128],
                                             q_t[:, 256 * h:256 * h + 256],
                                             start=True, stop=True)
                            return r
                        return f
                    tms_last = rec.op("t", mk_sc(), waits=wts, inc="tm")
                    def mk_mask(h=h):
                        def f(v):
                            r = None
                            for tt in range(2):
                                r = v.tensor_tensor(
                                    out=sc_bf[:, 512 * h + 256 * tt:
                                              512 * h + 256 * tt + 256],
                                    in0=ps_sc[h % 2][:, 256 * tt:256 * tt + 256],
                                    in1=dmask[tt][:, 256 * h:256 * h + 256],
                                    op=ALU.mult)
                            return r
                        return f
                    mask_done[h] = rec.op("v", mk_mask(), waits=[("tm", tms_last)],
                                          inc="vm")
                prev_mask = (mask_done[6], mask_done[7])

                apply_done = {}
                tma = 0
                for ts in range(2):
                    wts = [("vm", mask_done[NH - 1]), ("vm", vqf), ("ld", ldv_),
                           ("vm", su_done)]
                    if ts == 1:
                        wts.append(("vm", apply_done[0]))
                    elif m > NCH:
                        wts.append(("vm", ch3[m - 1]))
                    def mk_av(ts=ts, vb=vb):
                        def f(t):
                            r = None
                            for h in range(NH):
                                sl = slice(256 * h, 256 * h + 256)
                                for tt in range(2):
                                    t.matmul(ps_o[:, sl],
                                             sc_bf[:, 512 * h + 256 * tt + 128 * ts:
                                                   512 * h + 256 * tt + 128 * ts + 128],
                                             vb[:, 2048 * tt + 256 * h:
                                                2048 * tt + 256 * h + 256],
                                             start=(tt == 0), stop=False)
                                r = t.matmul(ps_o[:, sl],
                                             qf_t[:, 256 * h + 128 * ts:
                                                  256 * h + 128 * ts + 128],
                                             S_bf[:, sl], start=False, stop=True)
                            return r
                        return f
                    tma = rec.op("t", mk_av(), waits=wts, inc="tm")
                    smq = rec.op("sc", lambda sc: sc.activation(
                        out=o_sq[:], in_=ps_o[:], func=AF.Square),
                        waits=[("tm", tma), ("vm", rec.val("vm")),
                               ("gw", rec.val("gw") if DBG_SCROUT else 0)], inc="sm")
                    s0 = 32 * ts
                    def mk_red(s0=s0):
                        def f(v):
                            r = None
                            for h in range(NH):
                                sl = slice(256 * h, 256 * h + 256)
                                v.tensor_reduce(out=stats[:, s0 + h:s0 + h + 1],
                                                in_=ps_o[:, sl], axis=AX, op=ALU.add)
                                r = v.tensor_reduce(out=stats[:, s0 + 8 + h:s0 + 9 + h],
                                                    in_=o_sq[:, sl], axis=AX, op=ALU.add)
                            return r
                        return f
                    vmr = rec.op("v", mk_red(), waits=[("tm", tma), ("sm", smq)],
                                 inc="vm")
                    if DBG_SCROUT:
                        rec.op("g", (lambda mm=mm, ts=ts: lambda g: g.dma_start(
                            out=dbg_osq[mm, ts], in_=o_sq[:]))(),
                            waits=[("vm", vmr)], inc="gw")
                        rec.op("g", (lambda mm=mm, ts=ts: lambda g: g.dma_start(
                            out=dbg_a[mm, ts], in_=stats[:]))(),
                            waits=[("vm", vmr)], inc="gw")
                    e1 = rec.op("v", (lambda s0=s0: lambda v: v.tensor_scalar(
                        out=stats[:, s0 + 16:s0 + 24], in0=stats[:, s0:s0 + 8],
                        scalar1=1.0 / HDV, scalar2=None, op0=ALU.mult))(),
                        waits=[("vm", vmr)], inc="vm")
                    e2 = rec.op("v", (lambda s0=s0: lambda v: v.tensor_tensor(
                        out=stats[:, s0 + 24:s0 + 32], in0=stats[:, s0 + 16:s0 + 24],
                        in1=stats[:, s0 + 16:s0 + 24], op=ALU.mult))(),
                        waits=[("vm", e1)], inc="vm")
                    vmc = rec.op("v", (lambda s0=s0: lambda v: v.scalar_tensor_tensor(
                        out=stats[:, s0 + 8:s0 + 16], in0=stats[:, s0 + 8:s0 + 16],
                        scalar=1.0 / HDV, in1=stats[:, s0 + 24:s0 + 32],
                        op0=ALU.mult, op1=ALU.subtract))(),
                        waits=[("vm", e2)], inc="vm")
                    if DBG_SCROUT:
                        rec.op("g", (lambda mm=mm, ts=ts: lambda g: g.dma_start(
                            out=dbg_b[mm, ts], in_=stats[:]))(),
                            waits=[("vm", vmc)], inc="gw")
                    smln = rec.op("sc", (lambda s0=s0: lambda sc: sc.activation(
                        out=stats[:, s0 + 8:s0 + 16], in_=stats[:, s0 + 8:s0 + 16],
                        func=AF.Ln, bias=epsc[:, 0:1]))(),
                        waits=[("vm", vmc), ("gw", rec.val("gw") if DBG_SCROUT else 0)],
                        inc="sm")
                    smc = rec.op("sc", (lambda s0=s0: lambda sc: sc.activation(
                        out=stats[:, s0 + 8:s0 + 16], in_=stats[:, s0 + 8:s0 + 16],
                        func=AF.Exp, scale=-0.5))(), waits=[("sm", smln)], inc="sm")
                    if DBG_SCROUT and ts == 1:
                        rec.op("g", (lambda mm=mm: lambda g: g.dma_start(
                            out=dbg_st[mm], in_=stats[:]))(),
                            waits=[("sm", smc)], inc="gw")
                    def mk_app(ts=ts, s0=s0, gb=gb):
                        def f(v):
                            for h in range(NH):
                                sl = slice(256 * h, 256 * h + 256)
                                v.tensor_scalar(
                                    out=nrm_t[:, sl], in0=ps_o[:, sl],
                                    scalar1=stats[:, s0 + 16 + h:s0 + 17 + h],
                                    scalar2=stats[:, s0 + 8 + h:s0 + 9 + h],
                                    op0=ALU.subtract, op1=ALU.mult)
                            return v.tensor_tensor(
                                out=z_t[:], in0=nrm_t[:],
                                in1=gb[:, 2048 * ts:2048 * ts + 2048], op=ALU.mult)
                        return f
                    vap = rec.op("v", mk_app(), waits=[("sm", smc), ("ld", ldg)],
                                 inc="vm")
                    apply_done[ts] = vap
                    for wv in range(2):
                        def mk_ztr(wv=wv):
                            def f(t):
                                r = None
                                for jj in range(8):
                                    j = 8 * wv + jj
                                    r = t.transpose(ps_tr2[:, 128 * jj:128 * jj + 128],
                                                    z_t[:, 128 * j:128 * j + 128],
                                                    ident[:])
                                return r
                            return f
                        tmz = rec.op("t", mk_ztr(),
                                     waits=[("vm", vap), ("vm", rec.val("vm"))],
                                     inc="tm")
                        def mk_zev(wv=wv, ts=ts):
                            def f(v):
                                r = None
                                for jj in range(8):
                                    j = 8 * wv + jj
                                    r = v.tensor_copy(
                                        zf_t[:, 256 * j + 128 * ts:
                                             256 * j + 128 * ts + 128],
                                        ps_tr2[:, 128 * jj:128 * jj + 128])
                                return r
                            return f
                        apply_done[ts] = rec.op("v", mk_zev(), waits=[("tm", tmz)],
                                                inc="vm")
                prev_av = tma
                prev_scq = tma
                zf_done = apply_done[1]

                def mk_U2(kwb=kwb, vb=vb):
                    def f(t):
                        r = None
                        for h in range(NH):
                            for tt in range(2):
                                r = t.matmul(ps_o[:, 256 * h:256 * h + 256],
                                             kwb[:, 1024 * tt + 128 * h:
                                                1024 * tt + 128 * h + 128],
                                             vb[:, 2048 * tt + 256 * h:
                                                2048 * tt + 256 * h + 256],
                                             start=(tt == 0), stop=(tt == 1))
                        return r
                    return f
                tmu = rec.op("t", mk_U2(), waits=[("ld", ldv_), ("vm", apply_done[1])],
                             inc="tm")
                def mk_su2a():
                    def f(v):
                        r = None
                        for h in range(NH):
                            r = v.tensor_scalar(out=S[:, 256 * h:256 * h + 256],
                                                in0=S[:, 256 * h:256 * h + 256],
                                                scalar1=alph[:, h:h + 1], scalar2=None,
                                                op0=ALU.mult)
                        return r
                    return f
                ea = rec.op("v", mk_su2a(), waits=[("tm", tmu)], inc="vm")
                ea = rec.op("v", lambda v: v.tensor_tensor(
                    out=S[:], in0=S[:], in1=ps_o[:], op=ALU.add),
                    waits=[("vm", ea)], inc="vm")
                su_done = rec.op("v", lambda v: v.tensor_copy(S_bf[:], S[:]),
                                 waits=[("vm", ea)], inc="vm")

                def mk_proj():
                    def f(t):
                        r = None
                        for dmc in range(8):
                            for j in range(16):
                                r = t.matmul(ps_o[:, 256 * dmc:256 * dmc + 256],
                                             arena[:, 1024 * j + 128 * dmc:
                                                   1024 * j + 128 * dmc + 128],
                                             zf_t[:, 256 * j:256 * j + 256],
                                             start=(j == 0), stop=(j == 15))
                        return r
                    return f
                tmp_ = rec.op("t", mk_proj(), waits=[("vm", su_done), ("vm", zf_done),
                                                     ("gw", gw_wp)], inc="tm")
                def mk_x1():
                    def f(v):
                        r = None
                        for cc in range(8):
                            sl = slice(256 * cc, 256 * cc + 256)
                            r = v.tensor_tensor(out=x1_t[:, sl], in0=x1_t[:, sl],
                                                in1=ps_o[:, sl], op=ALU.add)
                        return r
                    return f
                vx1 = rec.op("v", mk_x1(), waits=[("tm", tmp_), ("ld", ldx_)], inc="vm")
                x1w_gw[mm] = rec.op("g", (lambda mm=mm: lambda g: g.dma_start(
                    out=x1_scr[mm], in_=x1_t[:]))(), waits=[("vm", vx1)], inc="gw")
                smsq = rec.op("sc", lambda sc: sc.activation(
                    out=o_sq[:], in_=x1_t[:], func=AF.Square),
                    waits=[("vm", vx1)], inc="sm")
                def mk_rows2():
                    def f(t):
                        r = None
                        for cc in range(8):
                            r = t.matmul(ps_ab2[0:1, 0:256], ones_c[:],
                                         x1_t[:, 256 * cc:256 * cc + 256],
                                         start=(cc == 0), stop=(cc == 7))
                        for cc in range(8):
                            r = t.matmul(ps_ab2[0:1, 256:512], ones_c[:],
                                         o_sq[:, 256 * cc:256 * cc + 256],
                                         start=(cc == 0), stop=(cc == 7))
                        return r
                    return f
                tmr = rec.op("t", mk_rows2(),
                             waits=[("sm", smsq), ("vm", rec.val("vm"))], inc="tm")
                ee = rec.op("v", lambda v: v.tensor_copy(rows[0:1, 0:512],
                                                         ps_ab2[0:1, 0:512]),
                            waits=[("tm", tmr)], inc="vm")
                ee = rec.op("v", lambda v: v.tensor_scalar(
                    out=rows[0:1, 512:768], in0=rows[0:1, 0:256], scalar1=1.0 / D,
                    scalar2=None, op0=ALU.mult), waits=[("vm", ee)], inc="vm")
                ee = rec.op("v", lambda v: v.tensor_tensor(
                    out=rows[0:1, 768:1024], in0=rows[0:1, 512:768],
                    in1=rows[0:1, 512:768], op=ALU.mult), waits=[("vm", ee)], inc="vm")
                vch = rec.op("v", lambda v: v.scalar_tensor_tensor(
                    out=rows[0:1, 1024:1280], in0=rows[0:1, 256:512], scalar=1.0 / D,
                    in1=rows[0:1, 768:1024], op0=ALU.mult, op1=ALU.subtract),
                    waits=[("vm", ee)], inc="vm")
                smln = rec.op("sc", lambda sc: sc.activation(
                    out=rows[0:1, 1024:1280], in_=rows[0:1, 1024:1280], func=AF.Ln,
                    bias=eps1[0:1, 0:1]), waits=[("vm", vch)], inc="sm")
                sch = rec.op("sc", lambda sc: sc.activation(
                    out=rows[0:1, 0:256], in_=rows[0:1, 1024:1280], func=AF.Exp,
                    scale=-0.5), waits=[("sm", smln)], inc="sm")
                vbr = rec.op("v", lambda v: v.scalar_tensor_tensor(
                    out=rows[0:1, 256:512], in0=rows[0:1, 512:768], scalar=-1.0,
                    in1=rows[0:1, 0:256], op0=ALU.mult, op1=ALU.mult),
                    waits=[("sm", sch)], inc="vm")
                tmb = rec.op("t", lambda t: (
                    t.matmul(ps_ab2[:, 0:256], ones_r[:], rows[0:1, 0:256],
                             start=True, stop=True),
                    t.matmul(ps_ab2[:, 256:512], ones_r[:], rows[0:1, 256:512],
                             start=True, stop=True))[-1],
                    waits=[("vm", vbr)], inc="tm")
                def mk_h2a():
                    def f(v):
                        r = None
                        for cc in range(8):
                            sl = slice(256 * cc, 256 * cc + 256)
                            r = v.tensor_tensor(out=o_sq[:, sl], in0=x1_t[:, sl],
                                                in1=ps_ab2[:, 0:256], op=ALU.mult)
                        return r
                    return f
                def mk_h2b():
                    def f(v):
                        r = None
                        for cc in range(8):
                            sl = slice(256 * cc, 256 * cc + 256)
                            r = v.tensor_tensor(out=h2_t[:, sl], in0=o_sq[:, sl],
                                                in1=ps_ab2[:, 256:512], op=ALU.add)
                        return r
                    return f
                va2 = rec.op("v", mk_h2a(),
                             waits=[("tm", tmb), ("vm", rec.val("vm"))], inc="vm")
                vh2 = rec.op("v", mk_h2b(),
                             waits=[("vm", va2),
                                    ("gw", h2w_gw[mm - 1] if mm >= 1 else 0)], inc="vm")
                h2w_gw[mm] = rec.op("g", (lambda mm=mm: lambda g: g.dma_start(
                    out=h2_scr[mm], in_=h2_t[:]))(), waits=[("vm", vh2)], inc="gw")
                ch3[m] = vh2
            barrier = rec.finish()
        recs.append(rec)
        if DBG_MAX_PHASE < 5:
            return _assemble(nc, recs, sems)

        rec = Rec(counts)
        with ExitStack() as p4s:
            pe4 = p4s.enter_context
            w2x = pe4(nc.sbuf_tensor("p4_w2", [128, 32768], BF16))
            h2_l = pe4(nc.sbuf_tensor("p4_h2", [128, 2048], BF16))
            act_sb = pe4(nc.sbuf_tensor("p4_act", [128, 16384], BF16))
            qcc = pe4(nc.sbuf_tensor("p4_qcc", [128, 32], F32))
            th2 = pe4(nc.sbuf_tensor("p4_th2", [128, 32], F32))
            b1s = pe4(nc.sbuf_tensor("p4_b1", [128, 32], F32))
            b2s = pe4(nc.sbuf_tensor("p4_b2", [128, 8], F32))
            wi_s = pe4(nc.sbuf_tensor("p4_wi", [128, 32], F32))
            tmp32 = pe4(nc.sbuf_tensor("p4_tmp32", [128, 96], F32))
            tmpi = pe4(nc.sbuf_tensor("p4_tmpi", [128, 32], I32))
            a_blk = [pe4(nc.sbuf_tensor(f"p4_a{i}", [128, 1024], F32)) for i in range(2)]
            ki_blk = pe4(nc.sbuf_tensor("p4_ki", [128, 1024], I32))
            s_bf = [pe4(nc.sbuf_tensor(f"p4_s{i}", [128, 1024], BF16)) for i in range(2)]
            h2q = [pe4(nc.sbuf_tensor(f"p4_h2q{i}", [128, 256], F32)) for i in range(2)]
            ps_hid = [pe4(nc.psum_tensor(f"p4_hid{i}", [128, 1024], F32))
                      for i in range(2)]
            ps_y2 = pe4(nc.psum_tensor("p4_y2", [128, 2048], F32))
            x1_l = px

            rec.op("g", lambda g: g.dma_start(
                out=arena[:].rearrange("p (c n) -> p c n", c=8),
                in_=W1.rearrange("(c p) n -> p c n", p=128)),
                waits=barrier, inc="gw")
            gw_w1 = rec.val("gw")
            rec.op("g", lambda g: g.dma_start(
                out=w2x[:].rearrange("p (f n) -> p f n", f=32),
                in_=W2.rearrange("(f p) n -> p f n", p=128)), inc="gw")
            gw_w2 = rec.val("gw")
            ldc = 0
            for sb, di in [(b1s, b1c), (b2s, b2c), (th2, qpc), (qcc, qrc),
                           (wi_s, qic)]:
                ldc = rec.op("sy", (lambda sb=sb, di=di: lambda s: s.dma_start(
                    out=sb[:], in_=di))(),
                    waits=(barrier if ldc == 0 else ()), inc="ld")

            def vop(fn, waits):
                return rec.op("v", fn, waits=waits, inc="vm")

            e = vop(lambda v: v.tensor_scalar(out=tmp32[:, 0:32], in0=th2[:],
                                              scalar1=1.0 / TWO_PI, scalar2=None,
                                              op0=ALU.mult),
                    [("ld", ldc)] + list(barrier))
            e = vop(lambda v: v.tensor_copy(tmpi[:], tmp32[:, 0:32]), [("vm", e)])
            e = vop(lambda v: v.scalar_tensor_tensor(
                out=tmp32[:, 0:32], in0=tmpi[:], scalar=-1.0,
                in1=tmp32[:, 0:32], op0=ALU.mult, op1=ALU.add), [("vm", e)])
            sq = rec.op("sc", lambda sc: sc.activation(
                out=tmp32[:, 64:96], in_=tmp32[:, 0:32], func=AF.Sin, scale=TWO_PI),
                waits=[("vm", e)], inc="sm")
            e = vop(lambda v: v.tensor_scalar(out=tmp32[:, 0:32], in0=th2[:],
                                              scalar1=1.0 / TWO_PI, scalar2=0.25,
                                              op0=ALU.mult, op1=ALU.add),
                    [("sm", sq)])
            e = vop(lambda v: v.tensor_copy(tmpi[:], tmp32[:, 0:32]), [("vm", e)])
            e = vop(lambda v: v.scalar_tensor_tensor(
                out=tmp32[:, 0:32], in0=tmpi[:], scalar=-1.0,
                in1=tmp32[:, 0:32], op0=ALU.mult, op1=ALU.add), [("vm", e)])
            sq = rec.op("sc", lambda sc: sc.activation(
                out=tmp32[:, 0:32], in_=tmp32[:, 0:32], func=AF.Sin, scale=TWO_PI),
                waits=[("vm", e)], inc="sm")
            e = vop(lambda v: v.tensor_tensor(out=tmp32[:, 32:64], in0=tmp32[:, 0:32],
                                              in1=qcc[:], op=ALU.mult), [("sm", sq)])
            e = vop(lambda v: v.tensor_tensor(out=tmp32[:, 0:32], in0=tmp32[:, 64:96],
                                              in1=wi_s[:], op=ALU.mult), [("vm", e)])
            e = vop(lambda v: v.tensor_tensor(out=tmp32[:, 32:64], in0=tmp32[:, 32:64],
                                              in1=tmp32[:, 0:32], op=ALU.add),
                    [("vm", e)])
            e = vop(lambda v: v.scalar_tensor_tensor(
                out=qcc[:], in0=tmp32[:, 32:64], scalar=0.1, in1=wi_s[:],
                op0=ALU.mult, op1=ALU.mult), [("vm", e)])
            e = vop(lambda v: v.scalar_tensor_tensor(
                out=th2[:], in0=b1s[:], scalar=0.1, in1=th2[:],
                op0=ALU.mult, op1=ALU.add), [("vm", e)])
            vq = vop(lambda v: v.tensor_scalar(out=th2[:], in0=th2[:],
                                               scalar1=1.0 / TWO_PI, scalar2=None,
                                               op0=ALU.mult), [("vm", e)])
            if DBG_SCROUT:
                rec.op("g", lambda g: g.dma_start(out=dbg_qc[:, 0:32], in_=qcc[:]),
                       waits=[("vm", vq)], inc="gw")
                rec.op("g", lambda g: g.dma_start(out=dbg_qc[:, 32:64], in_=th2[:]),
                       waits=[("vm", vq)], inc="gw")

            NBLK = 8
            ch4 = [0] * NCH
            st_w = [0] * NCH
            ldx1_of = [0] * NCH
            tmo_of = [0] * NCH
            relu_vals = {}
            sin_of = {}
            sttl_of = {}
            rstt_of = {}
            tmf_last_prev = 0
            for m in range(NCH):
                x1b = x1_l[m % 2]
                asel = m % 2
                wfree = ([("vm", ch4[m - 2])] if m >= 2 else [("vm", vq)])
                ldh2 = rec.op("sy", (lambda m=m: lambda s: s.dma_start(
                    out=h2_l[:], in_=h2_scr[m]))(),
                    waits=wfree + [("gw", h2w_gw[m]), ("tm", tmf_last_prev)], inc="ld")
                ldx1_of[m] = rec.op("sy", (lambda x1b=x1b, m=m: lambda s: s.dma_start(
                    out=x1b[:], in_=x1_scr[m]))(),
                    waits=[("gw", x1w_gw[m]), ("st", st_w[m - 2] if m >= 2 else 0)],
                    inc="ld")

                tmf_of = {}
                def emit_blk(b, m=m, ldh2=ldh2):
                    g = m * NBLK + b
                    wts = [("ld", ldh2), ("gw", gw_w1), ("vm", vq)]
                    gprev = g - 2
                    if gprev >= 0:
                        mp, bp = gprev // NBLK, gprev % NBLK
                        if (mp, 4 * bp + 3) in relu_vals:
                            wts.append(("sm", relu_vals[(mp, 4 * bp + 3)]))
                    def mk_f1(b=b):
                        def f(t):
                            r = None
                            for fo in range(4):
                                ff = 4 * b + fo
                                for cc in range(8):
                                    r = t.matmul(
                                        ps_hid[b % 2][:, 256 * fo:256 * fo + 256],
                                        arena[:, 4096 * cc + 128 * ff:
                                              4096 * cc + 128 * ff + 128],
                                        h2_l[:, 256 * cc:256 * cc + 256],
                                        start=(cc == 0), stop=(cc == 7))
                            return r
                        return f
                    return rec.op("t", mk_f1(), waits=wts, inc="tm")

                tmf_of[0] = emit_blk(0)
                tmf_of[1] = emit_blk(1)
                for b in range(NBLK):
                    pr = b % 2
                    g = m * NBLK + b
                    wts = [("tm", tmf_of[b])]
                    if (g - 2) >= 0 and ((g - 2) // NBLK, (g - 2) % NBLK) in sin_of:
                        wts.append(("sm", sin_of[((g - 2) // NBLK, (g - 2) % NBLK)]))
                    def mk_ts(b=b, pr=pr, m=m):
                        def f(v):
                            r = None
                            for fo in range(4):
                                ff = 4 * b + fo
                                r = v.tensor_scalar(
                                    out=a_blk[pr][:, 256 * fo:256 * fo + 256],
                                    in0=ps_hid[b % 2][:, 256 * fo:256 * fo + 256],
                                    scalar1=0.1 / TWO_PI, scalar2=th2[:, ff:ff + 1],
                                    op0=ALU.mult, op1=ALU.add)
                            return r
                        return f
                    vts = rec.op("v", mk_ts(), waits=wts, inc="vm")
                    pm1 = rec.op("g", (lambda pr=pr: lambda g: g.tensor_copy(
                        ki_blk[:], a_blk[pr][:]))(),
                        waits=[("vm", vts), ("pm", rec.val("pm"))], inc="pm", amt=1)
                    pm2 = rec.op("g", (lambda pr=pr: lambda g: g.tensor_tensor(
                        out=a_blk[pr][:], in0=a_blk[pr][:], in1=ki_blk[:],
                        op=ALU.subtract))(), waits=[("pm", pm1)], inc="pm", amt=1)
                    rstt_of[(m, b)] = pm2
                    wts = [("pm", pm2)]
                    if (g - 2) >= 0 and ((g - 2) // NBLK, (g - 2) % NBLK) in sttl_of:
                        wts.append(("vm", sttl_of[((g - 2) // NBLK, (g - 2) % NBLK)]))
                    sin_of[(m, b)] = rec.op("sc", (lambda pr=pr: lambda sc: sc.activation(
                        out=s_bf[pr][:], in_=a_blk[pr][:], func=AF.Sin,
                        scale=TWO_PI))(), waits=wts, inc="sm")
                    for fo in range(4):
                        ff = 4 * b + fo
                        wts = [("sm", sin_of[(m, b)])]
                        if ff >= 2:
                            wts.append(("sm", relu_vals[(m, ff - 2)]))
                        elif m >= 1:
                            wts.append(("sm", relu_vals[(m - 1, 30 + ff)]))
                        stv_ = rec.op("v", (lambda pr=pr, fo=fo, ff=ff, b=b: lambda v:
                                            v.scalar_tensor_tensor(
                            out=h2q[ff % 2][:], in0=s_bf[pr][:, 256 * fo:256 * fo + 256],
                            scalar=qcc[:, ff:ff + 1],
                            in1=ps_hid[b % 2][:, 256 * fo:256 * fo + 256],
                            op0=ALU.mult, op1=ALU.add))(), waits=wts, inc="vm")
                        if fo == 3:
                            sttl_of[(m, b)] = stv_
                        w2_ = [("vm", stv_)]
                        if ff == 0 and m >= 2:
                            w2_.append(("tm", tmo_of[m - 2]))
                        relu_vals[(m, ff)] = rec.op("sc", (lambda pr2=ff % 2, ff=ff,
                                                           asel=asel: lambda sc:
                                                           sc.activation(
                            out=act_sb[:, 8192 * asel + 256 * ff:
                                     8192 * asel + 256 * ff + 256],
                            in_=h2q[pr2][:], func=AF.Relu,
                            bias=b1s[:, ff:ff + 1]))(), waits=w2_, inc="sm")
                    if b + 2 < NBLK:
                        tmf_of[b + 2] = emit_blk(b + 2)
                tmf_last_prev = tmf_of[NBLK - 1]
                if DBG_SCROUT:
                    rec.op("g", (lambda m=m, asel=asel: lambda g: g.dma_start(
                        out=dbg_act[m], in_=act_sb[:, 8192 * asel:8192 * asel + 8192]))(),
                        waits=[("sm", relu_vals[(m, 31)])], inc="gw")

                def emit_ffn2(mm2):
                    asel2 = mm2 % 2
                    def mk_f2(asel2=asel2):
                        def f(t):
                            r = None
                            for dmc in range(8):
                                for ff in range(32):
                                    r = t.matmul(ps_y2[:, 256 * dmc:256 * dmc + 256],
                                                 w2x[:, 1024 * ff + 128 * dmc:
                                                     1024 * ff + 128 * dmc + 128],
                                                 act_sb[:, 8192 * asel2 + 256 * ff:
                                                       8192 * asel2 + 256 * ff + 256],
                                                 start=(ff == 0), stop=(ff == 31))
                            return r
                        return f
                    tmo = rec.op("t", mk_f2(),
                                 waits=[("sm", relu_vals[(mm2, 31)]), ("gw", gw_w2),
                                        ("vm", ch4[mm2 - 1] if mm2 >= 1 else 0)],
                                 inc="tm")
                    tmo_of[mm2] = tmo
                    x1c = x1_l[mm2 % 2]
                    def mk_out(x1c=x1c):
                        def f(v):
                            r = None
                            for dmc in range(8):
                                sl = slice(256 * dmc, 256 * dmc + 256)
                                r = v.scalar_tensor_tensor(
                                    out=x1c[:, sl], in0=ps_y2[:, sl],
                                    scalar=b2s[:, dmc:dmc + 1], in1=x1c[:, sl],
                                    op0=ALU.add, op1=ALU.add)
                            return r
                        return f
                    vo = rec.op("v", mk_out(),
                                waits=[("tm", tmo), ("ld", ldx1_of[mm2])], inc="vm")
                    ch4[mm2] = vo
                    st_w[mm2] = rec.op("sy", (lambda x1c=x1c, mm2=mm2: lambda s:
                                              s.dma_start(
                        out=out3[:, :, C * mm2:C * mm2 + C],
                        in_=x1c[:].rearrange("p (c n) -> p c n", c=8)))(),
                        waits=[("vm", vo)], inc="st")

                if m >= 1:
                    emit_ffn2(m - 1)
            emit_ffn2(NCH - 1)
            stv = rec.val("st")
            rec.streams["sy"].append([lambda s: s.nop(), [("st", stv)], None, 1])
            rec.finish()
        recs.append(rec)

        return _assemble(nc, recs, sems)


def _prep_inputs(L0_attn_hidden_states, L0_attn_w_qkv, L0_attn_w_gated,
                 L0_attn_w_proj, L0_attn_decay, L0_ffn_W1, L0_ffn_b1,
                 L0_ffn_quantum_phases, L0_ffn_quantum_weights_real,
                 L0_ffn_quantum_weights_imag, L0_ffn_W2, L0_ffn_b2):
    hc = _host_consts()
    x = np.asarray(L0_attn_hidden_states, np.float32)
    wqkv = np.asarray(L0_attn_w_qkv, np.float32)
    wg = np.asarray(L0_attn_w_gated, np.float32)
    wp = np.asarray(L0_attn_w_proj, np.float32)
    decay = np.asarray(L0_attn_decay, np.float32)
    W1 = np.ascontiguousarray(np.asarray(L0_ffn_W1, np.float32))
    W2 = np.ascontiguousarray(np.asarray(L0_ffn_W2, np.float32))
    b1 = np.asarray(L0_ffn_b1, np.float32)
    b2 = np.asarray(L0_ffn_b2, np.float32)
    qp = np.asarray(L0_ffn_quantum_phases, np.float32)
    qr = np.asarray(L0_ffn_quantum_weights_real, np.float32)
    qi = np.asarray(L0_ffn_quantum_weights_imag, np.float32)

    perm = hc["perm_qk"]
    wqkT = np.ascontiguousarray(
        np.concatenate([wqkv[0:1024][perm], wqkv[1024:2048][perm]], axis=0).T)
    wvgT = np.ascontiguousarray(np.concatenate([wqkv[2048:4096], wg], 0).T)
    wpT = np.ascontiguousarray(wp.T)

    cols = lambda a, n: np.ascontiguousarray(a.reshape(n, 128).T)
    common = {
        "wqkT": wqkT, "wvgT": wvgT, "wpT": wpT, "W1": W1, "W2": W2,
        "csk_p": hc["csk0"], "snk_p": hc["snk0"],
        "delta0": hc["delta0"], "delta1": hc["delta1"], "qexp": hc["qexp"],
        "kexp": hc["kexp"], "c256": hc["c256"],
        "decay_c": np.ascontiguousarray(np.tile(decay[None, :], (128, 1))),
        "b1c": cols(b1, 32), "b2c": cols(b2, 8), "qpc": cols(qp, 32),
        "qrc": cols(qr, 32), "qic": cols(qi, 32),
        "ones_c": hc["ones_c"], "ones_r": hc["ones_r"], "ident": hc["ident"],
        "eps1": hc["eps1"], "epsc": hc["epsc"],
    }
    in_maps = []
    for c in range(NCORE):
        b, th = c // 2, c % 2
        xT = np.ascontiguousarray(x[b].T)
        m = dict(common)
        m["xT_own"] = np.ascontiguousarray(xT[:, TOW * th:TOW * (th + 1)])
        m["xT_prev"] = np.ascontiguousarray(xT[:, 0:TOW])
        m["csq"] = hc["csq1"] if th else hc["csq0"]
        m["snq"] = hc["snq1"] if th else hc["snq0"]
        m["csk_o"] = hc["csk1"] if th else hc["csk0"]
        m["snk_o"] = hc["snk1"] if th else hc["snk0"]
        m["carry"] = np.full((128, 1), float(th), np.float32)
        in_maps.append(m)
    return in_maps


def kernel(**inputs):
    in_maps = _prep_inputs(**inputs)
    nc = build_program()
    res = run_bass_kernel_spmd(nc, in_maps, core_ids=list(range(NCORE)))
    out = np.empty((B, T, D), np.float32)
    for c in range(NCORE):
        b, th = c // 2, c % 2
        out[b, TOW * th:TOW * (th + 1), :] = np.asarray(res.results[c]["outT"]).T
    return out


# version 05
# speedup vs baseline: 1.0787x; 1.0787x over previous
import math
from contextlib import ExitStack

import numpy as np
import ml_dtypes

import concourse.bass as bass
import concourse.mybir as mybir
from concourse.bass_utils import run_bass_kernel_spmd

F32 = mybir.dt.float32
BF16 = mybir.dt.bfloat16
I32 = mybir.dt.int32
AF = mybir.ActivationFunctionType
ALU = mybir.AluOpType
AX = mybir.AxisListType.X

B, T, D = 4, 8192, 1024
NH, HDQ, HDV = 8, 128, 256
TOW = T // 2
C = 256
NCH = TOW // C
SCALE = 1.0 / math.sqrt(128.0)
EPS = 1e-5
TWO_PI = 2.0 * math.pi
NCORE = 8

DBG_MAX_PHASE = 99
DBG_SCROUT = False
DBG_SERIAL = 0

_HOST_CACHE = {}


def _host_consts():
    if _HOST_CACHE:
        return _HOST_CACHE
    p = np.arange(128)
    i_loc = 16 * (p // 32) + (p % 16)
    is_e = (p % 32) < 16
    perm = np.zeros(1024, dtype=np.int64)
    for h in range(NH):
        perm[128 * h: 128 * h + 128] = 2 * (64 * h + i_loc) + (~is_e).astype(np.int64)
    _HOST_CACHE["perm_qk"] = perm
    sgn = np.where(is_e, -1.0, 1.0)

    def tables(t0, n_tok, kind):
        t = np.arange(t0, t0 + n_tok, dtype=np.float64)
        nch = n_tok // C
        cs = np.empty((128, nch, NH, C), dtype=np.float64)
        sn = np.empty((128, nch, NH, C), dtype=np.float64)
        for h in range(NH):
            i = (64 * h + i_loc).astype(np.float64)
            sv = (2.0 * i + 0.4 * D) / (1.4 * D)
            inv = 10000.0 ** (-i / 512.0)
            a = t[None, :] * inv[:, None]
            s = sv[:, None] ** (t[None, :] / 512.0)
            s = s * SCALE if kind == "q" else 1.0 / s
            cs[:, :, h, :] = (np.cos(a) * s).reshape(128, nch, C)
            sn[:, :, h, :] = (np.sin(a) * s * sgn[:, None]).reshape(128, nch, C)
        return (np.ascontiguousarray(cs.reshape(128, -1)).astype(ml_dtypes.bfloat16),
                np.ascontiguousarray(sn.reshape(128, -1)).astype(ml_dtypes.bfloat16))

    _HOST_CACHE["csq0"], _HOST_CACHE["snq0"] = tables(0, TOW, "q")
    _HOST_CACHE["csq1"], _HOST_CACHE["snq1"] = tables(TOW, TOW, "q")
    _HOST_CACHE["csk0"], _HOST_CACHE["snk0"] = tables(0, TOW, "k")
    _HOST_CACHE["csk1"], _HOST_CACHE["snk1"] = tables(TOW, TOW, "k")

    j = np.arange(C)
    for tile in range(2):
        dlt = j[None, :] - (p[:, None] + 128 * tile)
        _HOST_CACHE[f"delta{tile}"] = np.where(dlt >= 0, dlt, 1.0e9).astype(np.float32)
    _HOST_CACHE["qexp"] = np.ascontiguousarray(
        np.broadcast_to(j[None, :].astype(np.float32), (128, C)))
    _HOST_CACHE["kexp"] = np.stack([256.0 - p, 128.0 - p], axis=1).astype(np.float32)
    _HOST_CACHE["c256"] = np.full((128, 1), 256.0, np.float32)
    _HOST_CACHE["ones_c"] = np.ones((128, 1), np.float32)
    _HOST_CACHE["ones_r"] = np.ones((1, 128), np.float32)
    _HOST_CACHE["ident"] = np.eye(128).astype(ml_dtypes.bfloat16)
    _HOST_CACHE["eps1"] = np.full((1, 1), EPS, np.float32)
    _HOST_CACHE["epsc"] = np.full((128, 1), EPS, np.float32)
    return _HOST_CACHE


ENGS = ("t", "v", "sc", "sy", "g")


class Rec:
    def __init__(self, counts):
        self.streams = {e: [] for e in ENGS}
        self.counts = counts

    def op(self, eng, fn, waits=(), inc=None, amt=None):
        w = [(s, v) for (s, v) in waits if v and v > 0]
        ser = getattr(self, "serial", False)
        if ser is True or (isinstance(ser, (set, frozenset)) and eng in ser):
            w = [(sname, val) for sname, val in self.counts.items() if val > 0]
        if amt is None:
            amt = 16 if eng in ("sy", "g") else 1
        self.streams[eng].append([fn, w, inc, amt])
        if inc is not None:
            self.counts[inc] = self.counts.get(inc, 0) + amt
            return self.counts[inc]
        return None

    def val(self, sem):
        return self.counts.get(sem, 0)

    def finish(self):
        bar = []
        for e in ENGS:
            st = self.streams[e]
            if not st:
                continue
            fn, w, inc, amt = st[-1]
            if inc is None:
                st[-1] = [fn, w, "bar", amt]
                self.counts["bar"] = self.counts.get("bar", 0) + amt
                bar.append(("bar", self.counts["bar"]))
            else:
                bar.append((inc, self.counts[inc]))
        return bar


def _replay(rec, block, sems):
    def run(stream):
        def body(eng):
            for fn, waits, inc, amt in stream:
                for s, v in waits:
                    eng.wait_ge(sems[s], v)
                inst = fn(eng)
                if inc is not None:
                    inst.then_inc(sems[inc], amt)
        return body
    if rec.streams["t"]:
        block.tensor(run(rec.streams["t"]))
    if rec.streams["v"]:
        block.vector(run(rec.streams["v"]))
    if rec.streams["sc"]:
        block.scalar(run(rec.streams["sc"]))
    if rec.streams["sy"]:
        block.sync(run(rec.streams["sy"]))
    if rec.streams["g"]:
        block.gpsimd(run(rec.streams["g"]))


def _assemble(nc, recs, sems):
    for i, rec in enumerate(recs):
        with nc.Block(f"ph{i}") as blk:
            _replay(rec, blk, sems)
    return nc


def build_program():
    nc = bass.Bass("TRN2", target_bir_lowering=False)
    scr_kind = "ExternalOutput" if DBG_SCROUT else "Internal"
    dI = lambda n, sh, dt: nc.dram_tensor(n, sh, dt, kind="ExternalInput").ap()
    dS = lambda n, sh, dt: nc.dram_tensor(n, sh, dt, kind=scr_kind).ap()

    xT_own = dI("xT_own", [D, TOW], F32)
    xT_prev = dI("xT_prev", [D, TOW], F32)
    wqkT = dI("wqkT", [D, 2048], F32)
    wvgT = dI("wvgT", [D, 4096], F32)
    wpT = dI("wpT", [2048, 1024], F32)
    W1 = dI("W1", [1024, 4096], F32)
    W2 = dI("W2", [4096, 1024], F32)
    csq = dI("csq", [128, NCH * NH * C], BF16)
    snq = dI("snq", [128, NCH * NH * C], BF16)
    csk_o = dI("csk_o", [128, NCH * NH * C], BF16)
    snk_o = dI("snk_o", [128, NCH * NH * C], BF16)
    csk_p = dI("csk_p", [128, NCH * NH * C], BF16)
    snk_p = dI("snk_p", [128, NCH * NH * C], BF16)
    delta0 = dI("delta0", [128, C], F32)
    delta1 = dI("delta1", [128, C], F32)
    qexp = dI("qexp", [128, C], F32)
    kexp = dI("kexp", [128, 2], F32)
    c256 = dI("c256", [128, 1], F32)
    decay_c = dI("decay_c", [128, NH], F32)
    b1c = dI("b1c", [128, 32], F32)
    b2c = dI("b2c", [128, 8], F32)
    qpc = dI("qpc", [128, 32], F32)
    qrc = dI("qrc", [128, 32], F32)
    qic = dI("qic", [128, 32], F32)
    ones_c_i = dI("ones_c", [128, 1], F32)
    ones_r_i = dI("ones_r", [1, 128], F32)
    ident_i = dI("ident", [128, 128], BF16)
    eps1_i = dI("eps1", [1, 1], F32)
    epsc_i = dI("epsc", [128, 1], F32)
    carry_i = dI("carry", [128, 1], F32)
    outT = nc.dram_tensor("outT", [D, TOW], F32, kind="ExternalOutput").ap()

    ab_scr = dS("ab_scr", [2 * NCH, 1, 512], F32)
    h_scr = dS("h_scr", [2 * NCH, 128, 2048], BF16)
    q_scr = dS("q_scr", [NCH, 128, 2048], BF16)
    kf_scr = dS("kf_scr", [NCH, 128, 2048], BF16)
    kw_scr = dS("kw_scr", [2 * NCH, 2, 128, 1024], BF16)
    v_scr = dS("v_scr", [2 * NCH, 2, 128, 2048], BF16)
    g_scr = dS("g_scr", [NCH, 2, 128, 2048], BF16)
    x1_scr = dS("x1_scr", [NCH, 128, 2048], F32)
    dbg_st = dS("dbg_st", [NCH, 128, 64], F32)
    dbg_osq = dS("dbg_osq", [NCH, 2, 128, 2048], F32)
    dbg_a = dS("dbg_a", [NCH, 2, 128, 64], F32)
    dbg_act = dS("dbg_act", [NCH, 128, 8192], BF16)
    dbg_qc = dS("dbg_qc", [128, 64], F32)
    dbg_b = dS("dbg_b", [NCH, 2, 128, 64], F32)
    h2_scr = dS("h2_scr", [NCH, 128, 2048], BF16)

    xo3 = xT_own.rearrange("(c p) n -> p c n", p=128)
    xp3 = xT_prev.rearrange("(c p) n -> p c n", p=128)
    out3 = outT.rearrange("(c p) n -> p c n", p=128)

    counts = {}
    recs = []
    with ExitStack() as stack:
        e = stack.enter_context
        sems = {n: e(nc.semaphore(n)) for n in
                ["ld", "gw", "tm", "vm", "sm", "st", "bar", "pm"]}

        arena = e(nc.sbuf_tensor("arena", [128, 32768], BF16))
        px = [e(nc.sbuf_tensor(f"px{i}", [128, 2048], F32)) for i in range(2)]
        ones_c = e(nc.sbuf_tensor("ones_c_s", [128, 1], F32))
        ones_r = e(nc.sbuf_tensor("ones_r_s", [1, 128], F32))
        ident = e(nc.sbuf_tensor("ident_s", [128, 128], BF16))
        eps1 = e(nc.sbuf_tensor("eps1_s", [1, 1], F32))
        epsc = e(nc.sbuf_tensor("epsc_s", [128, 1], F32))
        carry = e(nc.sbuf_tensor("carry_s", [128, 1], F32))
        dcol = e(nc.sbuf_tensor("dcol", [128, NH], F32))
        lnd = e(nc.sbuf_tensor("lnd", [128, NH], F32))
        kfac = e(nc.sbuf_tensor("kfac", [128, 16], F32))
        kx_s = e(nc.sbuf_tensor("kx_s", [128, 2], F32))
        c256_s = e(nc.sbuf_tensor("c256_s", [128, 1], F32))

        rec = Rec(counts)
        for sb, di in [(ones_c, ones_c_i), (ones_r, ones_r_i), (ident, ident_i),
                       (eps1, eps1_i), (epsc, epsc_i), (carry, carry_i),
                       (dcol, decay_c), (kx_s, kexp), (c256_s, c256)]:
            rec.op("sy", (lambda sb=sb, di=di: lambda s: s.dma_start(out=sb[:], in_=di))(),
                   inc="ld")
        ldv = rec.val("ld")
        smln = rec.op("sc", lambda sc: sc.activation(out=lnd[:], in_=dcol[:],
                                                     func=AF.Ln),
                      waits=[("ld", ldv)], inc="sm")
        for h in range(NH):
            for t in range(2):
                rec.op("sc", (lambda h=h, t=t: lambda sc: sc.activation(
                    out=kfac[:, 2 * h + t: 2 * h + t + 1], in_=kx_s[:, t:t + 1],
                    func=AF.Exp, scale=lnd[:, h:h + 1]))(),
                    waits=[("sm", smln)], inc="sm")
        rec.op("g", lambda g: g.dma_start(
            out=arena[:, 0:16384].rearrange("p (c n) -> p c n", c=8),
            in_=wqkT.rearrange("(c p) n -> p c n", p=128)), inc="gw")
        gw_wqk = rec.val("gw")
        barrier = rec.finish()
        recs.append(rec)

        ab_gw = [0] * (2 * NCH)
        with ExitStack() as s01:
            pe = s01.enter_context
            ps_row = [pe(nc.psum_tensor(f"ps_row{i}", [1, 512], F32)) for i in range(2)]
            rows05 = pe(nc.sbuf_tensor("rows05", [1, 2560], F32))
            xsq2 = [pe(nc.sbuf_tensor(f"xsq{i}", [128, 2048], F32)) for i in range(2)]

            rec = Rec(counts)
            ch_done = [0] * (2 * NCH)
            row_mm = [0] * (2 * NCH)
            row_ev = [0] * (2 * NCH)
            for m in range(2 * NCH):
                xb = px[m % 2]
                xq = xsq2[m % 2]
                pr = ps_row[m % 2]
                r0 = 1280 * (m % 2)
                src = xp3 if m < NCH else xo3
                off = (m % NCH) * C
                w = ([("vm", ch_done[m - 2])] if m >= 2 else list(barrier))
                ldx = rec.op("sy", (lambda xb=xb, src=src, off=off: lambda s: s.dma_start(
                    out=xb[:].rearrange("p (c n) -> p c n", c=8),
                    in_=src[:, :, off:off + C]))(), waits=w, inc="ld")
                smv = rec.op("sc", (lambda xb=xb, xq=xq: lambda sc: sc.activation(
                    out=xq[:], in_=xb[:], func=AF.Square))(),
                    waits=[("ld", ldx), ("tm", row_mm[m - 2] if m >= 2 else 0)],
                    inc="sm")

                def mk_rows(xb=xb, xq=xq, pr=pr):
                    def f(t):
                        r = None
                        for cc in range(8):
                            r = t.matmul(pr[0:1, 0:256], ones_c[:],
                                         xb[:, 256 * cc:256 * cc + 256],
                                         start=(cc == 0), stop=(cc == 7))
                        for cc in range(8):
                            r = t.matmul(pr[0:1, 256:512], ones_c[:],
                                         xq[:, 256 * cc:256 * cc + 256],
                                         start=(cc == 0), stop=(cc == 7))
                        return r
                    return f
                row_mm[m] = tmv = rec.op("t", mk_rows(),
                                         waits=[("sm", smv),
                                                ("vm", row_ev[m - 2] if m >= 2 else 0)],
                                         inc="tm")
                ee = rec.op("v", (lambda pr=pr, r0=r0: lambda v: v.tensor_copy(
                    rows05[0:1, r0:r0 + 512], pr[0:1, 0:512]))(),
                    waits=[("tm", tmv), ("gw", ab_gw[m - 2] if m >= 2 else 0)], inc="vm")
                row_ev[m] = ee
                ee = rec.op("v", (lambda r0=r0: lambda v: v.tensor_scalar(
                    out=rows05[0:1, r0 + 512:r0 + 768], in0=rows05[0:1, r0:r0 + 256],
                    scalar1=1.0 / D, scalar2=None, op0=ALU.mult))(),
                    waits=[("vm", ee)], inc="vm")
                ee = rec.op("v", (lambda r0=r0: lambda v: v.tensor_tensor(
                    out=rows05[0:1, r0 + 768:r0 + 1024], in0=rows05[0:1, r0 + 512:r0 + 768],
                    in1=rows05[0:1, r0 + 512:r0 + 768], op=ALU.mult))(),
                    waits=[("vm", ee)], inc="vm")
                vmv = rec.op("v", (lambda r0=r0: lambda v: v.scalar_tensor_tensor(
                    out=rows05[0:1, r0 + 1024:r0 + 1280], in0=rows05[0:1, r0 + 256:r0 + 512],
                    scalar=1.0 / D, in1=rows05[0:1, r0 + 768:r0 + 1024],
                    op0=ALU.mult, op1=ALU.subtract))(), waits=[("vm", ee)], inc="vm")
                smln = rec.op("sc", (lambda r0=r0: lambda sc: sc.activation(
                    out=rows05[0:1, r0 + 1024:r0 + 1280], in_=rows05[0:1, r0 + 1024:r0 + 1280],
                    func=AF.Ln, bias=eps1[0:1, 0:1]))(), waits=[("vm", vmv)], inc="sm")
                smln = rec.op("sc", (lambda r0=r0: lambda sc: sc.activation(
                    out=rows05[0:1, r0:r0 + 256], in_=rows05[0:1, r0 + 1024:r0 + 1280],
                    func=AF.Exp, scale=-0.5))(), waits=[("sm", smln)], inc="sm")
                vmv = rec.op("v", (lambda r0=r0: lambda v: v.scalar_tensor_tensor(
                    out=rows05[0:1, r0 + 256:r0 + 512], in0=rows05[0:1, r0 + 512:r0 + 768],
                    scalar=-1.0, in1=rows05[0:1, r0:r0 + 256],
                    op0=ALU.mult, op1=ALU.mult))(), waits=[("sm", smln)], inc="vm")
                ch_done[m] = vmv
                ab_gw[m] = rec.op("g", (lambda m=m, r0=r0: lambda g: g.dma_start(
                    out=ab_scr[m], in_=rows05[0:1, r0:r0 + 512]))(),
                    waits=[("vm", vmv)], inc="gw")
            barrier = rec.finish()
            recs.append(rec)
            if DBG_MAX_PHASE < 2:
                return _assemble(nc, recs, sems)

            rec = Rec(counts)
            with ExitStack() as p1s:
                pe1 = p1s.enter_context
                h_t = [pe1(nc.sbuf_tensor(f"p1a_h{i}", [128, 2048], BF16)) for i in range(2)]
                cs_t = [pe1(nc.sbuf_tensor(f"p1a_cs{i}", [128, 2048], BF16)) for i in range(2)]
                sn_t = [pe1(nc.sbuf_tensor(f"p1a_sn{i}", [128, 2048], BF16)) for i in range(2)]
                csk_t = [pe1(nc.sbuf_tensor(f"p1a_ck{i}", [128, 2048], BF16)) for i in range(2)]
                snk_t = [pe1(nc.sbuf_tensor(f"p1a_sk{i}", [128, 2048], BF16)) for i in range(2)]
                ab_t = [pe1(nc.sbuf_tensor(f"p1a_ab{i}", [1, 512], F32)) for i in range(2)]
                t1f = pe1(nc.sbuf_tensor("p1a_t1", [128, 1024], F32))
                t2f = pe1(nc.sbuf_tensor("p1a_t2", [128, 1024], F32))
                q_sb = pe1(nc.sbuf_tensor("p1a_q", [128, 2048], BF16))
                k_sb = pe1(nc.sbuf_tensor("p1a_k", [128, 2048], BF16))
                kw_t = pe1(nc.sbuf_tensor("p1a_kw", [128, 2048], BF16))
                ps_ab = pe1(nc.psum_tensor("p1a_psab", [128, 512], F32))
                ps_mm = [pe1(nc.psum_tensor(f"p1a_mm{i}", [128, 1024], F32))
                         for i in range(2)]
                ps_tr = pe1(nc.psum_tensor("p1a_tr", [128, 1024], BF16))

                SWAP = list(range(16, 32)) + list(range(16))
                ch1 = [0] * (2 * NCH)
                pc_evac = {}
                trw = {}
                hw_gw = [0] * (2 * NCH)
                qw_gw = [0] * (2 * NCH)
                kfw_gw = [0] * (2 * NCH)
                kww_gw = [0] * (2 * NCH)
                for m in range(2 * NCH):
                    own = m >= NCH
                    mm = m % NCH
                    xb, hb = px[m % 2], h_t[m % 2]
                    csb, snb = cs_t[m % 2], sn_t[m % 2]
                    cskb, snkb = csk_t[m % 2], snk_t[m % 2]
                    abb = ab_t[m % 2]
                    src = xo3 if own else xp3
                    off = mm * C
                    wfree = ([("vm", ch1[m - 2])] if m >= 2 else list(barrier))
                    rec.op("sy", (lambda xb=xb, src=src, off=off: lambda s: s.dma_start(
                        out=xb[:].rearrange("p (c n) -> p c n", c=8),
                        in_=src[:, :, off:off + C]))(), waits=wfree, inc="ld")
                    if own:
                        rec.op("sy", (lambda csb=csb, mm=mm: lambda s: s.dma_start(
                            out=csb[:], in_=csq[:, 2048 * mm:2048 * (mm + 1)]))(),
                            inc="ld")
                        rec.op("sy", (lambda snb=snb, mm=mm: lambda s: s.dma_start(
                            out=snb[:], in_=snq[:, 2048 * mm:2048 * (mm + 1)]))(),
                            inc="ld")
                    ck = csk_o if own else csk_p
                    sk = snk_o if own else snk_p
                    rec.op("sy", (lambda cskb=cskb, ck=ck, mm=mm: lambda s: s.dma_start(
                        out=cskb[:], in_=ck[:, 2048 * mm:2048 * (mm + 1)]))(), inc="ld")
                    rec.op("sy", (lambda snkb=snkb, sk=sk, mm=mm: lambda s: s.dma_start(
                        out=snkb[:], in_=sk[:, 2048 * mm:2048 * (mm + 1)]))(), inc="ld")
                    ldt = rec.op("sy", (lambda abb=abb, m=m: lambda s: s.dma_start(
                        out=abb[:], in_=ab_scr[m]))(), waits=[("gw", ab_gw[m])], inc="ld")

                    tmv = rec.op("t", (lambda abb=abb: lambda t: (
                        t.matmul(ps_ab[:, 0:256], ones_r[:], abb[0:1, 0:256],
                                 start=True, stop=True),
                        t.matmul(ps_ab[:, 256:512], ones_r[:], abb[0:1, 256:512],
                                 start=True, stop=True))[-1])(),
                        waits=[("ld", ldt), ("vm", ch1[m - 1] if m >= 1 else 0)],
                        inc="tm")

                    def mk_applyA(xb=xb, hb=hb):
                        def f(v):
                            r = None
                            for cc in range(8):
                                sl = slice(256 * cc, 256 * cc + 256)
                                dst = (t1f if cc < 4 else t2f)[:, 256 * (cc % 4):
                                                               256 * (cc % 4) + 256]
                                r = v.tensor_tensor(out=dst, in0=xb[:, sl],
                                                    in1=ps_ab[:, 0:256], op=ALU.mult)
                            return r
                        return f
                    def mk_applyB(hb=hb):
                        def f(v):
                            r = None
                            for cc in range(8):
                                sl = slice(256 * cc, 256 * cc + 256)
                                src = (t1f if cc < 4 else t2f)[:, 256 * (cc % 4):
                                                               256 * (cc % 4) + 256]
                                r = v.tensor_tensor(out=hb[:, sl], in0=src,
                                                    in1=ps_ab[:, 256:512], op=ALU.add)
                            return r
                        return f
                    va_ = rec.op("v", mk_applyA(),
                                 waits=[("tm", tmv),
                                        ("gw", hw_gw[m - 2] if m >= 2 else 0)],
                                 inc="vm")
                    vh = rec.op("v", mk_applyB(), waits=[("vm", va_)], inc="vm")
                    hw_gw[m] = rec.op("g", (lambda hb=hb, m=m: lambda g: g.dma_start(
                        out=h_scr[m], in_=hb[:]))(), waits=[("vm", vh)], inc="gw")

                    pieces = ([(0, "q"), (1, "q"), (2, "k"), (3, "k")] if own
                              else [(2, "k"), (3, "k")])
                    npieces = len(pieces)
                    for pi, (pc, kind) in enumerate(pieces):
                        dst = ps_mm[pi % 2]
                        wts = [("vm", vh), ("gw", gw_wqk)]
                        if pi >= 2:
                            wts.append(("vm", pc_evac[(m, pi - 2)]))
                        elif m >= 1:
                            prev_np = 4 if (m - 1) >= NCH else 2
                            cand = [q for q in range(prev_np) if q % 2 == pi % 2]
                            wts.append(("vm", pc_evac[(m - 1, max(cand))]))
                        def mk_piece(dst=dst, pc=pc, hb=hb):
                            def f(t):
                                r = None
                                for oi in range(4):
                                    oc = 4 * pc + oi
                                    for cc in range(8):
                                        r = t.matmul(
                                            dst[:, 256 * oi:256 * oi + 256],
                                            arena[:, 2048 * cc + 128 * oc:
                                                  2048 * cc + 128 * oc + 128],
                                            hb[:, 256 * cc:256 * cc + 256],
                                            start=(cc == 0), stop=(cc == 7))
                                return r
                            return f
                        tmp = rec.op("t", mk_piece(), waits=wts, inc="tm")
                        half = pc % 2
                        if kind == "q":
                            csx, snx, dstq = csb, snb, q_sb
                            wgw = qw_gw[m - 1] if m >= 1 else 0
                        else:
                            csx, snx, dstq = cskb, snkb, k_sb
                            wgw = max(kfw_gw[m - 1] if m >= 1 else 0,
                                      kww_gw[m - 1] if m >= 1 else 0)
                        sl = slice(1024 * half, 1024 * half + 1024)
                        def mk_xpos(dst=dst, csx=csx, snx=snx, dstq=dstq, sl=sl):
                            def f(v):
                                v.tensor_tensor(out=t1f[:], in0=dst[:], in1=csx[:, sl],
                                                op=ALU.mult)
                                v.stream_shuffle(out=t2f[:], in_=dst[:], mask=SWAP)
                                v.tensor_tensor(out=t2f[:], in0=t2f[:], in1=snx[:, sl],
                                                op=ALU.mult)
                                return v.tensor_tensor(out=dstq[:, sl], in0=t1f[:],
                                                       in1=t2f[:], op=ALU.add)
                            return f
                        pc_evac[(m, pi)] = rec.op("v", mk_xpos(),
                                                  waits=[("tm", tmp), ("ld", ldt),
                                                         ("gw", wgw)], inc="vm")
                    kdone = pc_evac[(m, npieces - 1)]

                    for ts in range(2):
                        prev = trw.get((m, ts - 1), trw.get((m - 1, 1), 0))
                        def mk_tr(ts=ts):
                            def f(t):
                                r = None
                                for h in range(NH):
                                    r = t.transpose(
                                        ps_tr[:, 128 * h:128 * h + 128],
                                        k_sb[:, 256 * h + 128 * ts:
                                             256 * h + 128 * ts + 128], ident[:])
                                return r
                            return f
                        tmv2 = rec.op("t", mk_tr(), waits=[("vm", kdone), ("vm", prev)],
                                      inc="tm")
                        def mk_trev(ts=ts):
                            def f(v):
                                r = None
                                for h in range(NH):
                                    r = v.tensor_scalar(
                                        out=kw_t[:, 1024 * ts + 128 * h:
                                                 1024 * ts + 128 * h + 128],
                                        in0=ps_tr[:, 128 * h:128 * h + 128],
                                        scalar1=kfac[:, 2 * h + ts:2 * h + ts + 1],
                                        scalar2=None, op0=ALU.mult)
                                return r
                            return f
                        w2_ = [("tm", tmv2)]
                        if ts == 0:
                            w2_.append(("gw", kww_gw[m - 1] if m >= 1 else 0))
                        trw[(m, ts)] = rec.op("v", mk_trev(), waits=w2_, inc="vm")

                    if own:
                        qw_gw[m] = rec.op("g", (lambda mm=mm: lambda g: g.dma_start(
                            out=q_scr[mm], in_=q_sb[:]))(),
                            waits=[("vm", pc_evac[(m, 1)])], inc="gw")
                        kfw_gw[m] = rec.op("g", (lambda mm=mm: lambda g: g.dma_start(
                            out=kf_scr[mm], in_=k_sb[:]))(), waits=[("vm", kdone)],
                            inc="gw")
                    kww_gw[m] = rec.op("g", (lambda m=m: lambda g: g.dma_start(
                        out=kw_scr[m].rearrange("a p n -> p a n"),
                        in_=kw_t[:].rearrange("p (a n) -> p a n", a=2)))(),
                        waits=[("vm", trw[(m, 1)])], inc="gw")
                    ch1[m] = trw[(m, 1)]
                barrier = rec.finish()
            recs.append(rec)
            if DBG_MAX_PHASE < 3:
                return _assemble(nc, recs, sems)

            rec = Rec(counts)
            with ExitStack() as p1bs:
                pe2 = p1bs.enter_context
                h_t2 = [pe2(nc.sbuf_tensor(f"p1b_h{i}", [128, 2048], BF16))
                        for i in range(2)]
                v_t2 = pe2(nc.sbuf_tensor("p1b_v", [128, 4096], BF16))
                g_t2 = pe2(nc.sbuf_tensor("p1b_g", [128, 4096], BF16))
                ps_mm2 = [pe2(nc.psum_tensor(f"p1b_mm{i}", [128, 1024], F32))
                          for i in range(3)]

                rec.op("g", lambda g: g.dma_start(
                    out=arena[:].rearrange("p (c n) -> p c n", c=8),
                    in_=wvgT.rearrange("(c p) n -> p c n", p=128)),
                    waits=barrier, inc="gw")
                gw_vg = rec.val("gw")

                ch2 = [0] * (2 * NCH)
                vw_gw = [0] * (2 * NCH)
                gw_gw = [0] * (2 * NCH)
                piece_ev = {}
                pln = {}
                for m in range(2 * NCH):
                    own = m >= NCH
                    mm = m % NCH
                    hb = h_t2[m % 2]
                    wfree = ([("vm", ch2[m - 2])] if m >= 2 else list(barrier))
                    ldh = rec.op("sy", (lambda hb=hb, m=m: lambda s: s.dma_start(
                        out=hb[:], in_=h_scr[m]))(),
                        waits=wfree + [("gw", gw_vg), ("gw", hw_gw[m])], inc="ld")
                    pl = [(ts, hf, "v") for ts in range(2) for hf in range(2)]
                    if own:
                        pl += [(ts, hf, "g") for ts in range(2) for hf in range(2)]
                    pln[m] = len(pl)
                    for pi, (ts, hf, kind) in enumerate(pl):
                        dst = ps_mm2[pi % 3]
                        off = (0 if kind == "v" else 2048) + 1024 * hf
                        wts = [("ld", ldh)]
                        if pi >= 3:
                            wts.append(piece_ev[(m, pi - 3)])
                        elif m >= 1:
                            cand = [q for q in range(pln[m - 1]) if q % 3 == pi % 3]
                            wts.append(piece_ev[(m - 1, max(cand))])
                        def mk_vg(dst=dst, off=off, hb=hb, ts=ts):
                            def f(t):
                                r = None
                                for cc in range(8):
                                    for qn in range(2):
                                        r = t.matmul(
                                            dst[:, 512 * qn:512 * qn + 512],
                                            hb[:, 256 * cc + 128 * ts:
                                               256 * cc + 128 * ts + 128],
                                            arena[:, 4096 * cc + off + 512 * qn:
                                                  4096 * cc + off + 512 * qn + 512],
                                            start=(cc == 0), stop=(cc == 7))
                                return r
                            return f
                        tmv = rec.op("t", mk_vg(), waits=wts, inc="tm")
                        dsl = slice(2048 * ts + 1024 * hf, 2048 * ts + 1024 * hf + 1024)
                        if kind == "v":
                            ev = rec.op("v", (lambda dst=dst, dsl=dsl: lambda v:
                                              v.tensor_copy(v_t2[:, dsl], dst[:]))(),
                                        waits=[("tm", tmv),
                                               ("gw", vw_gw[m - 1] if m >= 1 else 0)],
                                        inc="vm")
                            piece_ev[(m, pi)] = ("vm", ev)
                        else:
                            ev = rec.op("sc", (lambda dst=dst, dsl=dsl: lambda sc:
                                               sc.activation(out=g_t2[:, dsl],
                                                             in_=dst[:], func=AF.Silu))(),
                                        waits=[("tm", tmv),
                                               ("gw", gw_gw[m - 1] if m >= 1 else 0)],
                                        inc="sm")
                            piece_ev[(m, pi)] = ("sm", ev)
                    vw_gw[m] = rec.op("g", (lambda m=m: lambda g: g.dma_start(
                        out=v_scr[m].rearrange("a p n -> p a n"),
                        in_=v_t2[:].rearrange("p (a n) -> p a n", a=2)))(),
                        waits=[piece_ev[(m, 3)]], inc="gw")
                    if own:
                        gw_gw[m] = rec.op("g", (lambda mm=mm: lambda g: g.dma_start(
                            out=g_scr[mm].rearrange("a p n -> p a n"),
                            in_=g_t2[:].rearrange("p (a n) -> p a n", a=2)))(),
                            waits=[piece_ev[(m, 7)]], inc="gw")
                    lsem, lval = piece_ev[(m, len(pl) - 1)]
                    if lsem == "vm":
                        ch2[m] = lval
                    else:
                        ch2[m] = rec.op("v", lambda v: v.tensor_copy(
                            v_t2[:, 0:2], v_t2[:, 0:2]), waits=[(lsem, lval)], inc="vm")
                barrier = rec.finish()
            recs.append(rec)
        if DBG_MAX_PHASE < 4:
            return _assemble(nc, recs, sems)

        rec = Rec(counts)
        rec.serial = bool(DBG_SERIAL & 1)
        x1w_gw = [0] * NCH
        h2w_gw = [0] * NCH
        with ExitStack() as p2s:
            pe3 = p2s.enter_context
            dmask = [pe3(nc.sbuf_tensor(f"p2_dm{i}", [128, 2048], BF16)) for i in range(2)]
            qfac = pe3(nc.sbuf_tensor("p2_qfac", [128, 2048], BF16))
            alph = pe3(nc.sbuf_tensor("p2_alpha", [128, NH], F32))
            S = pe3(nc.sbuf_tensor("p2_S", [128, 2048], F32))
            S_bf = pe3(nc.sbuf_tensor("p2_Sbf", [128, 2048], BF16))
            dl_t = [pe3(nc.sbuf_tensor(f"p2_dl{i}", [128, 256], F32)) for i in range(2)]
            qx_t = pe3(nc.sbuf_tensor("p2_qx", [128, 256], F32))
            q_t = pe3(nc.sbuf_tensor("p2_q", [128, 2048], BF16))
            kf_t = pe3(nc.sbuf_tensor("p2_kf", [128, 2048], BF16))
            kw_t2 = [pe3(nc.sbuf_tensor(f"p2_kw{i}", [128, 2048], BF16)) for i in range(2)]
            v_t3 = [pe3(nc.sbuf_tensor(f"p2_v{i}", [128, 4096], BF16)) for i in range(2)]
            g_t3 = [pe3(nc.sbuf_tensor(f"p2_g{i}", [128, 4096], BF16)) for i in range(2)]
            qf_t = pe3(nc.sbuf_tensor("p2_qf", [128, 2048], BF16))
            sc_bf = pe3(nc.sbuf_tensor("p2_scbf", [128, 4096], BF16))
            stats = pe3(nc.sbuf_tensor("p2_stats", [128, 64], F32))
            nrm_t = pe3(nc.sbuf_tensor("p2_nrm", [128, 2048], BF16))
            z_t = pe3(nc.sbuf_tensor("p2_z", [128, 2048], BF16))
            zf_t = pe3(nc.sbuf_tensor("p2_zf", [128, 4096], BF16))
            h2_t = pe3(nc.sbuf_tensor("p2_h2", [128, 2048], BF16))
            ps_sc = [pe3(nc.psum_tensor(f"p2_pssc{i}", [128, 512], F32)) for i in range(2)]
            ps_o = pe3(nc.psum_tensor("p2_pso", [128, 2048], F32))
            ps_tr2 = pe3(nc.psum_tensor("p2_pstr", [128, 1024], BF16))
            ps_ab2 = pe3(nc.psum_tensor("p2_psab", [128, 512], F32))
            x1_t = px[0]
            scrA = pe3(nc.sbuf_tensor("scrA", [128, 2048], F32))
            rows = pe3(nc.sbuf_tensor("rows_p", [1, 1280], F32))
            o_sq = scrA

            rec.op("g", lambda g: g.dma_start(
                out=arena[:, 0:16384].rearrange("p (j n) -> p j n", j=16),
                in_=wpT.rearrange("(j p) n -> p j n", p=128)),
                waits=barrier, inc="gw")
            gw_wp = rec.val("gw")

            rec.op("sy", lambda s: s.dma_start(out=dl_t[0][:], in_=delta0),
                   waits=barrier, inc="ld")
            rec.op("sy", lambda s: s.dma_start(out=dl_t[1][:], in_=delta1), inc="ld")
            ldq = rec.op("sy", lambda s: s.dma_start(out=qx_t[:], in_=qexp), inc="ld")
            first = True
            for tile in range(2):
                for h in range(NH):
                    rec.op("sc", (lambda tile=tile, h=h: lambda sc: sc.activation(
                        out=dmask[tile][:, 256 * h:256 * h + 256], in_=dl_t[tile][:],
                        func=AF.Exp, scale=lnd[:, h:h + 1]))(),
                        waits=[("ld", ldq)] + (list(barrier) if first else []),
                        inc="sm")
                    first = False
            for h in range(NH):
                rec.op("sc", (lambda h=h: lambda sc: sc.activation(
                    out=qfac[:, 256 * h:256 * h + 256], in_=qx_t[:],
                    func=AF.Exp, scale=lnd[:, h:h + 1]))(), inc="sm")
            for h in range(NH):
                rec.op("sc", (lambda h=h: lambda sc: sc.activation(
                    out=alph[:, h:h + 1], in_=c256_s[:], func=AF.Exp,
                    scale=lnd[:, h:h + 1]))(), inc="sm")
            consts_sm = rec.val("sm")
            su_done = rec.op("v", lambda v: v.memset(S[:], 0.0), waits=barrier, inc="vm")

            ch3 = [0] * (2 * NCH)
            prev_mask = (0, 0)
            prev_av = 0
            prev_scq = 0
            for m in range(2 * NCH):
                if DBG_SERIAL & 2:
                    rec.serial = m < 18
                if DBG_SERIAL & 4:
                    rec.serial = m < 16
                if DBG_SERIAL & 8:
                    rec.serial = m in (16, 17)
                if DBG_SERIAL & 16:
                    rec.serial = {"v"} if m < 18 else False
                if DBG_SERIAL & 32:
                    rec.serial = {"t"} if m < 18 else False
                if DBG_SERIAL & 64:
                    rec.serial = {"sc"} if m < 18 else False
                if DBG_SERIAL & 128:
                    rec.serial = {"sy", "g"} if m < 18 else False
                own = m >= NCH
                mm = m % NCH
                bsel = m % 2
                kwb, vb = kw_t2[bsel], v_t3[bsel]
                wfree = ([("vm", ch3[m - 2])] if m >= 2 else [("sm", consts_sm)])
                rec.op("sy", (lambda kwb=kwb, m=m: lambda s: s.dma_start(
                    out=kwb[:].rearrange("p (a n) -> p a n", a=2),
                    in_=kw_scr[m].rearrange("a p n -> p a n")))(),
                    waits=wfree + [("gw", kww_gw[m])], inc="ld")
                ldv_ = rec.op("sy", (lambda vb=vb, m=m: lambda s: s.dma_start(
                    out=vb[:].rearrange("p (a n) -> p a n", a=2),
                    in_=v_scr[m].rearrange("a p n -> p a n")))(),
                    waits=[("gw", vw_gw[m])], inc="ld")
                if not own:
                    def mk_U(kwb=kwb, vb=vb):
                        def f(t):
                            r = None
                            for h in range(NH):
                                for tt in range(2):
                                    r = t.matmul(ps_o[:, 256 * h:256 * h + 256],
                                                 kwb[:, 1024 * tt + 128 * h:
                                                    1024 * tt + 128 * h + 128],
                                                 vb[:, 2048 * tt + 256 * h:
                                                    2048 * tt + 256 * h + 256],
                                                 start=(tt == 0), stop=(tt == 1))
                            return r
                        return f
                    tmu = rec.op("t", mk_U(), waits=[("ld", ldv_), ("vm", su_done)],
                                 inc="tm")
                    def mk_sua():
                        def f(v):
                            r = None
                            for h in range(NH):
                                r = v.tensor_scalar(out=S[:, 256 * h:256 * h + 256],
                                                    in0=S[:, 256 * h:256 * h + 256],
                                                    scalar1=alph[:, h:h + 1],
                                                    scalar2=None, op0=ALU.mult)
                            return r
                        return f
                    ea = rec.op("v", mk_sua(), waits=[("tm", tmu)], inc="vm")
                    su_done = rec.op("v", lambda v: v.tensor_tensor(
                        out=S[:], in0=S[:], in1=ps_o[:], op=ALU.add),
                        waits=[("vm", ea)], inc="vm")
                    ch3[m] = su_done
                    if m == NCH - 1:
                        ea = rec.op("v", lambda v: v.tensor_scalar(
                            out=S[:], in0=S[:], scalar1=carry[:, 0:1], scalar2=None,
                            op0=ALU.mult), waits=[("vm", su_done)], inc="vm")
                        su_done = rec.op("v", lambda v: v.tensor_copy(S_bf[:], S[:]),
                                         waits=[("vm", ea)], inc="vm")
                    continue

                gb = g_t3[bsel]
                ldq_ = rec.op("sy", (lambda mm=mm: lambda s: s.dma_start(
                    out=q_t[:], in_=q_scr[mm]))(),
                    waits=[("tm", prev_scq), ("vm", rec.val("vm") if m > NCH else 0),
                           ("gw", qw_gw[NCH + mm])], inc="ld")
                ldkf = rec.op("sy", (lambda mm=mm: lambda s: s.dma_start(
                    out=kf_t[:], in_=kf_scr[mm]))(),
                    waits=[("gw", kfw_gw[NCH + mm])], inc="ld")
                ldg = rec.op("sy", (lambda gb=gb, mm=mm: lambda s: s.dma_start(
                    out=gb[:].rearrange("p (a n) -> p a n", a=2),
                    in_=g_scr[mm].rearrange("a p n -> p a n")))(),
                    waits=[("gw", gw_gw[NCH + mm])], inc="ld")
                ldx_ = rec.op("sy", (lambda mm=mm: lambda s: s.dma_start(
                    out=x1_t[:].rearrange("p (c n) -> p c n", c=8),
                    in_=xo3[:, :, C * mm:C * mm + C]))(),
                    waits=[("vm", ch3[m - 1]),
                           ("gw", x1w_gw[mm - 1] if mm >= 1 else 0)], inc="ld")
                vqf = rec.op("v", lambda v: v.tensor_tensor(
                    out=qf_t[:], in0=q_t[:], in1=qfac[:], op=ALU.mult),
                    waits=[("ld", ldq_), ("tm", rec.val("tm")), ("sm", consts_sm)],
                    inc="vm")
                mask_done = {}
                tms_last = 0
                for h in range(NH):
                    wts = [("ld", ldkf), ("sm", consts_sm)]
                    if h >= 2:
                        wts.append(("vm", mask_done[h - 2]))
                    else:
                        wts.append(("vm", prev_mask[h]))
                        wts.append(("tm", prev_av))
                    def mk_sc(h=h):
                        def f(t):
                            r = None
                            for tt in range(2):
                                r = t.matmul(ps_sc[h % 2][:, 256 * tt:256 * tt + 256],
                                             kf_t[:, 256 * h + 128 * tt:
                                                  256 * h + 128 * tt + 128],
                                             q_t[:, 256 * h:256 * h + 256],
                                             start=True, stop=True)
                            return r
                        return f
                    tms_last = rec.op("t", mk_sc(), waits=wts, inc="tm")
                    def mk_mask(h=h):
                        def f(v):
                            r = None
                            for tt in range(2):
                                r = v.tensor_tensor(
                                    out=sc_bf[:, 512 * h + 256 * tt:
                                              512 * h + 256 * tt + 256],
                                    in0=ps_sc[h % 2][:, 256 * tt:256 * tt + 256],
                                    in1=dmask[tt][:, 256 * h:256 * h + 256],
                                    op=ALU.mult)
                            return r
                        return f
                    mask_done[h] = rec.op("v", mk_mask(), waits=[("tm", tms_last)],
                                          inc="vm")
                prev_mask = (mask_done[6], mask_done[7])

                apply_done = {}
                tma = 0
                for ts in range(2):
                    wts = [("vm", mask_done[NH - 1]), ("vm", vqf), ("ld", ldv_),
                           ("vm", su_done)]
                    if ts == 1:
                        wts.append(("vm", apply_done[0]))
                    elif m > NCH:
                        wts.append(("vm", ch3[m - 1]))
                    def mk_av(ts=ts, vb=vb):
                        def f(t):
                            r = None
                            for h in range(NH):
                                sl = slice(256 * h, 256 * h + 256)
                                for tt in range(2):
                                    t.matmul(ps_o[:, sl],
                                             sc_bf[:, 512 * h + 256 * tt + 128 * ts:
                                                   512 * h + 256 * tt + 128 * ts + 128],
                                             vb[:, 2048 * tt + 256 * h:
                                                2048 * tt + 256 * h + 256],
                                             start=(tt == 0), stop=False)
                                r = t.matmul(ps_o[:, sl],
                                             qf_t[:, 256 * h + 128 * ts:
                                                  256 * h + 128 * ts + 128],
                                             S_bf[:, sl], start=False, stop=True)
                            return r
                        return f
                    tma = rec.op("t", mk_av(), waits=wts, inc="tm")
                    smq = rec.op("sc", lambda sc: sc.activation(
                        out=o_sq[:], in_=ps_o[:], func=AF.Square),
                        waits=[("tm", tma), ("vm", rec.val("vm")),
                               ("gw", rec.val("gw") if DBG_SCROUT else 0)], inc="sm")
                    s0 = 32 * ts
                    def mk_red(s0=s0):
                        def f(v):
                            r = None
                            for h in range(NH):
                                sl = slice(256 * h, 256 * h + 256)
                                v.tensor_reduce(out=stats[:, s0 + h:s0 + h + 1],
                                                in_=ps_o[:, sl], axis=AX, op=ALU.add)
                                r = v.tensor_reduce(out=stats[:, s0 + 8 + h:s0 + 9 + h],
                                                    in_=o_sq[:, sl], axis=AX, op=ALU.add)
                            return r
                        return f
                    vmr = rec.op("v", mk_red(), waits=[("tm", tma), ("sm", smq)],
                                 inc="vm")
                    if DBG_SCROUT:
                        rec.op("g", (lambda mm=mm, ts=ts: lambda g: g.dma_start(
                            out=dbg_osq[mm, ts], in_=o_sq[:]))(),
                            waits=[("vm", vmr)], inc="gw")
                        rec.op("g", (lambda mm=mm, ts=ts: lambda g: g.dma_start(
                            out=dbg_a[mm, ts], in_=stats[:]))(),
                            waits=[("vm", vmr)], inc="gw")
                    e1 = rec.op("v", (lambda s0=s0: lambda v: v.tensor_scalar(
                        out=stats[:, s0 + 16:s0 + 24], in0=stats[:, s0:s0 + 8],
                        scalar1=1.0 / HDV, scalar2=None, op0=ALU.mult))(),
                        waits=[("vm", vmr)], inc="vm")
                    e2 = rec.op("v", (lambda s0=s0: lambda v: v.tensor_tensor(
                        out=stats[:, s0 + 24:s0 + 32], in0=stats[:, s0 + 16:s0 + 24],
                        in1=stats[:, s0 + 16:s0 + 24], op=ALU.mult))(),
                        waits=[("vm", e1)], inc="vm")
                    vmc = rec.op("v", (lambda s0=s0: lambda v: v.scalar_tensor_tensor(
                        out=stats[:, s0 + 8:s0 + 16], in0=stats[:, s0 + 8:s0 + 16],
                        scalar=1.0 / HDV, in1=stats[:, s0 + 24:s0 + 32],
                        op0=ALU.mult, op1=ALU.subtract))(),
                        waits=[("vm", e2)], inc="vm")
                    if DBG_SCROUT:
                        rec.op("g", (lambda mm=mm, ts=ts: lambda g: g.dma_start(
                            out=dbg_b[mm, ts], in_=stats[:]))(),
                            waits=[("vm", vmc)], inc="gw")
                    smln = rec.op("sc", (lambda s0=s0: lambda sc: sc.activation(
                        out=stats[:, s0 + 8:s0 + 16], in_=stats[:, s0 + 8:s0 + 16],
                        func=AF.Ln, bias=epsc[:, 0:1]))(),
                        waits=[("vm", vmc), ("gw", rec.val("gw") if DBG_SCROUT else 0)],
                        inc="sm")
                    smc = rec.op("sc", (lambda s0=s0: lambda sc: sc.activation(
                        out=stats[:, s0 + 8:s0 + 16], in_=stats[:, s0 + 8:s0 + 16],
                        func=AF.Exp, scale=-0.5))(), waits=[("sm", smln)], inc="sm")
                    if DBG_SCROUT and ts == 1:
                        rec.op("g", (lambda mm=mm: lambda g: g.dma_start(
                            out=dbg_st[mm], in_=stats[:]))(),
                            waits=[("sm", smc)], inc="gw")
                    def mk_app(ts=ts, s0=s0, gb=gb):
                        def f(v):
                            for h in range(NH):
                                sl = slice(256 * h, 256 * h + 256)
                                v.tensor_scalar(
                                    out=nrm_t[:, sl], in0=ps_o[:, sl],
                                    scalar1=stats[:, s0 + 16 + h:s0 + 17 + h],
                                    scalar2=stats[:, s0 + 8 + h:s0 + 9 + h],
                                    op0=ALU.subtract, op1=ALU.mult)
                            return v.tensor_tensor(
                                out=z_t[:], in0=nrm_t[:],
                                in1=gb[:, 2048 * ts:2048 * ts + 2048], op=ALU.mult)
                        return f
                    vap = rec.op("v", mk_app(), waits=[("sm", smc), ("ld", ldg)],
                                 inc="vm")
                    apply_done[ts] = vap
                    for wv in range(2):
                        def mk_ztr(wv=wv):
                            def f(t):
                                r = None
                                for jj in range(8):
                                    j = 8 * wv + jj
                                    r = t.transpose(ps_tr2[:, 128 * jj:128 * jj + 128],
                                                    z_t[:, 128 * j:128 * j + 128],
                                                    ident[:])
                                return r
                            return f
                        tmz = rec.op("t", mk_ztr(),
                                     waits=[("vm", vap), ("vm", rec.val("vm"))],
                                     inc="tm")
                        def mk_zev(wv=wv, ts=ts):
                            def f(v):
                                r = None
                                for jj in range(8):
                                    j = 8 * wv + jj
                                    r = v.tensor_copy(
                                        zf_t[:, 256 * j + 128 * ts:
                                             256 * j + 128 * ts + 128],
                                        ps_tr2[:, 128 * jj:128 * jj + 128])
                                return r
                            return f
                        apply_done[ts] = rec.op("v", mk_zev(), waits=[("tm", tmz)],
                                                inc="vm")
                prev_av = tma
                prev_scq = tma
                zf_done = apply_done[1]

                def mk_U2(kwb=kwb, vb=vb):
                    def f(t):
                        r = None
                        for h in range(NH):
                            for tt in range(2):
                                r = t.matmul(ps_o[:, 256 * h:256 * h + 256],
                                             kwb[:, 1024 * tt + 128 * h:
                                                1024 * tt + 128 * h + 128],
                                             vb[:, 2048 * tt + 256 * h:
                                                2048 * tt + 256 * h + 256],
                                             start=(tt == 0), stop=(tt == 1))
                        return r
                    return f
                tmu = rec.op("t", mk_U2(), waits=[("ld", ldv_), ("vm", apply_done[1])],
                             inc="tm")
                def mk_su2a():
                    def f(v):
                        r = None
                        for h in range(NH):
                            r = v.tensor_scalar(out=S[:, 256 * h:256 * h + 256],
                                                in0=S[:, 256 * h:256 * h + 256],
                                                scalar1=alph[:, h:h + 1], scalar2=None,
                                                op0=ALU.mult)
                        return r
                    return f
                ea = rec.op("v", mk_su2a(), waits=[("tm", tmu)], inc="vm")
                ea = rec.op("v", lambda v: v.tensor_tensor(
                    out=S[:], in0=S[:], in1=ps_o[:], op=ALU.add),
                    waits=[("vm", ea)], inc="vm")
                su_done = rec.op("v", lambda v: v.tensor_copy(S_bf[:], S[:]),
                                 waits=[("vm", ea)], inc="vm")

                def mk_proj():
                    def f(t):
                        r = None
                        for dmc in range(8):
                            for j in range(16):
                                r = t.matmul(ps_o[:, 256 * dmc:256 * dmc + 256],
                                             arena[:, 1024 * j + 128 * dmc:
                                                   1024 * j + 128 * dmc + 128],
                                             zf_t[:, 256 * j:256 * j + 256],
                                             start=(j == 0), stop=(j == 15))
                        return r
                    return f
                tmp_ = rec.op("t", mk_proj(), waits=[("vm", su_done), ("vm", zf_done),
                                                     ("gw", gw_wp)], inc="tm")
                def mk_x1():
                    def f(v):
                        r = None
                        for cc in range(8):
                            sl = slice(256 * cc, 256 * cc + 256)
                            r = v.tensor_tensor(out=x1_t[:, sl], in0=x1_t[:, sl],
                                                in1=ps_o[:, sl], op=ALU.add)
                        return r
                    return f
                vx1 = rec.op("v", mk_x1(), waits=[("tm", tmp_), ("ld", ldx_)], inc="vm")
                x1w_gw[mm] = rec.op("g", (lambda mm=mm: lambda g: g.dma_start(
                    out=x1_scr[mm], in_=x1_t[:]))(), waits=[("vm", vx1)], inc="gw")
                smsq = rec.op("sc", lambda sc: sc.activation(
                    out=o_sq[:], in_=x1_t[:], func=AF.Square),
                    waits=[("vm", vx1)], inc="sm")
                def mk_rows2():
                    def f(t):
                        r = None
                        for cc in range(8):
                            r = t.matmul(ps_ab2[0:1, 0:256], ones_c[:],
                                         x1_t[:, 256 * cc:256 * cc + 256],
                                         start=(cc == 0), stop=(cc == 7))
                        for cc in range(8):
                            r = t.matmul(ps_ab2[0:1, 256:512], ones_c[:],
                                         o_sq[:, 256 * cc:256 * cc + 256],
                                         start=(cc == 0), stop=(cc == 7))
                        return r
                    return f
                tmr = rec.op("t", mk_rows2(),
                             waits=[("sm", smsq), ("vm", rec.val("vm"))], inc="tm")
                ee = rec.op("v", lambda v: v.tensor_copy(rows[0:1, 0:512],
                                                         ps_ab2[0:1, 0:512]),
                            waits=[("tm", tmr)], inc="vm")
                ee = rec.op("v", lambda v: v.tensor_scalar(
                    out=rows[0:1, 512:768], in0=rows[0:1, 0:256], scalar1=1.0 / D,
                    scalar2=None, op0=ALU.mult), waits=[("vm", ee)], inc="vm")
                ee = rec.op("v", lambda v: v.tensor_tensor(
                    out=rows[0:1, 768:1024], in0=rows[0:1, 512:768],
                    in1=rows[0:1, 512:768], op=ALU.mult), waits=[("vm", ee)], inc="vm")
                vch = rec.op("v", lambda v: v.scalar_tensor_tensor(
                    out=rows[0:1, 1024:1280], in0=rows[0:1, 256:512], scalar=1.0 / D,
                    in1=rows[0:1, 768:1024], op0=ALU.mult, op1=ALU.subtract),
                    waits=[("vm", ee)], inc="vm")
                smln = rec.op("sc", lambda sc: sc.activation(
                    out=rows[0:1, 1024:1280], in_=rows[0:1, 1024:1280], func=AF.Ln,
                    bias=eps1[0:1, 0:1]), waits=[("vm", vch)], inc="sm")
                sch = rec.op("sc", lambda sc: sc.activation(
                    out=rows[0:1, 0:256], in_=rows[0:1, 1024:1280], func=AF.Exp,
                    scale=-0.5), waits=[("sm", smln)], inc="sm")
                vbr = rec.op("v", lambda v: v.scalar_tensor_tensor(
                    out=rows[0:1, 256:512], in0=rows[0:1, 512:768], scalar=-1.0,
                    in1=rows[0:1, 0:256], op0=ALU.mult, op1=ALU.mult),
                    waits=[("sm", sch)], inc="vm")
                tmb = rec.op("t", lambda t: (
                    t.matmul(ps_ab2[:, 0:256], ones_r[:], rows[0:1, 0:256],
                             start=True, stop=True),
                    t.matmul(ps_ab2[:, 256:512], ones_r[:], rows[0:1, 256:512],
                             start=True, stop=True))[-1],
                    waits=[("vm", vbr)], inc="tm")
                def mk_h2a():
                    def f(v):
                        r = None
                        for cc in range(8):
                            sl = slice(256 * cc, 256 * cc + 256)
                            r = v.tensor_tensor(out=o_sq[:, sl], in0=x1_t[:, sl],
                                                in1=ps_ab2[:, 0:256], op=ALU.mult)
                        return r
                    return f
                def mk_h2b():
                    def f(v):
                        r = None
                        for cc in range(8):
                            sl = slice(256 * cc, 256 * cc + 256)
                            r = v.tensor_tensor(out=h2_t[:, sl], in0=o_sq[:, sl],
                                                in1=ps_ab2[:, 256:512], op=ALU.add)
                        return r
                    return f
                va2 = rec.op("v", mk_h2a(),
                             waits=[("tm", tmb), ("vm", rec.val("vm"))], inc="vm")
                vh2 = rec.op("v", mk_h2b(),
                             waits=[("vm", va2),
                                    ("gw", h2w_gw[mm - 1] if mm >= 1 else 0)], inc="vm")
                h2w_gw[mm] = rec.op("g", (lambda mm=mm: lambda g: g.dma_start(
                    out=h2_scr[mm], in_=h2_t[:]))(), waits=[("vm", vh2)], inc="gw")
                ch3[m] = vh2
            barrier = rec.finish()
        recs.append(rec)
        if DBG_MAX_PHASE < 5:
            return _assemble(nc, recs, sems)

        rec = Rec(counts)
        with ExitStack() as p4s:
            pe4 = p4s.enter_context
            w2x = pe4(nc.sbuf_tensor("p4_w2", [128, 32768], BF16))
            h2_l = [pe4(nc.sbuf_tensor(f"p4_h2{i}", [128, 2048], BF16))
                    for i in range(2)]
            act_sb = pe4(nc.sbuf_tensor("p4_act", [128, 16384], BF16))
            qcc = pe4(nc.sbuf_tensor("p4_qcc", [128, 32], F32))
            th2 = pe4(nc.sbuf_tensor("p4_th2", [128, 32], F32))
            b1s = pe4(nc.sbuf_tensor("p4_b1", [128, 32], F32))
            b2s = pe4(nc.sbuf_tensor("p4_b2", [128, 8], F32))
            wi_s = pe4(nc.sbuf_tensor("p4_wi", [128, 32], F32))
            tmp32 = pe4(nc.sbuf_tensor("p4_tmp32", [128, 96], F32))
            tmpi = pe4(nc.sbuf_tensor("p4_tmpi", [128, 32], I32))
            a_blk = [pe4(nc.sbuf_tensor(f"p4_a{i}", [128, 1024], F32)) for i in range(2)]
            ki_blk = pe4(nc.sbuf_tensor("p4_ki", [128, 1024], I32))
            s_bf = [pe4(nc.sbuf_tensor(f"p4_s{i}", [128, 1024], BF16)) for i in range(2)]
            h2q = [pe4(nc.sbuf_tensor(f"p4_h2q{i}", [128, 256], F32)) for i in range(2)]
            ps_hid = [pe4(nc.psum_tensor(f"p4_hid{i}", [128, 1024], F32))
                      for i in range(2)]
            ps_y2 = pe4(nc.psum_tensor("p4_y2", [128, 2048], F32))
            x1_l = px

            rec.op("g", lambda g: g.dma_start(
                out=arena[:].rearrange("p (c n) -> p c n", c=8),
                in_=W1.rearrange("(c p) n -> p c n", p=128)),
                waits=barrier, inc="gw")
            gw_w1 = rec.val("gw")
            rec.op("g", lambda g: g.dma_start(
                out=w2x[:].rearrange("p (f n) -> p f n", f=32),
                in_=W2.rearrange("(f p) n -> p f n", p=128)), inc="gw")
            gw_w2 = rec.val("gw")
            ldc = 0
            for sb, di in [(b1s, b1c), (b2s, b2c), (th2, qpc), (qcc, qrc),
                           (wi_s, qic)]:
                ldc = rec.op("sy", (lambda sb=sb, di=di: lambda s: s.dma_start(
                    out=sb[:], in_=di))(),
                    waits=(barrier if ldc == 0 else ()), inc="ld")

            def vop(fn, waits):
                return rec.op("v", fn, waits=waits, inc="vm")

            e = vop(lambda v: v.tensor_scalar(out=tmp32[:, 0:32], in0=th2[:],
                                              scalar1=1.0 / TWO_PI, scalar2=None,
                                              op0=ALU.mult),
                    [("ld", ldc)] + list(barrier))
            e = vop(lambda v: v.tensor_copy(tmpi[:], tmp32[:, 0:32]), [("vm", e)])
            e = vop(lambda v: v.scalar_tensor_tensor(
                out=tmp32[:, 0:32], in0=tmpi[:], scalar=-1.0,
                in1=tmp32[:, 0:32], op0=ALU.mult, op1=ALU.add), [("vm", e)])
            sq = rec.op("sc", lambda sc: sc.activation(
                out=tmp32[:, 64:96], in_=tmp32[:, 0:32], func=AF.Sin, scale=TWO_PI),
                waits=[("vm", e)], inc="sm")
            e = vop(lambda v: v.tensor_scalar(out=tmp32[:, 0:32], in0=th2[:],
                                              scalar1=1.0 / TWO_PI, scalar2=0.25,
                                              op0=ALU.mult, op1=ALU.add),
                    [("sm", sq)])
            e = vop(lambda v: v.tensor_copy(tmpi[:], tmp32[:, 0:32]), [("vm", e)])
            e = vop(lambda v: v.scalar_tensor_tensor(
                out=tmp32[:, 0:32], in0=tmpi[:], scalar=-1.0,
                in1=tmp32[:, 0:32], op0=ALU.mult, op1=ALU.add), [("vm", e)])
            sq = rec.op("sc", lambda sc: sc.activation(
                out=tmp32[:, 0:32], in_=tmp32[:, 0:32], func=AF.Sin, scale=TWO_PI),
                waits=[("vm", e)], inc="sm")
            e = vop(lambda v: v.tensor_tensor(out=tmp32[:, 32:64], in0=tmp32[:, 0:32],
                                              in1=qcc[:], op=ALU.mult), [("sm", sq)])
            e = vop(lambda v: v.tensor_tensor(out=tmp32[:, 0:32], in0=tmp32[:, 64:96],
                                              in1=wi_s[:], op=ALU.mult), [("vm", e)])
            e = vop(lambda v: v.tensor_tensor(out=tmp32[:, 32:64], in0=tmp32[:, 32:64],
                                              in1=tmp32[:, 0:32], op=ALU.add),
                    [("vm", e)])
            e = vop(lambda v: v.scalar_tensor_tensor(
                out=qcc[:], in0=tmp32[:, 32:64], scalar=0.1, in1=wi_s[:],
                op0=ALU.mult, op1=ALU.mult), [("vm", e)])
            e = vop(lambda v: v.scalar_tensor_tensor(
                out=th2[:], in0=b1s[:], scalar=0.1, in1=th2[:],
                op0=ALU.mult, op1=ALU.add), [("vm", e)])
            vq = vop(lambda v: v.tensor_scalar(out=th2[:], in0=th2[:],
                                               scalar1=1.0 / TWO_PI, scalar2=None,
                                               op0=ALU.mult), [("vm", e)])
            if DBG_SCROUT:
                rec.op("g", lambda g: g.dma_start(out=dbg_qc[:, 0:32], in_=qcc[:]),
                       waits=[("vm", vq)], inc="gw")
                rec.op("g", lambda g: g.dma_start(out=dbg_qc[:, 32:64], in_=th2[:]),
                       waits=[("vm", vq)], inc="gw")

            NBLK = 8
            ch4 = [0] * NCH
            st_w = [0] * NCH
            ldx1_of = [0] * NCH
            tmo_of = [0] * NCH
            relu_vals = {}
            sin_of = {}
            sttl_of = {}
            rstt_of = {}
            tmf_last_prev = 0
            tmf_last2 = 0
            for m in range(NCH):
                x1b = x1_l[m % 2]
                asel = m % 2
                wfree = ([("vm", ch4[m - 2])] if m >= 2 else [("vm", vq)])
                h2b = h2_l[m % 2]
                ldh2 = rec.op("sy", (lambda h2b=h2b, m=m: lambda s: s.dma_start(
                    out=h2b[:], in_=h2_scr[m]))(),
                    waits=wfree + [("gw", h2w_gw[m]),
                                   ("tm", tmf_last2 if m >= 2 else 0)], inc="ld")
                ldx1_of[m] = rec.op("sy", (lambda x1b=x1b, m=m: lambda s: s.dma_start(
                    out=x1b[:], in_=x1_scr[m]))(),
                    waits=[("gw", x1w_gw[m]), ("st", st_w[m - 2] if m >= 2 else 0)],
                    inc="ld")

                tmf_of = {}
                def emit_blk(b, m=m, ldh2=ldh2):
                    g = m * NBLK + b
                    wts = [("ld", ldh2), ("gw", gw_w1), ("vm", vq)]
                    gprev = g - 2
                    if gprev >= 0:
                        mp, bp = gprev // NBLK, gprev % NBLK
                        if (mp, 4 * bp + 3) in relu_vals:
                            wts.append(("sm", relu_vals[(mp, 4 * bp + 3)]))
                    def mk_f1(b=b, h2b=h2b):
                        def f(t):
                            r = None
                            for fo in range(4):
                                ff = 4 * b + fo
                                for cc in range(8):
                                    r = t.matmul(
                                        ps_hid[b % 2][:, 256 * fo:256 * fo + 256],
                                        arena[:, 4096 * cc + 128 * ff:
                                              4096 * cc + 128 * ff + 128],
                                        h2b[:, 256 * cc:256 * cc + 256],
                                        start=(cc == 0), stop=(cc == 7))
                            return r
                        return f
                    return rec.op("t", mk_f1(), waits=wts, inc="tm")

                tmf_of[0] = emit_blk(0)
                tmf_of[1] = emit_blk(1)
                for b in range(NBLK):
                    pr = b % 2
                    g = m * NBLK + b
                    wts = [("tm", tmf_of[b])]
                    if (g - 2) >= 0 and ((g - 2) // NBLK, (g - 2) % NBLK) in sin_of:
                        wts.append(("sm", sin_of[((g - 2) // NBLK, (g - 2) % NBLK)]))
                    def mk_ts(b=b, pr=pr, m=m):
                        def f(v):
                            r = None
                            for fo in range(4):
                                ff = 4 * b + fo
                                r = v.tensor_scalar(
                                    out=a_blk[pr][:, 256 * fo:256 * fo + 256],
                                    in0=ps_hid[b % 2][:, 256 * fo:256 * fo + 256],
                                    scalar1=0.1 / TWO_PI, scalar2=th2[:, ff:ff + 1],
                                    op0=ALU.mult, op1=ALU.add)
                            return r
                        return f
                    vts = rec.op("v", mk_ts(), waits=wts, inc="vm")
                    pm1 = rec.op("v", (lambda pr=pr: lambda v: v.tensor_copy(
                        ki_blk[:], a_blk[pr][:]))(), waits=[("vm", vts)], inc="vm")
                    pm2 = rec.op("v", (lambda pr=pr: lambda v: v.tensor_tensor(
                        out=a_blk[pr][:], in0=a_blk[pr][:], in1=ki_blk[:],
                        op=ALU.subtract))(), waits=[("vm", pm1)], inc="vm")
                    rstt_of[(m, b)] = pm2
                    wts = [("vm", pm2)]
                    if (g - 2) >= 0 and ((g - 2) // NBLK, (g - 2) % NBLK) in sttl_of:
                        wts.append(("vm", sttl_of[((g - 2) // NBLK, (g - 2) % NBLK)]))
                    sin_of[(m, b)] = rec.op("sc", (lambda pr=pr: lambda sc: sc.activation(
                        out=s_bf[pr][:], in_=a_blk[pr][:], func=AF.Sin,
                        scale=TWO_PI))(), waits=wts, inc="sm")
                    for fo in range(4):
                        ff = 4 * b + fo
                        wts = [("sm", sin_of[(m, b)])]
                        if ff >= 2:
                            wts.append(("sm", relu_vals[(m, ff - 2)]))
                        elif m >= 1:
                            wts.append(("sm", relu_vals[(m - 1, 30 + ff)]))
                        stv_ = rec.op("v", (lambda pr=pr, fo=fo, ff=ff, b=b: lambda v:
                                            v.scalar_tensor_tensor(
                            out=h2q[ff % 2][:], in0=s_bf[pr][:, 256 * fo:256 * fo + 256],
                            scalar=qcc[:, ff:ff + 1],
                            in1=ps_hid[b % 2][:, 256 * fo:256 * fo + 256],
                            op0=ALU.mult, op1=ALU.add))(), waits=wts, inc="vm")
                        if fo == 3:
                            sttl_of[(m, b)] = stv_
                        w2_ = [("vm", stv_)]
                        if ff == 0 and m >= 2:
                            w2_.append(("tm", tmo_of[m - 2]))
                        relu_vals[(m, ff)] = rec.op("sc", (lambda pr2=ff % 2, ff=ff,
                                                           asel=asel: lambda sc:
                                                           sc.activation(
                            out=act_sb[:, 8192 * asel + 256 * ff:
                                     8192 * asel + 256 * ff + 256],
                            in_=h2q[pr2][:], func=AF.Relu,
                            bias=b1s[:, ff:ff + 1]))(), waits=w2_, inc="sm")
                    if b + 2 < NBLK:
                        tmf_of[b + 2] = emit_blk(b + 2)
                tmf_last2 = tmf_last_prev
                tmf_last_prev = tmf_of[NBLK - 1]
                if DBG_SCROUT:
                    rec.op("g", (lambda m=m, asel=asel: lambda g: g.dma_start(
                        out=dbg_act[m], in_=act_sb[:, 8192 * asel:8192 * asel + 8192]))(),
                        waits=[("sm", relu_vals[(m, 31)])], inc="gw")

                def emit_ffn2(mm2):
                    asel2 = mm2 % 2
                    def mk_f2(asel2=asel2):
                        def f(t):
                            r = None
                            for dmc in range(8):
                                for ff in range(32):
                                    r = t.matmul(ps_y2[:, 256 * dmc:256 * dmc + 256],
                                                 w2x[:, 1024 * ff + 128 * dmc:
                                                     1024 * ff + 128 * dmc + 128],
                                                 act_sb[:, 8192 * asel2 + 256 * ff:
                                                       8192 * asel2 + 256 * ff + 256],
                                                 start=(ff == 0), stop=(ff == 31))
                            return r
                        return f
                    tmo = rec.op("t", mk_f2(),
                                 waits=[("sm", relu_vals[(mm2, 31)]), ("gw", gw_w2),
                                        ("vm", ch4[mm2 - 1] if mm2 >= 1 else 0)],
                                 inc="tm")
                    tmo_of[mm2] = tmo
                    x1c = x1_l[mm2 % 2]
                    def mk_out(x1c=x1c):
                        def f(v):
                            r = None
                            for dmc in range(8):
                                sl = slice(256 * dmc, 256 * dmc + 256)
                                r = v.scalar_tensor_tensor(
                                    out=x1c[:, sl], in0=ps_y2[:, sl],
                                    scalar=b2s[:, dmc:dmc + 1], in1=x1c[:, sl],
                                    op0=ALU.add, op1=ALU.add)
                            return r
                        return f
                    vo = rec.op("v", mk_out(),
                                waits=[("tm", tmo), ("ld", ldx1_of[mm2])], inc="vm")
                    ch4[mm2] = vo
                    st_w[mm2] = rec.op("sy", (lambda x1c=x1c, mm2=mm2: lambda s:
                                              s.dma_start(
                        out=out3[:, :, C * mm2:C * mm2 + C],
                        in_=x1c[:].rearrange("p (c n) -> p c n", c=8)))(),
                        waits=[("vm", vo)], inc="st")

                if m >= 1:
                    emit_ffn2(m - 1)
            emit_ffn2(NCH - 1)
            stv = rec.val("st")
            rec.streams["sy"].append([lambda s: s.nop(), [("st", stv)], None, 1])
            rec.finish()
        recs.append(rec)

        return _assemble(nc, recs, sems)


def _prep_inputs(L0_attn_hidden_states, L0_attn_w_qkv, L0_attn_w_gated,
                 L0_attn_w_proj, L0_attn_decay, L0_ffn_W1, L0_ffn_b1,
                 L0_ffn_quantum_phases, L0_ffn_quantum_weights_real,
                 L0_ffn_quantum_weights_imag, L0_ffn_W2, L0_ffn_b2):
    hc = _host_consts()
    x = np.asarray(L0_attn_hidden_states, np.float32)
    wqkv = np.asarray(L0_attn_w_qkv, np.float32)
    wg = np.asarray(L0_attn_w_gated, np.float32)
    wp = np.asarray(L0_attn_w_proj, np.float32)
    decay = np.asarray(L0_attn_decay, np.float32)
    W1 = np.ascontiguousarray(np.asarray(L0_ffn_W1, np.float32))
    W2 = np.ascontiguousarray(np.asarray(L0_ffn_W2, np.float32))
    b1 = np.asarray(L0_ffn_b1, np.float32)
    b2 = np.asarray(L0_ffn_b2, np.float32)
    qp = np.asarray(L0_ffn_quantum_phases, np.float32)
    qr = np.asarray(L0_ffn_quantum_weights_real, np.float32)
    qi = np.asarray(L0_ffn_quantum_weights_imag, np.float32)

    perm = hc["perm_qk"]
    wqkT = np.ascontiguousarray(
        np.concatenate([wqkv[0:1024][perm], wqkv[1024:2048][perm]], axis=0).T)
    wvgT = np.ascontiguousarray(np.concatenate([wqkv[2048:4096], wg], 0).T)
    wpT = np.ascontiguousarray(wp.T)

    cols = lambda a, n: np.ascontiguousarray(a.reshape(n, 128).T)
    common = {
        "wqkT": wqkT, "wvgT": wvgT, "wpT": wpT, "W1": W1, "W2": W2,
        "csk_p": hc["csk0"], "snk_p": hc["snk0"],
        "delta0": hc["delta0"], "delta1": hc["delta1"], "qexp": hc["qexp"],
        "kexp": hc["kexp"], "c256": hc["c256"],
        "decay_c": np.ascontiguousarray(np.tile(decay[None, :], (128, 1))),
        "b1c": cols(b1, 32), "b2c": cols(b2, 8), "qpc": cols(qp, 32),
        "qrc": cols(qr, 32), "qic": cols(qi, 32),
        "ones_c": hc["ones_c"], "ones_r": hc["ones_r"], "ident": hc["ident"],
        "eps1": hc["eps1"], "epsc": hc["epsc"],
    }
    in_maps = []
    for c in range(NCORE):
        b, th = c // 2, c % 2
        xT = np.ascontiguousarray(x[b].T)
        m = dict(common)
        m["xT_own"] = np.ascontiguousarray(xT[:, TOW * th:TOW * (th + 1)])
        m["xT_prev"] = np.ascontiguousarray(xT[:, 0:TOW])
        m["csq"] = hc["csq1"] if th else hc["csq0"]
        m["snq"] = hc["snq1"] if th else hc["snq0"]
        m["csk_o"] = hc["csk1"] if th else hc["csk0"]
        m["snk_o"] = hc["snk1"] if th else hc["snk0"]
        m["carry"] = np.full((128, 1), float(th), np.float32)
        in_maps.append(m)
    return in_maps


def kernel(**inputs):
    in_maps = _prep_inputs(**inputs)
    nc = build_program()
    res = run_bass_kernel_spmd(nc, in_maps, core_ids=list(range(NCORE)))
    out = np.empty((B, T, D), np.float32)
    for c in range(NCORE):
        b, th = c // 2, c % 2
        out[b, TOW * th:TOW * (th + 1), :] = np.asarray(res.results[c]["outT"]).T
    return out


# version 06
# speedup vs baseline: 1.0934x; 1.0136x over previous
import math
from contextlib import ExitStack

import numpy as np
import ml_dtypes

import concourse.bass as bass
import concourse.mybir as mybir
from concourse.bass_utils import run_bass_kernel_spmd

F32 = mybir.dt.float32
BF16 = mybir.dt.bfloat16
I32 = mybir.dt.int32
AF = mybir.ActivationFunctionType
ALU = mybir.AluOpType
AX = mybir.AxisListType.X

B, T, D = 4, 8192, 1024
NH, HDQ, HDV = 8, 128, 256
TOW = T // 2
C = 256
NCH = TOW // C
SCALE = 1.0 / math.sqrt(128.0)
EPS = 1e-5
TWO_PI = 2.0 * math.pi
NCORE = 8

DBG_MAX_PHASE = 99
DBG_SCROUT = False
DBG_SERIAL = 0

_HOST_CACHE = {}


def _host_consts():
    if _HOST_CACHE:
        return _HOST_CACHE
    p = np.arange(128)
    i_loc = 16 * (p // 32) + (p % 16)
    is_e = (p % 32) < 16
    perm = np.zeros(1024, dtype=np.int64)
    for h in range(NH):
        perm[128 * h: 128 * h + 128] = 2 * (64 * h + i_loc) + (~is_e).astype(np.int64)
    _HOST_CACHE["perm_qk"] = perm
    sgn = np.where(is_e, -1.0, 1.0)

    def tables(t0, n_tok, kind):
        t = np.arange(t0, t0 + n_tok, dtype=np.float64)
        nch = n_tok // C
        cs = np.empty((128, nch, NH, C), dtype=np.float64)
        sn = np.empty((128, nch, NH, C), dtype=np.float64)
        for h in range(NH):
            i = (64 * h + i_loc).astype(np.float64)
            sv = (2.0 * i + 0.4 * D) / (1.4 * D)
            inv = 10000.0 ** (-i / 512.0)
            a = t[None, :] * inv[:, None]
            s = sv[:, None] ** (t[None, :] / 512.0)
            s = s * SCALE if kind == "q" else 1.0 / s
            cs[:, :, h, :] = (np.cos(a) * s).reshape(128, nch, C)
            sn[:, :, h, :] = (np.sin(a) * s * sgn[:, None]).reshape(128, nch, C)
        return (np.ascontiguousarray(cs.reshape(128, -1)).astype(ml_dtypes.bfloat16),
                np.ascontiguousarray(sn.reshape(128, -1)).astype(ml_dtypes.bfloat16))

    _HOST_CACHE["csq0"], _HOST_CACHE["snq0"] = tables(0, TOW, "q")
    _HOST_CACHE["csq1"], _HOST_CACHE["snq1"] = tables(TOW, TOW, "q")
    _HOST_CACHE["csk0"], _HOST_CACHE["snk0"] = tables(0, TOW, "k")
    _HOST_CACHE["csk1"], _HOST_CACHE["snk1"] = tables(TOW, TOW, "k")

    j = np.arange(C)
    for tile in range(2):
        dlt = j[None, :] - (p[:, None] + 128 * tile)
        _HOST_CACHE[f"delta{tile}"] = np.where(dlt >= 0, dlt, 1.0e9).astype(np.float32)
    _HOST_CACHE["qexp"] = np.ascontiguousarray(
        np.broadcast_to(j[None, :].astype(np.float32), (128, C)))
    _HOST_CACHE["kexp"] = np.stack([256.0 - p, 128.0 - p], axis=1).astype(np.float32)
    _HOST_CACHE["c256"] = np.full((128, 1), 256.0, np.float32)
    _HOST_CACHE["ones_c"] = np.ones((128, 1), np.float32)
    _HOST_CACHE["ones_r"] = np.ones((1, 128), np.float32)
    _HOST_CACHE["ident"] = np.eye(128).astype(ml_dtypes.bfloat16)
    _HOST_CACHE["eps1"] = np.full((1, 1), EPS, np.float32)
    _HOST_CACHE["epsc"] = np.full((128, 1), EPS, np.float32)
    return _HOST_CACHE


ENGS = ("t", "v", "sc", "sy", "g")


class Rec:
    def __init__(self, counts):
        self.streams = {e: [] for e in ENGS}
        self.counts = counts

    def op(self, eng, fn, waits=(), inc=None, amt=None):
        w = [(s, v) for (s, v) in waits if v and v > 0]
        ser = getattr(self, "serial", False)
        if ser is True or (isinstance(ser, (set, frozenset)) and eng in ser):
            w = [(sname, val) for sname, val in self.counts.items() if val > 0]
        if amt is None:
            amt = 16 if eng in ("sy", "g") else 1
        self.streams[eng].append([fn, w, inc, amt])
        if inc is not None:
            self.counts[inc] = self.counts.get(inc, 0) + amt
            return self.counts[inc]
        return None

    def val(self, sem):
        return self.counts.get(sem, 0)

    def finish(self):
        bar = []
        for e in ENGS:
            st = self.streams[e]
            if not st:
                continue
            fn, w, inc, amt = st[-1]
            if inc is None:
                st[-1] = [fn, w, "bar", amt]
                self.counts["bar"] = self.counts.get("bar", 0) + amt
                bar.append(("bar", self.counts["bar"]))
            else:
                bar.append((inc, self.counts[inc]))
        return bar


def _replay(rec, block, sems):
    def run(stream):
        def body(eng):
            for fn, waits, inc, amt in stream:
                for s, v in waits:
                    eng.wait_ge(sems[s], v)
                inst = fn(eng)
                if inc is not None:
                    inst.then_inc(sems[inc], amt)
        return body
    if rec.streams["t"]:
        block.tensor(run(rec.streams["t"]))
    if rec.streams["v"]:
        block.vector(run(rec.streams["v"]))
    if rec.streams["sc"]:
        block.scalar(run(rec.streams["sc"]))
    if rec.streams["sy"]:
        block.sync(run(rec.streams["sy"]))
    if rec.streams["g"]:
        block.gpsimd(run(rec.streams["g"]))


def _assemble(nc, recs, sems):
    for i, rec in enumerate(recs):
        with nc.Block(f"ph{i}") as blk:
            _replay(rec, blk, sems)
    return nc


def build_program():
    nc = bass.Bass("TRN2", target_bir_lowering=False)
    scr_kind = "ExternalOutput" if DBG_SCROUT else "Internal"
    dI = lambda n, sh, dt: nc.dram_tensor(n, sh, dt, kind="ExternalInput").ap()
    dS = lambda n, sh, dt: nc.dram_tensor(n, sh, dt, kind=scr_kind).ap()

    xT_own = dI("xT_own", [D, TOW], F32)
    xT_prev = dI("xT_prev", [D, TOW], F32)
    wqkT = dI("wqkT", [D, 2048], F32)
    wvgT = dI("wvgT", [D, 4096], F32)
    wpT = dI("wpT", [2048, 1024], F32)
    W1 = dI("W1", [1024, 4096], F32)
    W2 = dI("W2", [4096, 1024], F32)
    csq = dI("csq", [128, NCH * NH * C], BF16)
    snq = dI("snq", [128, NCH * NH * C], BF16)
    csk_o = dI("csk_o", [128, NCH * NH * C], BF16)
    snk_o = dI("snk_o", [128, NCH * NH * C], BF16)
    csk_p = dI("csk_p", [128, NCH * NH * C], BF16)
    snk_p = dI("snk_p", [128, NCH * NH * C], BF16)
    delta0 = dI("delta0", [128, C], F32)
    delta1 = dI("delta1", [128, C], F32)
    qexp = dI("qexp", [128, C], F32)
    kexp = dI("kexp", [128, 2], F32)
    c256 = dI("c256", [128, 1], F32)
    decay_c = dI("decay_c", [128, NH], F32)
    b1c = dI("b1c", [128, 32], F32)
    b2c = dI("b2c", [128, 8], F32)
    qpc = dI("qpc", [128, 32], F32)
    qrc = dI("qrc", [128, 32], F32)
    qic = dI("qic", [128, 32], F32)
    ones_c_i = dI("ones_c", [128, 1], F32)
    ones_r_i = dI("ones_r", [1, 128], F32)
    ident_i = dI("ident", [128, 128], BF16)
    eps1_i = dI("eps1", [1, 1], F32)
    epsc_i = dI("epsc", [128, 1], F32)
    carry_i = dI("carry", [128, 1], F32)
    outT = nc.dram_tensor("outT", [D, TOW], F32, kind="ExternalOutput").ap()

    ab_scr = dS("ab_scr", [2 * NCH, 1, 512], F32)
    h_scr = dS("h_scr", [2 * NCH, 128, 2048], BF16)
    q_scr = dS("q_scr", [NCH, 128, 2048], BF16)
    kf_scr = dS("kf_scr", [NCH, 128, 2048], BF16)
    kw_scr = dS("kw_scr", [2 * NCH, 2, 128, 1024], BF16)
    v_scr = dS("v_scr", [2 * NCH, 2, 128, 2048], BF16)
    g_scr = dS("g_scr", [NCH, 2, 128, 2048], BF16)
    x1_scr = dS("x1_scr", [NCH, 128, 2048], F32)
    dbg_st = dS("dbg_st", [NCH, 128, 64], F32)
    dbg_osq = dS("dbg_osq", [NCH, 2, 128, 2048], F32)
    dbg_a = dS("dbg_a", [NCH, 2, 128, 64], F32)
    dbg_act = dS("dbg_act", [NCH, 128, 8192], BF16)
    dbg_qc = dS("dbg_qc", [128, 64], F32)
    dbg_b = dS("dbg_b", [NCH, 2, 128, 64], F32)
    h2_scr = dS("h2_scr", [NCH, 128, 2048], BF16)

    xo3 = xT_own.rearrange("(c p) n -> p c n", p=128)
    xp3 = xT_prev.rearrange("(c p) n -> p c n", p=128)
    out3 = outT.rearrange("(c p) n -> p c n", p=128)

    counts = {}
    recs = []
    with ExitStack() as stack:
        e = stack.enter_context
        sems = {n: e(nc.semaphore(n)) for n in
                ["ld", "gw", "tm", "vm", "sm", "st", "bar", "pm"]}

        arena = e(nc.sbuf_tensor("arena", [128, 32768], BF16))
        px = [e(nc.sbuf_tensor(f"px{i}", [128, 2048], F32)) for i in range(2)]
        ones_c = e(nc.sbuf_tensor("ones_c_s", [128, 1], F32))
        ones_r = e(nc.sbuf_tensor("ones_r_s", [1, 128], F32))
        ident = e(nc.sbuf_tensor("ident_s", [128, 128], BF16))
        eps1 = e(nc.sbuf_tensor("eps1_s", [1, 1], F32))
        epsc = e(nc.sbuf_tensor("epsc_s", [128, 1], F32))
        carry = e(nc.sbuf_tensor("carry_s", [128, 1], F32))
        dcol = e(nc.sbuf_tensor("dcol", [128, NH], F32))
        lnd = e(nc.sbuf_tensor("lnd", [128, NH], F32))
        kfac = e(nc.sbuf_tensor("kfac", [128, 16], F32))
        kx_s = e(nc.sbuf_tensor("kx_s", [128, 2], F32))
        c256_s = e(nc.sbuf_tensor("c256_s", [128, 1], F32))

        rec = Rec(counts)
        for sb, di in [(ones_c, ones_c_i), (ones_r, ones_r_i), (ident, ident_i),
                       (eps1, eps1_i), (epsc, epsc_i), (carry, carry_i),
                       (dcol, decay_c), (kx_s, kexp), (c256_s, c256)]:
            rec.op("sy", (lambda sb=sb, di=di: lambda s: s.dma_start(out=sb[:], in_=di))(),
                   inc="ld")
        ldv = rec.val("ld")
        smln = rec.op("sc", lambda sc: sc.activation(out=lnd[:], in_=dcol[:],
                                                     func=AF.Ln),
                      waits=[("ld", ldv)], inc="sm")
        for h in range(NH):
            for t in range(2):
                rec.op("sc", (lambda h=h, t=t: lambda sc: sc.activation(
                    out=kfac[:, 2 * h + t: 2 * h + t + 1], in_=kx_s[:, t:t + 1],
                    func=AF.Exp, scale=lnd[:, h:h + 1]))(),
                    waits=[("sm", smln)], inc="sm")
        rec.op("g", lambda g: g.dma_start(
            out=arena[:, 0:16384].rearrange("p (c n) -> p c n", c=8),
            in_=wqkT.rearrange("(c p) n -> p c n", p=128)), inc="gw")
        gw_wqk = rec.val("gw")
        barrier = rec.finish()
        recs.append(rec)

        ab_gw = [0] * (2 * NCH)
        with ExitStack() as s01:
            pe = s01.enter_context
            ps_row = [pe(nc.psum_tensor(f"ps_row{i}", [1, 512], F32)) for i in range(2)]
            rows05 = pe(nc.sbuf_tensor("rows05", [1, 2560], F32))
            xsq2 = [pe(nc.sbuf_tensor(f"xsq{i}", [128, 2048], F32)) for i in range(2)]

            rec = Rec(counts)
            ch_done = [0] * (2 * NCH)
            row_mm = [0] * (2 * NCH)
            row_ev = [0] * (2 * NCH)
            for m in range(2 * NCH):
                xb = px[m % 2]
                xq = xsq2[m % 2]
                pr = ps_row[m % 2]
                r0 = 1280 * (m % 2)
                src = xp3 if m < NCH else xo3
                off = (m % NCH) * C
                w = ([("vm", ch_done[m - 2])] if m >= 2 else list(barrier))
                ldx = rec.op("sy", (lambda xb=xb, src=src, off=off: lambda s: s.dma_start(
                    out=xb[:].rearrange("p (c n) -> p c n", c=8),
                    in_=src[:, :, off:off + C]))(), waits=w, inc="ld")
                smv = rec.op("sc", (lambda xb=xb, xq=xq: lambda sc: sc.activation(
                    out=xq[:], in_=xb[:], func=AF.Square))(),
                    waits=[("ld", ldx), ("tm", row_mm[m - 2] if m >= 2 else 0)],
                    inc="sm")

                def mk_rows(xb=xb, xq=xq, pr=pr):
                    def f(t):
                        r = None
                        for cc in range(8):
                            r = t.matmul(pr[0:1, 0:256], ones_c[:],
                                         xb[:, 256 * cc:256 * cc + 256],
                                         start=(cc == 0), stop=(cc == 7))
                        for cc in range(8):
                            r = t.matmul(pr[0:1, 256:512], ones_c[:],
                                         xq[:, 256 * cc:256 * cc + 256],
                                         start=(cc == 0), stop=(cc == 7))
                        return r
                    return f
                row_mm[m] = tmv = rec.op("t", mk_rows(),
                                         waits=[("sm", smv),
                                                ("vm", row_ev[m - 2] if m >= 2 else 0)],
                                         inc="tm")
                ee = rec.op("v", (lambda pr=pr, r0=r0: lambda v: v.tensor_copy(
                    rows05[0:1, r0:r0 + 512], pr[0:1, 0:512]))(),
                    waits=[("tm", tmv), ("gw", ab_gw[m - 2] if m >= 2 else 0)], inc="vm")
                row_ev[m] = ee
                ee = rec.op("v", (lambda r0=r0: lambda v: v.tensor_scalar(
                    out=rows05[0:1, r0 + 512:r0 + 768], in0=rows05[0:1, r0:r0 + 256],
                    scalar1=1.0 / D, scalar2=None, op0=ALU.mult))(),
                    waits=[("vm", ee)], inc="vm")
                ee = rec.op("v", (lambda r0=r0: lambda v: v.tensor_tensor(
                    out=rows05[0:1, r0 + 768:r0 + 1024], in0=rows05[0:1, r0 + 512:r0 + 768],
                    in1=rows05[0:1, r0 + 512:r0 + 768], op=ALU.mult))(),
                    waits=[("vm", ee)], inc="vm")
                vmv = rec.op("v", (lambda r0=r0: lambda v: v.scalar_tensor_tensor(
                    out=rows05[0:1, r0 + 1024:r0 + 1280], in0=rows05[0:1, r0 + 256:r0 + 512],
                    scalar=1.0 / D, in1=rows05[0:1, r0 + 768:r0 + 1024],
                    op0=ALU.mult, op1=ALU.subtract))(), waits=[("vm", ee)], inc="vm")
                smln = rec.op("sc", (lambda r0=r0: lambda sc: sc.activation(
                    out=rows05[0:1, r0 + 1024:r0 + 1280], in_=rows05[0:1, r0 + 1024:r0 + 1280],
                    func=AF.Ln, bias=eps1[0:1, 0:1]))(), waits=[("vm", vmv)], inc="sm")
                smln = rec.op("sc", (lambda r0=r0: lambda sc: sc.activation(
                    out=rows05[0:1, r0:r0 + 256], in_=rows05[0:1, r0 + 1024:r0 + 1280],
                    func=AF.Exp, scale=-0.5))(), waits=[("sm", smln)], inc="sm")
                vmv = rec.op("v", (lambda r0=r0: lambda v: v.scalar_tensor_tensor(
                    out=rows05[0:1, r0 + 256:r0 + 512], in0=rows05[0:1, r0 + 512:r0 + 768],
                    scalar=-1.0, in1=rows05[0:1, r0:r0 + 256],
                    op0=ALU.mult, op1=ALU.mult))(), waits=[("sm", smln)], inc="vm")
                ch_done[m] = vmv
                ab_gw[m] = rec.op("g", (lambda m=m, r0=r0: lambda g: g.dma_start(
                    out=ab_scr[m], in_=rows05[0:1, r0:r0 + 512]))(),
                    waits=[("vm", vmv)], inc="gw")
            barrier = rec.finish()
            recs.append(rec)
            if DBG_MAX_PHASE < 2:
                return _assemble(nc, recs, sems)

            rec = Rec(counts)
            with ExitStack() as p1s:
                pe1 = p1s.enter_context
                h_t = [pe1(nc.sbuf_tensor(f"p1a_h{i}", [128, 2048], BF16)) for i in range(2)]
                cs_t = [pe1(nc.sbuf_tensor(f"p1a_cs{i}", [128, 2048], BF16)) for i in range(2)]
                sn_t = [pe1(nc.sbuf_tensor(f"p1a_sn{i}", [128, 2048], BF16)) for i in range(2)]
                csk_t = [pe1(nc.sbuf_tensor(f"p1a_ck{i}", [128, 2048], BF16)) for i in range(2)]
                snk_t = [pe1(nc.sbuf_tensor(f"p1a_sk{i}", [128, 2048], BF16)) for i in range(2)]
                ab_t = [pe1(nc.sbuf_tensor(f"p1a_ab{i}", [1, 512], F32)) for i in range(2)]
                t1f = pe1(nc.sbuf_tensor("p1a_t1", [128, 1024], F32))
                t2f = pe1(nc.sbuf_tensor("p1a_t2", [128, 1024], F32))
                q_sb = pe1(nc.sbuf_tensor("p1a_q", [128, 2048], BF16))
                k_sb = pe1(nc.sbuf_tensor("p1a_k", [128, 2048], BF16))
                kw_t = pe1(nc.sbuf_tensor("p1a_kw", [128, 2048], BF16))
                ps_ab = pe1(nc.psum_tensor("p1a_psab", [128, 512], F32))
                ps_mm = [pe1(nc.psum_tensor(f"p1a_mm{i}", [128, 1024], F32))
                         for i in range(2)]
                ps_tr = pe1(nc.psum_tensor("p1a_tr", [128, 1024], BF16))

                SWAP = list(range(16, 32)) + list(range(16))
                ch1 = [0] * (2 * NCH)
                pc_evac = {}
                trw = {}
                hw_gw = [0] * (2 * NCH)
                qw_gw = [0] * (2 * NCH)
                kfw_gw = [0] * (2 * NCH)
                kww_gw = [0] * (2 * NCH)
                for m in range(2 * NCH):
                    own = m >= NCH
                    mm = m % NCH
                    xb, hb = px[m % 2], h_t[m % 2]
                    csb, snb = cs_t[m % 2], sn_t[m % 2]
                    cskb, snkb = csk_t[m % 2], snk_t[m % 2]
                    abb = ab_t[m % 2]
                    src = xo3 if own else xp3
                    off = mm * C
                    wfree = ([("vm", ch1[m - 2])] if m >= 2 else list(barrier))
                    rec.op("sy", (lambda xb=xb, src=src, off=off: lambda s: s.dma_start(
                        out=xb[:].rearrange("p (c n) -> p c n", c=8),
                        in_=src[:, :, off:off + C]))(), waits=wfree, inc="ld")
                    if own:
                        rec.op("sy", (lambda csb=csb, mm=mm: lambda s: s.dma_start(
                            out=csb[:], in_=csq[:, 2048 * mm:2048 * (mm + 1)]))(),
                            inc="ld")
                        rec.op("sy", (lambda snb=snb, mm=mm: lambda s: s.dma_start(
                            out=snb[:], in_=snq[:, 2048 * mm:2048 * (mm + 1)]))(),
                            inc="ld")
                    ck = csk_o if own else csk_p
                    sk = snk_o if own else snk_p
                    rec.op("sy", (lambda cskb=cskb, ck=ck, mm=mm: lambda s: s.dma_start(
                        out=cskb[:], in_=ck[:, 2048 * mm:2048 * (mm + 1)]))(), inc="ld")
                    rec.op("sy", (lambda snkb=snkb, sk=sk, mm=mm: lambda s: s.dma_start(
                        out=snkb[:], in_=sk[:, 2048 * mm:2048 * (mm + 1)]))(), inc="ld")
                    ldt = rec.op("sy", (lambda abb=abb, m=m: lambda s: s.dma_start(
                        out=abb[:], in_=ab_scr[m]))(), waits=[("gw", ab_gw[m])], inc="ld")

                    tmv = rec.op("t", (lambda abb=abb: lambda t: (
                        t.matmul(ps_ab[:, 0:256], ones_r[:], abb[0:1, 0:256],
                                 start=True, stop=True),
                        t.matmul(ps_ab[:, 256:512], ones_r[:], abb[0:1, 256:512],
                                 start=True, stop=True))[-1])(),
                        waits=[("ld", ldt), ("vm", ch1[m - 1] if m >= 1 else 0)],
                        inc="tm")

                    def mk_applyA(xb=xb, hb=hb):
                        def f(v):
                            r = None
                            for cc in range(8):
                                sl = slice(256 * cc, 256 * cc + 256)
                                dst = (t1f if cc < 4 else t2f)[:, 256 * (cc % 4):
                                                               256 * (cc % 4) + 256]
                                r = v.tensor_tensor(out=dst, in0=xb[:, sl],
                                                    in1=ps_ab[:, 0:256], op=ALU.mult)
                            return r
                        return f
                    def mk_applyB(hb=hb):
                        def f(v):
                            r = None
                            for cc in range(8):
                                sl = slice(256 * cc, 256 * cc + 256)
                                src = (t1f if cc < 4 else t2f)[:, 256 * (cc % 4):
                                                               256 * (cc % 4) + 256]
                                r = v.tensor_tensor(out=hb[:, sl], in0=src,
                                                    in1=ps_ab[:, 256:512], op=ALU.add)
                            return r
                        return f
                    va_ = rec.op("v", mk_applyA(),
                                 waits=[("tm", tmv),
                                        ("gw", hw_gw[m - 2] if m >= 2 else 0)],
                                 inc="vm")
                    vh = rec.op("v", mk_applyB(), waits=[("vm", va_)], inc="vm")
                    hw_gw[m] = rec.op("g", (lambda hb=hb, m=m: lambda g: g.dma_start(
                        out=h_scr[m], in_=hb[:]))(), waits=[("vm", vh)], inc="gw")

                    pieces = ([(0, "q"), (1, "q"), (2, "k"), (3, "k")] if own
                              else [(2, "k"), (3, "k")])
                    npieces = len(pieces)
                    for pi, (pc, kind) in enumerate(pieces):
                        dst = ps_mm[pi % 2]
                        wts = [("vm", vh), ("gw", gw_wqk)]
                        if pi >= 2:
                            wts.append(("vm", pc_evac[(m, pi - 2)]))
                        elif m >= 1:
                            prev_np = 4 if (m - 1) >= NCH else 2
                            cand = [q for q in range(prev_np) if q % 2 == pi % 2]
                            wts.append(("vm", pc_evac[(m - 1, max(cand))]))
                        def mk_piece(dst=dst, pc=pc, hb=hb):
                            def f(t):
                                r = None
                                for oi in range(4):
                                    oc = 4 * pc + oi
                                    for cc in range(8):
                                        r = t.matmul(
                                            dst[:, 256 * oi:256 * oi + 256],
                                            arena[:, 2048 * cc + 128 * oc:
                                                  2048 * cc + 128 * oc + 128],
                                            hb[:, 256 * cc:256 * cc + 256],
                                            start=(cc == 0), stop=(cc == 7))
                                return r
                            return f
                        tmp = rec.op("t", mk_piece(), waits=wts, inc="tm")
                        half = pc % 2
                        if kind == "q":
                            csx, snx, dstq = csb, snb, q_sb
                            wgw = qw_gw[m - 1] if m >= 1 else 0
                        else:
                            csx, snx, dstq = cskb, snkb, k_sb
                            wgw = max(kfw_gw[m - 1] if m >= 1 else 0,
                                      kww_gw[m - 1] if m >= 1 else 0)
                        sl = slice(1024 * half, 1024 * half + 1024)
                        def mk_xpos(dst=dst, csx=csx, snx=snx, dstq=dstq, sl=sl):
                            def f(v):
                                v.tensor_tensor(out=t1f[:], in0=dst[:], in1=csx[:, sl],
                                                op=ALU.mult)
                                v.stream_shuffle(out=t2f[:], in_=dst[:], mask=SWAP)
                                v.tensor_tensor(out=t2f[:], in0=t2f[:], in1=snx[:, sl],
                                                op=ALU.mult)
                                return v.tensor_tensor(out=dstq[:, sl], in0=t1f[:],
                                                       in1=t2f[:], op=ALU.add)
                            return f
                        pc_evac[(m, pi)] = rec.op("v", mk_xpos(),
                                                  waits=[("tm", tmp), ("ld", ldt),
                                                         ("gw", wgw)], inc="vm")
                    kdone = pc_evac[(m, npieces - 1)]

                    for ts in range(2):
                        prev = trw.get((m, ts - 1), trw.get((m - 1, 1), 0))
                        def mk_tr(ts=ts):
                            def f(t):
                                r = None
                                for h in range(NH):
                                    r = t.transpose(
                                        ps_tr[:, 128 * h:128 * h + 128],
                                        k_sb[:, 256 * h + 128 * ts:
                                             256 * h + 128 * ts + 128], ident[:])
                                return r
                            return f
                        tmv2 = rec.op("t", mk_tr(), waits=[("vm", kdone), ("vm", prev)],
                                      inc="tm")
                        def mk_trev(ts=ts):
                            def f(v):
                                r = None
                                for h in range(NH):
                                    r = v.tensor_scalar(
                                        out=kw_t[:, 1024 * ts + 128 * h:
                                                 1024 * ts + 128 * h + 128],
                                        in0=ps_tr[:, 128 * h:128 * h + 128],
                                        scalar1=kfac[:, 2 * h + ts:2 * h + ts + 1],
                                        scalar2=None, op0=ALU.mult)
                                return r
                            return f
                        w2_ = [("tm", tmv2)]
                        if ts == 0:
                            w2_.append(("gw", kww_gw[m - 1] if m >= 1 else 0))
                        trw[(m, ts)] = rec.op("v", mk_trev(), waits=w2_, inc="vm")

                    if own:
                        qw_gw[m] = rec.op("g", (lambda mm=mm: lambda g: g.dma_start(
                            out=q_scr[mm], in_=q_sb[:]))(),
                            waits=[("vm", pc_evac[(m, 1)])], inc="gw")
                        kfw_gw[m] = rec.op("g", (lambda mm=mm: lambda g: g.dma_start(
                            out=kf_scr[mm], in_=k_sb[:]))(), waits=[("vm", kdone)],
                            inc="gw")
                    kww_gw[m] = rec.op("g", (lambda m=m: lambda g: g.dma_start(
                        out=kw_scr[m].rearrange("a p n -> p a n"),
                        in_=kw_t[:].rearrange("p (a n) -> p a n", a=2)))(),
                        waits=[("vm", trw[(m, 1)])], inc="gw")
                    ch1[m] = trw[(m, 1)]
                barrier = rec.finish()
            recs.append(rec)
            if DBG_MAX_PHASE < 3:
                return _assemble(nc, recs, sems)

            rec = Rec(counts)
            with ExitStack() as p1bs:
                pe2 = p1bs.enter_context
                h_t2 = [pe2(nc.sbuf_tensor(f"p1b_h{i}", [128, 2048], BF16))
                        for i in range(2)]
                v_t2 = pe2(nc.sbuf_tensor("p1b_v", [128, 4096], BF16))
                g_t2 = pe2(nc.sbuf_tensor("p1b_g", [128, 4096], BF16))
                ps_mm2 = [pe2(nc.psum_tensor(f"p1b_mm{i}", [128, 1024], F32))
                          for i in range(3)]

                rec.op("g", lambda g: g.dma_start(
                    out=arena[:].rearrange("p (c n) -> p c n", c=8),
                    in_=wvgT.rearrange("(c p) n -> p c n", p=128)),
                    waits=barrier, inc="gw")
                gw_vg = rec.val("gw")

                ch2 = [0] * (2 * NCH)
                vw_gw = [0] * (2 * NCH)
                gw_gw = [0] * (2 * NCH)
                piece_ev = {}
                pln = {}
                for m in range(2 * NCH):
                    own = m >= NCH
                    mm = m % NCH
                    hb = h_t2[m % 2]
                    wfree = ([("vm", ch2[m - 2])] if m >= 2 else list(barrier))
                    ldh = rec.op("sy", (lambda hb=hb, m=m: lambda s: s.dma_start(
                        out=hb[:], in_=h_scr[m]))(),
                        waits=wfree + [("gw", gw_vg), ("gw", hw_gw[m])], inc="ld")
                    pl = [(ts, hf, "v") for ts in range(2) for hf in range(2)]
                    if own:
                        pl += [(ts, hf, "g") for ts in range(2) for hf in range(2)]
                    pln[m] = len(pl)
                    for pi, (ts, hf, kind) in enumerate(pl):
                        dst = ps_mm2[pi % 3]
                        off = (0 if kind == "v" else 2048) + 1024 * hf
                        wts = [("ld", ldh)]
                        if pi >= 3:
                            wts.append(piece_ev[(m, pi - 3)])
                        elif m >= 1:
                            cand = [q for q in range(pln[m - 1]) if q % 3 == pi % 3]
                            wts.append(piece_ev[(m - 1, max(cand))])
                        def mk_vg(dst=dst, off=off, hb=hb, ts=ts):
                            def f(t):
                                r = None
                                for cc in range(8):
                                    for qn in range(2):
                                        r = t.matmul(
                                            dst[:, 512 * qn:512 * qn + 512],
                                            hb[:, 256 * cc + 128 * ts:
                                               256 * cc + 128 * ts + 128],
                                            arena[:, 4096 * cc + off + 512 * qn:
                                                  4096 * cc + off + 512 * qn + 512],
                                            start=(cc == 0), stop=(cc == 7))
                                return r
                            return f
                        tmv = rec.op("t", mk_vg(), waits=wts, inc="tm")
                        dsl = slice(2048 * ts + 1024 * hf, 2048 * ts + 1024 * hf + 1024)
                        if kind == "v":
                            ev = rec.op("v", (lambda dst=dst, dsl=dsl: lambda v:
                                              v.tensor_copy(v_t2[:, dsl], dst[:]))(),
                                        waits=[("tm", tmv),
                                               ("gw", vw_gw[m - 1] if m >= 1 else 0)],
                                        inc="vm")
                            piece_ev[(m, pi)] = ("vm", ev)
                        else:
                            ev = rec.op("sc", (lambda dst=dst, dsl=dsl: lambda sc:
                                               sc.activation(out=g_t2[:, dsl],
                                                             in_=dst[:], func=AF.Silu))(),
                                        waits=[("tm", tmv),
                                               ("gw", gw_gw[m - 1] if m >= 1 else 0)],
                                        inc="sm")
                            piece_ev[(m, pi)] = ("sm", ev)
                    vw_gw[m] = rec.op("g", (lambda m=m: lambda g: g.dma_start(
                        out=v_scr[m].rearrange("a p n -> p a n"),
                        in_=v_t2[:].rearrange("p (a n) -> p a n", a=2)))(),
                        waits=[piece_ev[(m, 3)]], inc="gw")
                    if own:
                        gw_gw[m] = rec.op("g", (lambda mm=mm: lambda g: g.dma_start(
                            out=g_scr[mm].rearrange("a p n -> p a n"),
                            in_=g_t2[:].rearrange("p (a n) -> p a n", a=2)))(),
                            waits=[piece_ev[(m, 7)]], inc="gw")
                    lsem, lval = piece_ev[(m, len(pl) - 1)]
                    if lsem == "vm":
                        ch2[m] = lval
                    else:
                        ch2[m] = rec.op("v", lambda v: v.tensor_copy(
                            v_t2[:, 0:2], v_t2[:, 0:2]), waits=[(lsem, lval)], inc="vm")
                barrier = rec.finish()
            recs.append(rec)
        if DBG_MAX_PHASE < 4:
            return _assemble(nc, recs, sems)

        rec = Rec(counts)
        x1w_gw = [0] * NCH
        h2w_gw = [0] * NCH
        with ExitStack() as p2s:
            pe3 = p2s.enter_context
            dmask = [pe3(nc.sbuf_tensor(f"p2_dm{i}", [128, 2048], BF16)) for i in range(2)]
            qfac = pe3(nc.sbuf_tensor("p2_qfac", [128, 2048], BF16))
            alph = pe3(nc.sbuf_tensor("p2_alpha", [128, NH], F32))
            S = pe3(nc.sbuf_tensor("p2_S", [128, 2048], F32))
            S_bf = [pe3(nc.sbuf_tensor(f"p2_Sbf{i}", [128, 2048], BF16)) for i in range(2)]
            dl_t = [pe3(nc.sbuf_tensor(f"p2_dl{i}", [128, 256], F32)) for i in range(2)]
            qx_t = pe3(nc.sbuf_tensor("p2_qx", [128, 256], F32))
            q_t = pe3(nc.sbuf_tensor("p2_q", [128, 2048], BF16))
            kf_t = pe3(nc.sbuf_tensor("p2_kf", [128, 2048], BF16))
            kw_t2 = [pe3(nc.sbuf_tensor(f"p2_kw{i}", [128, 2048], BF16)) for i in range(2)]
            v_t3 = [pe3(nc.sbuf_tensor(f"p2_v{i}", [128, 4096], BF16)) for i in range(2)]
            g_t3 = [pe3(nc.sbuf_tensor(f"p2_g{i}", [128, 4096], BF16)) for i in range(2)]
            qf_t = pe3(nc.sbuf_tensor("p2_qf", [128, 2048], BF16))
            sc_bf = pe3(nc.sbuf_tensor("p2_scbf", [128, 4096], BF16))
            stats = pe3(nc.sbuf_tensor("p2_stats", [128, 64], F32))
            nrm_t = pe3(nc.sbuf_tensor("p2_nrm", [128, 2048], BF16))
            z_t = pe3(nc.sbuf_tensor("p2_z", [128, 2048], BF16))
            zf_t = pe3(nc.sbuf_tensor("p2_zf", [128, 4096], BF16))
            h2_t = pe3(nc.sbuf_tensor("p2_h2", [128, 2048], BF16))
            ps_sc = [pe3(nc.psum_tensor(f"p2_pssc{i}", [128, 512], F32)) for i in range(2)]
            ps_o = pe3(nc.psum_tensor("p2_pso", [128, 2048], F32))
            ps_tr2 = pe3(nc.psum_tensor("p2_pstr", [128, 1024], BF16))
            ps_ub = pe3(nc.psum_tensor("p2_psub", [128, 512], F32))
            x1_t = px[0]
            scrA = pe3(nc.sbuf_tensor("scrA", [128, 2048], F32))
            rows = pe3(nc.sbuf_tensor("rows_p", [1, 1280], F32))
            o_sq = scrA

            rec.op("g", lambda g: g.dma_start(
                out=arena[:, 0:16384].rearrange("p (j n) -> p j n", j=16),
                in_=wpT.rearrange("(j p) n -> p j n", p=128)),
                waits=barrier, inc="gw")
            gw_wp = rec.val("gw")

            rec.op("sy", lambda s: s.dma_start(out=dl_t[0][:], in_=delta0),
                   waits=barrier, inc="ld")
            rec.op("sy", lambda s: s.dma_start(out=dl_t[1][:], in_=delta1), inc="ld")
            ldq = rec.op("sy", lambda s: s.dma_start(out=qx_t[:], in_=qexp), inc="ld")
            first = True
            for tile in range(2):
                for h in range(NH):
                    rec.op("sc", (lambda tile=tile, h=h: lambda sc: sc.activation(
                        out=dmask[tile][:, 256 * h:256 * h + 256], in_=dl_t[tile][:],
                        func=AF.Exp, scale=lnd[:, h:h + 1]))(),
                        waits=[("ld", ldq)] + (list(barrier) if first else []),
                        inc="sm")
                    first = False
            for h in range(NH):
                rec.op("sc", (lambda h=h: lambda sc: sc.activation(
                    out=qfac[:, 256 * h:256 * h + 256], in_=qx_t[:],
                    func=AF.Exp, scale=lnd[:, h:h + 1]))(), inc="sm")
            for h in range(NH):
                rec.op("sc", (lambda h=h: lambda sc: sc.activation(
                    out=alph[:, h:h + 1], in_=c256_s[:], func=AF.Exp,
                    scale=lnd[:, h:h + 1]))(), inc="sm")
            consts_sm = rec.val("sm")
            sbf_done = [0] * (2 * NCH)
            vS0 = rec.op("v", lambda v: v.memset(S[:], 0.0), waits=barrier, inc="vm")

            ch3 = [0] * (2 * NCH)
            prev_mask = (0, 0)
            prev_av = 0
            prev_scq = 0
            for m in range(2 * NCH):
                own = m >= NCH
                mm = m % NCH
                bsel = m % 2
                kwb, vb = kw_t2[bsel], v_t3[bsel]
                wfree = ([("vm", ch3[m - 2])] if m >= 2 else [("sm", consts_sm)])
                rec.op("sy", (lambda kwb=kwb, m=m: lambda s: s.dma_start(
                    out=kwb[:].rearrange("p (a n) -> p a n", a=2),
                    in_=kw_scr[m].rearrange("a p n -> p a n")))(),
                    waits=wfree + [("gw", kww_gw[m])], inc="ld")
                ldv_ = rec.op("sy", (lambda vb=vb, m=m: lambda s: s.dma_start(
                    out=vb[:].rearrange("p (a n) -> p a n", a=2),
                    in_=v_scr[m].rearrange("a p n -> p a n")))(),
                    waits=[("gw", vw_gw[m])], inc="ld")
                if own:
                    gb = g_t3[bsel]
                    ldq_ = rec.op("sy", (lambda mm=mm: lambda s: s.dma_start(
                        out=q_t[:], in_=q_scr[mm]))(),
                        waits=[("tm", prev_scq),
                               ("vm", rec.val("vm") if m > NCH else 0),
                               ("gw", qw_gw[NCH + mm])], inc="ld")
                    ldkf = rec.op("sy", (lambda mm=mm: lambda s: s.dma_start(
                        out=kf_t[:], in_=kf_scr[mm]))(),
                        waits=[("gw", kfw_gw[NCH + mm])], inc="ld")
                    ldg = rec.op("sy", (lambda gb=gb, mm=mm: lambda s: s.dma_start(
                        out=gb[:].rearrange("p (a n) -> p a n", a=2),
                        in_=g_scr[mm].rearrange("a p n -> p a n")))(),
                        waits=[("gw", gw_gw[NCH + mm])], inc="ld")
                    ldx_ = rec.op("sy", (lambda mm=mm: lambda s: s.dma_start(
                        out=x1_t[:].rearrange("p (c n) -> p c n", c=8),
                        in_=xo3[:, :, C * mm:C * mm + C]))(),
                        waits=[("vm", ch3[m - 1]),
                               ("gw", x1w_gw[mm - 1] if mm >= 1 else 0)], inc="ld")
                    vqf = rec.op("v", lambda v: v.tensor_tensor(
                        out=qf_t[:], in0=q_t[:], in1=qfac[:], op=ALU.mult),
                        waits=[("ld", ldq_), ("tm", rec.val("tm")),
                               ("sm", consts_sm)], inc="vm")

                mask_done = {}
                tms_last = prev_scq
                vsu = 0
                for hp in range(4):
                    wts = [("ld", ldv_)]
                    if hp >= 1:
                        wts.append(("vm", vsu))
                    else:
                        wts.append(("vm", ch3[m - 1] if m >= 1 else 0))
                        wts.append(("vm", vS0))
                    def mk_up(hp=hp, kwb=kwb, vb=vb):
                        def f(t):
                            r = None
                            for hh in (2 * hp, 2 * hp + 1):
                                for tt in range(2):
                                    r = t.matmul(
                                        ps_ub[:, 256 * (hh % 2):256 * (hh % 2) + 256],
                                        kwb[:, 1024 * tt + 128 * hh:
                                           1024 * tt + 128 * hh + 128],
                                        vb[:, 2048 * tt + 256 * hh:
                                           2048 * tt + 256 * hh + 256],
                                        start=(tt == 0), stop=(tt == 1))
                            return r
                        return f
                    tmu = rec.op("t", mk_up(), waits=wts, inc="tm")
                    def mk_sup(hp=hp, bsel=bsel):
                        def f(v):
                            sl = slice(512 * hp, 512 * hp + 512)
                            for hh in (2 * hp, 2 * hp + 1):
                                v.tensor_scalar(out=S[:, 256 * hh:256 * hh + 256],
                                                in0=S[:, 256 * hh:256 * hh + 256],
                                                scalar1=alph[:, hh:hh + 1],
                                                scalar2=None, op0=ALU.mult)
                            v.tensor_tensor(out=S[:, sl], in0=S[:, sl],
                                            in1=ps_ub[:, 0:512], op=ALU.add)
                            return v.tensor_copy(S_bf[bsel][:, sl], S[:, sl])
                        return f
                    vsu = rec.op("v", mk_sup(), waits=[("tm", tmu)], inc="vm")
                    if own:
                        for h in (2 * hp, 2 * hp + 1):
                            wts = [("ld", ldkf), ("sm", consts_sm)]
                            if h >= 2:
                                wts.append(("vm", mask_done[h - 2]))
                            else:
                                wts.append(("vm", prev_mask[h]))
                                wts.append(("tm", prev_av))
                            def mk_sc(h=h):
                                def f(t):
                                    r = None
                                    for tt in range(2):
                                        r = t.matmul(
                                            ps_sc[h % 2][:, 256 * tt:256 * tt + 256],
                                            kf_t[:, 256 * h + 128 * tt:
                                                 256 * h + 128 * tt + 128],
                                            q_t[:, 256 * h:256 * h + 256],
                                            start=True, stop=True)
                                    return r
                                return f
                            tms_last = rec.op("t", mk_sc(), waits=wts, inc="tm")
                            def mk_mask(h=h):
                                def f(v):
                                    r = None
                                    for tt in range(2):
                                        r = v.tensor_tensor(
                                            out=sc_bf[:, 512 * h + 256 * tt:
                                                      512 * h + 256 * tt + 256],
                                            in0=ps_sc[h % 2][:, 256 * tt:256 * tt + 256],
                                            in1=dmask[tt][:, 256 * h:256 * h + 256],
                                            op=ALU.mult)
                                    return r
                                return f
                            mask_done[h] = rec.op("v", mk_mask(),
                                                  waits=[("tm", tms_last)], inc="vm")
                sbf_done[m] = vsu
                if not own:
                    ch3[m] = vsu
                    if m == NCH - 1:
                        ee = rec.op("v", lambda v: v.tensor_scalar(
                            out=S[:], in0=S[:], scalar1=carry[:, 0:1], scalar2=None,
                            op0=ALU.mult), waits=[("vm", vsu)], inc="vm")
                        sbf_done[m] = rec.op("v", (lambda bsel=bsel: lambda v:
                                                   v.tensor_copy(S_bf[bsel][:],
                                                                 S[:]))(),
                                             waits=[("vm", ee)], inc="vm")
                        ch3[m] = sbf_done[m]
                    continue

                prev_mask = (mask_done[6], mask_done[7])
                apply_done = {}
                tma = 0
                for ts in range(2):
                    wts = [("vm", mask_done[NH - 1]), ("vm", vqf), ("ld", ldv_),
                           ("vm", sbf_done[m - 1])]
                    if ts == 1:
                        wts.append(("vm", apply_done[0]))
                    else:
                        wts.append(("vm", ch3[m - 1] if m >= 1 else 0))
                    def mk_av(ts=ts, vb=vb, bprev=(m - 1) % 2):
                        def f(t):
                            r = None
                            for h in range(NH):
                                sl = slice(256 * h, 256 * h + 256)
                                for tt in range(2):
                                    t.matmul(ps_o[:, sl],
                                             sc_bf[:, 512 * h + 256 * tt + 128 * ts:
                                                   512 * h + 256 * tt + 128 * ts + 128],
                                             vb[:, 2048 * tt + 256 * h:
                                                2048 * tt + 256 * h + 256],
                                             start=(tt == 0), stop=False)
                                r = t.matmul(ps_o[:, sl],
                                             qf_t[:, 256 * h + 128 * ts:
                                                  256 * h + 128 * ts + 128],
                                             S_bf[bprev][:, sl], start=False, stop=True)
                            return r
                        return f
                    tma = rec.op("t", mk_av(), waits=wts, inc="tm")
                    smq = rec.op("sc", lambda sc: sc.activation(
                        out=o_sq[:], in_=ps_o[:], func=AF.Square),
                        waits=[("tm", tma), ("vm", rec.val("vm"))], inc="sm")
                    s0 = 32 * ts
                    def mk_red(s0=s0):
                        def f(v):
                            r = None
                            for h in range(NH):
                                sl = slice(256 * h, 256 * h + 256)
                                v.tensor_reduce(out=stats[:, s0 + h:s0 + h + 1],
                                                in_=ps_o[:, sl], axis=AX, op=ALU.add)
                                r = v.tensor_reduce(out=stats[:, s0 + 8 + h:s0 + 9 + h],
                                                    in_=o_sq[:, sl], axis=AX,
                                                    op=ALU.add)
                            return r
                        return f
                    vmr = rec.op("v", mk_red(), waits=[("tm", tma), ("sm", smq)],
                                 inc="vm")
                    e1 = rec.op("v", (lambda s0=s0: lambda v: v.tensor_scalar(
                        out=stats[:, s0 + 16:s0 + 24], in0=stats[:, s0:s0 + 8],
                        scalar1=1.0 / HDV, scalar2=None, op0=ALU.mult))(),
                        waits=[("vm", vmr)], inc="vm")
                    e2 = rec.op("v", (lambda s0=s0: lambda v: v.tensor_tensor(
                        out=stats[:, s0 + 24:s0 + 32], in0=stats[:, s0 + 16:s0 + 24],
                        in1=stats[:, s0 + 16:s0 + 24], op=ALU.mult))(),
                        waits=[("vm", e1)], inc="vm")
                    vmc = rec.op("v", (lambda s0=s0: lambda v: v.scalar_tensor_tensor(
                        out=stats[:, s0 + 8:s0 + 16], in0=stats[:, s0 + 8:s0 + 16],
                        scalar=1.0 / HDV, in1=stats[:, s0 + 24:s0 + 32],
                        op0=ALU.mult, op1=ALU.subtract))(),
                        waits=[("vm", e2)], inc="vm")
                    smln = rec.op("sc", (lambda s0=s0: lambda sc: sc.activation(
                        out=stats[:, s0 + 8:s0 + 16], in_=stats[:, s0 + 8:s0 + 16],
                        func=AF.Ln, bias=epsc[:, 0:1]))(),
                        waits=[("vm", vmc)], inc="sm")
                    smc = rec.op("sc", (lambda s0=s0: lambda sc: sc.activation(
                        out=stats[:, s0 + 8:s0 + 16], in_=stats[:, s0 + 8:s0 + 16],
                        func=AF.Exp, scale=-0.5))(), waits=[("sm", smln)], inc="sm")
                    def mk_app(ts=ts, s0=s0, gb=gb):
                        def f(v):
                            for h in range(NH):
                                sl = slice(256 * h, 256 * h + 256)
                                v.tensor_scalar(
                                    out=nrm_t[:, sl], in0=ps_o[:, sl],
                                    scalar1=stats[:, s0 + 16 + h:s0 + 17 + h],
                                    scalar2=stats[:, s0 + 8 + h:s0 + 9 + h],
                                    op0=ALU.subtract, op1=ALU.mult)
                            return v.tensor_tensor(
                                out=z_t[:], in0=nrm_t[:],
                                in1=gb[:, 2048 * ts:2048 * ts + 2048], op=ALU.mult)
                        return f
                    vap = rec.op("v", mk_app(), waits=[("sm", smc), ("ld", ldg)],
                                 inc="vm")
                    apply_done[ts] = vap
                    for wv in range(2):
                        def mk_ztr(wv=wv):
                            def f(t):
                                r = None
                                for jj in range(8):
                                    j = 8 * wv + jj
                                    r = t.transpose(ps_tr2[:, 128 * jj:128 * jj + 128],
                                                    z_t[:, 128 * j:128 * j + 128],
                                                    ident[:])
                                return r
                            return f
                        tmz = rec.op("t", mk_ztr(),
                                     waits=[("vm", vap), ("vm", rec.val("vm"))],
                                     inc="tm")
                        def mk_zev(wv=wv, ts=ts):
                            def f(v):
                                r = None
                                for jj in range(8):
                                    j = 8 * wv + jj
                                    r = v.tensor_copy(
                                        zf_t[:, 256 * j + 128 * ts:
                                             256 * j + 128 * ts + 128],
                                        ps_tr2[:, 128 * jj:128 * jj + 128])
                                return r
                            return f
                        apply_done[ts] = rec.op("v", mk_zev(), waits=[("tm", tmz)],
                                                inc="vm")
                prev_av = tma
                prev_scq = tma
                zf_done = apply_done[1]

                def mk_proj():
                    def f(t):
                        r = None
                        for dmc in range(8):
                            for j in range(16):
                                r = t.matmul(ps_o[:, 256 * dmc:256 * dmc + 256],
                                             arena[:, 1024 * j + 128 * dmc:
                                                   1024 * j + 128 * dmc + 128],
                                             zf_t[:, 256 * j:256 * j + 256],
                                             start=(j == 0), stop=(j == 15))
                        return r
                    return f
                tmp_ = rec.op("t", mk_proj(), waits=[("vm", zf_done), ("gw", gw_wp)],
                              inc="tm")
                def mk_x1():
                    def f(v):
                        r = None
                        for cc in range(8):
                            sl = slice(256 * cc, 256 * cc + 256)
                            r = v.tensor_tensor(out=x1_t[:, sl], in0=x1_t[:, sl],
                                                in1=ps_o[:, sl], op=ALU.add)
                        return r
                    return f
                vx1 = rec.op("v", mk_x1(), waits=[("tm", tmp_), ("ld", ldx_)], inc="vm")
                x1w_gw[mm] = rec.op("g", (lambda mm=mm: lambda g: g.dma_start(
                    out=x1_scr[mm], in_=x1_t[:]))(), waits=[("vm", vx1)], inc="gw")
                smsq = rec.op("sc", lambda sc: sc.activation(
                    out=o_sq[:], in_=x1_t[:], func=AF.Square),
                    waits=[("vm", vx1)], inc="sm")
                def mk_rows2():
                    def f(t):
                        r = None
                        for cc in range(8):
                            r = t.matmul(ps_ub[0:1, 0:256], ones_c[:],
                                         x1_t[:, 256 * cc:256 * cc + 256],
                                         start=(cc == 0), stop=(cc == 7))
                        for cc in range(8):
                            r = t.matmul(ps_ub[0:1, 256:512], ones_c[:],
                                         o_sq[:, 256 * cc:256 * cc + 256],
                                         start=(cc == 0), stop=(cc == 7))
                        return r
                    return f
                tmr = rec.op("t", mk_rows2(),
                             waits=[("sm", smsq), ("vm", sbf_done[m])], inc="tm")
                ee = rec.op("v", lambda v: v.tensor_copy(rows[0:1, 0:512],
                                                         ps_ub[0:1, 0:512]),
                            waits=[("tm", tmr)], inc="vm")
                ee = rec.op("v", lambda v: v.tensor_scalar(
                    out=rows[0:1, 512:768], in0=rows[0:1, 0:256], scalar1=1.0 / D,
                    scalar2=None, op0=ALU.mult), waits=[("vm", ee)], inc="vm")
                ee = rec.op("v", lambda v: v.tensor_tensor(
                    out=rows[0:1, 768:1024], in0=rows[0:1, 512:768],
                    in1=rows[0:1, 512:768], op=ALU.mult), waits=[("vm", ee)], inc="vm")
                vch = rec.op("v", lambda v: v.scalar_tensor_tensor(
                    out=rows[0:1, 1024:1280], in0=rows[0:1, 256:512], scalar=1.0 / D,
                    in1=rows[0:1, 768:1024], op0=ALU.mult, op1=ALU.subtract),
                    waits=[("vm", ee)], inc="vm")
                smln = rec.op("sc", lambda sc: sc.activation(
                    out=rows[0:1, 1024:1280], in_=rows[0:1, 1024:1280], func=AF.Ln,
                    bias=eps1[0:1, 0:1]), waits=[("vm", vch)], inc="sm")
                sch = rec.op("sc", lambda sc: sc.activation(
                    out=rows[0:1, 0:256], in_=rows[0:1, 1024:1280], func=AF.Exp,
                    scale=-0.5), waits=[("sm", smln)], inc="sm")
                vbr = rec.op("v", lambda v: v.scalar_tensor_tensor(
                    out=rows[0:1, 256:512], in0=rows[0:1, 512:768], scalar=-1.0,
                    in1=rows[0:1, 0:256], op0=ALU.mult, op1=ALU.mult),
                    waits=[("sm", sch)], inc="vm")
                tmb = rec.op("t", lambda t: (
                    t.matmul(ps_ub[:, 0:256], ones_r[:], rows[0:1, 0:256],
                             start=True, stop=True),
                    t.matmul(ps_ub[:, 256:512], ones_r[:], rows[0:1, 256:512],
                             start=True, stop=True))[-1],
                    waits=[("vm", vbr)], inc="tm")
                def mk_h2a():
                    def f(v):
                        r = None
                        for cc in range(8):
                            sl = slice(256 * cc, 256 * cc + 256)
                            r = v.tensor_tensor(out=o_sq[:, sl], in0=x1_t[:, sl],
                                                in1=ps_ub[:, 0:256], op=ALU.mult)
                        return r
                    return f
                def mk_h2b():
                    def f(v):
                        r = None
                        for cc in range(8):
                            sl = slice(256 * cc, 256 * cc + 256)
                            r = v.tensor_tensor(out=h2_t[:, sl], in0=o_sq[:, sl],
                                                in1=ps_ub[:, 256:512], op=ALU.add)
                        return r
                    return f
                va2 = rec.op("v", mk_h2a(),
                             waits=[("tm", tmb), ("vm", rec.val("vm"))], inc="vm")
                vh2 = rec.op("v", mk_h2b(),
                             waits=[("vm", va2),
                                    ("gw", h2w_gw[mm - 1] if mm >= 1 else 0)],
                             inc="vm")
                h2w_gw[mm] = rec.op("g", (lambda mm=mm: lambda g: g.dma_start(
                    out=h2_scr[mm], in_=h2_t[:]))(), waits=[("vm", vh2)], inc="gw")
                ch3[m] = vh2
            barrier = rec.finish()
        recs.append(rec)
        if DBG_MAX_PHASE < 5:
            return _assemble(nc, recs, sems)

        rec = Rec(counts)
        with ExitStack() as p4s:
            pe4 = p4s.enter_context
            w2x = pe4(nc.sbuf_tensor("p4_w2", [128, 32768], BF16))
            h2_l = [pe4(nc.sbuf_tensor(f"p4_h2{i}", [128, 2048], BF16))
                    for i in range(2)]
            act_sb = pe4(nc.sbuf_tensor("p4_act", [128, 16384], BF16))
            qcc = pe4(nc.sbuf_tensor("p4_qcc", [128, 32], F32))
            th2 = pe4(nc.sbuf_tensor("p4_th2", [128, 32], F32))
            b1s = pe4(nc.sbuf_tensor("p4_b1", [128, 32], F32))
            b2s = pe4(nc.sbuf_tensor("p4_b2", [128, 8], F32))
            wi_s = pe4(nc.sbuf_tensor("p4_wi", [128, 32], F32))
            tmp32 = pe4(nc.sbuf_tensor("p4_tmp32", [128, 96], F32))
            tmpi = pe4(nc.sbuf_tensor("p4_tmpi", [128, 32], I32))
            a_blk = [pe4(nc.sbuf_tensor(f"p4_a{i}", [128, 1024], F32)) for i in range(2)]
            ki_blk = pe4(nc.sbuf_tensor("p4_ki", [128, 1024], I32))
            s_bf = [pe4(nc.sbuf_tensor(f"p4_s{i}", [128, 1024], BF16)) for i in range(2)]
            h2q = [pe4(nc.sbuf_tensor(f"p4_h2q{i}", [128, 256], F32)) for i in range(2)]
            ps_hid = [pe4(nc.psum_tensor(f"p4_hid{i}", [128, 1024], F32))
                      for i in range(2)]
            ps_y2 = pe4(nc.psum_tensor("p4_y2", [128, 2048], F32))
            x1_l = px

            rec.op("g", lambda g: g.dma_start(
                out=arena[:].rearrange("p (c n) -> p c n", c=8),
                in_=W1.rearrange("(c p) n -> p c n", p=128)),
                waits=barrier, inc="gw")
            gw_w1 = rec.val("gw")
            rec.op("g", lambda g: g.dma_start(
                out=w2x[:].rearrange("p (f n) -> p f n", f=32),
                in_=W2.rearrange("(f p) n -> p f n", p=128)), inc="gw")
            gw_w2 = rec.val("gw")
            ldc = 0
            for sb, di in [(b1s, b1c), (b2s, b2c), (th2, qpc), (qcc, qrc),
                           (wi_s, qic)]:
                ldc = rec.op("sy", (lambda sb=sb, di=di: lambda s: s.dma_start(
                    out=sb[:], in_=di))(),
                    waits=(barrier if ldc == 0 else ()), inc="ld")

            def vop(fn, waits):
                return rec.op("v", fn, waits=waits, inc="vm")

            e = vop(lambda v: v.tensor_scalar(out=tmp32[:, 0:32], in0=th2[:],
                                              scalar1=1.0 / TWO_PI, scalar2=None,
                                              op0=ALU.mult),
                    [("ld", ldc)] + list(barrier))
            e = vop(lambda v: v.tensor_copy(tmpi[:], tmp32[:, 0:32]), [("vm", e)])
            e = vop(lambda v: v.scalar_tensor_tensor(
                out=tmp32[:, 0:32], in0=tmpi[:], scalar=-1.0,
                in1=tmp32[:, 0:32], op0=ALU.mult, op1=ALU.add), [("vm", e)])
            sq = rec.op("sc", lambda sc: sc.activation(
                out=tmp32[:, 64:96], in_=tmp32[:, 0:32], func=AF.Sin, scale=TWO_PI),
                waits=[("vm", e)], inc="sm")
            e = vop(lambda v: v.tensor_scalar(out=tmp32[:, 0:32], in0=th2[:],
                                              scalar1=1.0 / TWO_PI, scalar2=0.25,
                                              op0=ALU.mult, op1=ALU.add),
                    [("sm", sq)])
            e = vop(lambda v: v.tensor_copy(tmpi[:], tmp32[:, 0:32]), [("vm", e)])
            e = vop(lambda v: v.scalar_tensor_tensor(
                out=tmp32[:, 0:32], in0=tmpi[:], scalar=-1.0,
                in1=tmp32[:, 0:32], op0=ALU.mult, op1=ALU.add), [("vm", e)])
            sq = rec.op("sc", lambda sc: sc.activation(
                out=tmp32[:, 0:32], in_=tmp32[:, 0:32], func=AF.Sin, scale=TWO_PI),
                waits=[("vm", e)], inc="sm")
            e = vop(lambda v: v.tensor_tensor(out=tmp32[:, 32:64], in0=tmp32[:, 0:32],
                                              in1=qcc[:], op=ALU.mult), [("sm", sq)])
            e = vop(lambda v: v.tensor_tensor(out=tmp32[:, 0:32], in0=tmp32[:, 64:96],
                                              in1=wi_s[:], op=ALU.mult), [("vm", e)])
            e = vop(lambda v: v.tensor_tensor(out=tmp32[:, 32:64], in0=tmp32[:, 32:64],
                                              in1=tmp32[:, 0:32], op=ALU.add),
                    [("vm", e)])
            e = vop(lambda v: v.scalar_tensor_tensor(
                out=qcc[:], in0=tmp32[:, 32:64], scalar=0.1, in1=wi_s[:],
                op0=ALU.mult, op1=ALU.mult), [("vm", e)])
            e = vop(lambda v: v.scalar_tensor_tensor(
                out=th2[:], in0=b1s[:], scalar=0.1, in1=th2[:],
                op0=ALU.mult, op1=ALU.add), [("vm", e)])
            vq = vop(lambda v: v.tensor_scalar(out=th2[:], in0=th2[:],
                                               scalar1=1.0 / TWO_PI, scalar2=None,
                                               op0=ALU.mult), [("vm", e)])
            if DBG_SCROUT:
                rec.op("g", lambda g: g.dma_start(out=dbg_qc[:, 0:32], in_=qcc[:]),
                       waits=[("vm", vq)], inc="gw")
                rec.op("g", lambda g: g.dma_start(out=dbg_qc[:, 32:64], in_=th2[:]),
                       waits=[("vm", vq)], inc="gw")

            NBLK = 8
            ch4 = [0] * NCH
            st_w = [0] * NCH
            ldx1_of = [0] * NCH
            tmo_of = [0] * NCH
            relu_vals = {}
            sin_of = {}
            sttl_of = {}
            rstt_of = {}
            tmf_last_prev = 0
            tmf_last2 = 0
            for m in range(NCH):
                x1b = x1_l[m % 2]
                asel = m % 2
                wfree = ([("vm", ch4[m - 2])] if m >= 2 else [("vm", vq)])
                h2b = h2_l[m % 2]
                ldh2 = rec.op("sy", (lambda h2b=h2b, m=m: lambda s: s.dma_start(
                    out=h2b[:], in_=h2_scr[m]))(),
                    waits=wfree + [("gw", h2w_gw[m]),
                                   ("tm", tmf_last2 if m >= 2 else 0)], inc="ld")
                ldx1_of[m] = rec.op("sy", (lambda x1b=x1b, m=m: lambda s: s.dma_start(
                    out=x1b[:], in_=x1_scr[m]))(),
                    waits=[("gw", x1w_gw[m]), ("st", st_w[m - 2] if m >= 2 else 0)],
                    inc="ld")

                tmf_of = {}
                def emit_blk(b, m=m, ldh2=ldh2):
                    g = m * NBLK + b
                    wts = [("ld", ldh2), ("gw", gw_w1), ("vm", vq)]
                    gprev = g - 2
                    if gprev >= 0:
                        mp, bp = gprev // NBLK, gprev % NBLK
                        if (mp, 4 * bp + 3) in relu_vals:
                            wts.append(("sm", relu_vals[(mp, 4 * bp + 3)]))
                    def mk_f1(b=b, h2b=h2b):
                        def f(t):
                            r = None
                            for fo in range(4):
                                ff = 4 * b + fo
                                for cc in range(8):
                                    r = t.matmul(
                                        ps_hid[b % 2][:, 256 * fo:256 * fo + 256],
                                        arena[:, 4096 * cc + 128 * ff:
                                              4096 * cc + 128 * ff + 128],
                                        h2b[:, 256 * cc:256 * cc + 256],
                                        start=(cc == 0), stop=(cc == 7))
                            return r
                        return f
                    return rec.op("t", mk_f1(), waits=wts, inc="tm")

                tmf_of[0] = emit_blk(0)
                tmf_of[1] = emit_blk(1)
                for b in range(NBLK):
                    pr = b % 2
                    g = m * NBLK + b
                    wts = [("tm", tmf_of[b])]
                    if (g - 2) >= 0 and ((g - 2) // NBLK, (g - 2) % NBLK) in sin_of:
                        wts.append(("sm", sin_of[((g - 2) // NBLK, (g - 2) % NBLK)]))
                    def mk_ts(b=b, pr=pr, m=m):
                        def f(v):
                            r = None
                            for fo in range(4):
                                ff = 4 * b + fo
                                r = v.tensor_scalar(
                                    out=a_blk[pr][:, 256 * fo:256 * fo + 256],
                                    in0=ps_hid[b % 2][:, 256 * fo:256 * fo + 256],
                                    scalar1=0.1 / TWO_PI, scalar2=th2[:, ff:ff + 1],
                                    op0=ALU.mult, op1=ALU.add)
                            return r
                        return f
                    vts = rec.op("v", mk_ts(), waits=wts, inc="vm")
                    pm1 = rec.op("v", (lambda pr=pr: lambda v: v.tensor_copy(
                        ki_blk[:], a_blk[pr][:]))(), waits=[("vm", vts)], inc="vm")
                    pm2 = rec.op("v", (lambda pr=pr: lambda v: v.tensor_tensor(
                        out=a_blk[pr][:], in0=a_blk[pr][:], in1=ki_blk[:],
                        op=ALU.subtract))(), waits=[("vm", pm1)], inc="vm")
                    rstt_of[(m, b)] = pm2
                    wts = [("vm", pm2)]
                    if (g - 2) >= 0 and ((g - 2) // NBLK, (g - 2) % NBLK) in sttl_of:
                        wts.append(("vm", sttl_of[((g - 2) // NBLK, (g - 2) % NBLK)]))
                    sin_of[(m, b)] = rec.op("sc", (lambda pr=pr: lambda sc: sc.activation(
                        out=s_bf[pr][:], in_=a_blk[pr][:], func=AF.Sin,
                        scale=TWO_PI))(), waits=wts, inc="sm")
                    for fo in range(4):
                        ff = 4 * b + fo
                        wts = [("sm", sin_of[(m, b)])]
                        if ff >= 2:
                            wts.append(("sm", relu_vals[(m, ff - 2)]))
                        elif m >= 1:
                            wts.append(("sm", relu_vals[(m - 1, 30 + ff)]))
                        stv_ = rec.op("v", (lambda pr=pr, fo=fo, ff=ff, b=b: lambda v:
                                            v.scalar_tensor_tensor(
                            out=h2q[ff % 2][:], in0=s_bf[pr][:, 256 * fo:256 * fo + 256],
                            scalar=qcc[:, ff:ff + 1],
                            in1=ps_hid[b % 2][:, 256 * fo:256 * fo + 256],
                            op0=ALU.mult, op1=ALU.add))(), waits=wts, inc="vm")
                        if fo == 3:
                            sttl_of[(m, b)] = stv_
                        w2_ = [("vm", stv_)]
                        if ff == 0 and m >= 2:
                            w2_.append(("tm", tmo_of[m - 2]))
                        relu_vals[(m, ff)] = rec.op("sc", (lambda pr2=ff % 2, ff=ff,
                                                           asel=asel: lambda sc:
                                                           sc.activation(
                            out=act_sb[:, 8192 * asel + 256 * ff:
                                     8192 * asel + 256 * ff + 256],
                            in_=h2q[pr2][:], func=AF.Relu,
                            bias=b1s[:, ff:ff + 1]))(), waits=w2_, inc="sm")
                    if b + 2 < NBLK:
                        tmf_of[b + 2] = emit_blk(b + 2)
                tmf_last2 = tmf_last_prev
                tmf_last_prev = tmf_of[NBLK - 1]
                if DBG_SCROUT:
                    rec.op("g", (lambda m=m, asel=asel: lambda g: g.dma_start(
                        out=dbg_act[m], in_=act_sb[:, 8192 * asel:8192 * asel + 8192]))(),
                        waits=[("sm", relu_vals[(m, 31)])], inc="gw")

                def emit_ffn2(mm2):
                    asel2 = mm2 % 2
                    def mk_f2(asel2=asel2):
                        def f(t):
                            r = None
                            for dmc in range(8):
                                for ff in range(32):
                                    r = t.matmul(ps_y2[:, 256 * dmc:256 * dmc + 256],
                                                 w2x[:, 1024 * ff + 128 * dmc:
                                                     1024 * ff + 128 * dmc + 128],
                                                 act_sb[:, 8192 * asel2 + 256 * ff:
                                                       8192 * asel2 + 256 * ff + 256],
                                                 start=(ff == 0), stop=(ff == 31))
                            return r
                        return f
                    tmo = rec.op("t", mk_f2(),
                                 waits=[("sm", relu_vals[(mm2, 31)]), ("gw", gw_w2),
                                        ("vm", ch4[mm2 - 1] if mm2 >= 1 else 0)],
                                 inc="tm")
                    tmo_of[mm2] = tmo
                    x1c = x1_l[mm2 % 2]
                    def mk_out(x1c=x1c):
                        def f(v):
                            r = None
                            for dmc in range(8):
                                sl = slice(256 * dmc, 256 * dmc + 256)
                                r = v.scalar_tensor_tensor(
                                    out=x1c[:, sl], in0=ps_y2[:, sl],
                                    scalar=b2s[:, dmc:dmc + 1], in1=x1c[:, sl],
                                    op0=ALU.add, op1=ALU.add)
                            return r
                        return f
                    vo = rec.op("v", mk_out(),
                                waits=[("tm", tmo), ("ld", ldx1_of[mm2])], inc="vm")
                    ch4[mm2] = vo
                    st_w[mm2] = rec.op("sy", (lambda x1c=x1c, mm2=mm2: lambda s:
                                              s.dma_start(
                        out=out3[:, :, C * mm2:C * mm2 + C],
                        in_=x1c[:].rearrange("p (c n) -> p c n", c=8)))(),
                        waits=[("vm", vo)], inc="st")

                if m >= 1:
                    emit_ffn2(m - 1)
            emit_ffn2(NCH - 1)
            stv = rec.val("st")
            rec.streams["sy"].append([lambda s: s.nop(), [("st", stv)], None, 1])
            rec.finish()
        recs.append(rec)

        return _assemble(nc, recs, sems)


def _prep_inputs(L0_attn_hidden_states, L0_attn_w_qkv, L0_attn_w_gated,
                 L0_attn_w_proj, L0_attn_decay, L0_ffn_W1, L0_ffn_b1,
                 L0_ffn_quantum_phases, L0_ffn_quantum_weights_real,
                 L0_ffn_quantum_weights_imag, L0_ffn_W2, L0_ffn_b2):
    hc = _host_consts()
    x = np.asarray(L0_attn_hidden_states, np.float32)
    wqkv = np.asarray(L0_attn_w_qkv, np.float32)
    wg = np.asarray(L0_attn_w_gated, np.float32)
    wp = np.asarray(L0_attn_w_proj, np.float32)
    decay = np.asarray(L0_attn_decay, np.float32)
    W1 = np.ascontiguousarray(np.asarray(L0_ffn_W1, np.float32))
    W2 = np.ascontiguousarray(np.asarray(L0_ffn_W2, np.float32))
    b1 = np.asarray(L0_ffn_b1, np.float32)
    b2 = np.asarray(L0_ffn_b2, np.float32)
    qp = np.asarray(L0_ffn_quantum_phases, np.float32)
    qr = np.asarray(L0_ffn_quantum_weights_real, np.float32)
    qi = np.asarray(L0_ffn_quantum_weights_imag, np.float32)

    perm = hc["perm_qk"]
    wqkT = np.ascontiguousarray(
        np.concatenate([wqkv[0:1024][perm], wqkv[1024:2048][perm]], axis=0).T)
    wvgT = np.ascontiguousarray(np.concatenate([wqkv[2048:4096], wg], 0).T)
    wpT = np.ascontiguousarray(wp.T)

    cols = lambda a, n: np.ascontiguousarray(a.reshape(n, 128).T)
    common = {
        "wqkT": wqkT, "wvgT": wvgT, "wpT": wpT, "W1": W1, "W2": W2,
        "csk_p": hc["csk0"], "snk_p": hc["snk0"],
        "delta0": hc["delta0"], "delta1": hc["delta1"], "qexp": hc["qexp"],
        "kexp": hc["kexp"], "c256": hc["c256"],
        "decay_c": np.ascontiguousarray(np.tile(decay[None, :], (128, 1))),
        "b1c": cols(b1, 32), "b2c": cols(b2, 8), "qpc": cols(qp, 32),
        "qrc": cols(qr, 32), "qic": cols(qi, 32),
        "ones_c": hc["ones_c"], "ones_r": hc["ones_r"], "ident": hc["ident"],
        "eps1": hc["eps1"], "epsc": hc["epsc"],
    }
    in_maps = []
    for c in range(NCORE):
        b, th = c // 2, c % 2
        xT = np.ascontiguousarray(x[b].T)
        m = dict(common)
        m["xT_own"] = np.ascontiguousarray(xT[:, TOW * th:TOW * (th + 1)])
        m["xT_prev"] = np.ascontiguousarray(xT[:, 0:TOW])
        m["csq"] = hc["csq1"] if th else hc["csq0"]
        m["snq"] = hc["snq1"] if th else hc["snq0"]
        m["csk_o"] = hc["csk1"] if th else hc["csk0"]
        m["snk_o"] = hc["snk1"] if th else hc["snk0"]
        m["carry"] = np.full((128, 1), float(th), np.float32)
        in_maps.append(m)
    return in_maps


def kernel(**inputs):
    in_maps = _prep_inputs(**inputs)
    nc = build_program()
    res = run_bass_kernel_spmd(nc, in_maps, core_ids=list(range(NCORE)))
    out = np.empty((B, T, D), np.float32)
    for c in range(NCORE):
        b, th = c // 2, c % 2
        out[b, TOW * th:TOW * (th + 1), :] = np.asarray(res.results[c]["outT"]).T
    return out
